# version 11
# baseline (speedup 1.0000x reference)
import numpy as np
from contextlib import ExitStack

import concourse.bass as bass
import concourse.mybir as mybir
from concourse.bass_utils import run_bass_kernel_spmd

F32 = mybir.dt.float32
BF16 = mybir.dt.bfloat16
I32 = mybir.dt.int32
U32 = mybir.dt.uint32
AF = mybir.ActivationFunctionType
ALU = mybir.AluOpType
AX = mybir.AxisListType

T = 4096
D = 1024
NF = 8
NT = 8
TC = 512
NTT = 32

ENGINES = ("tensor", "vector", "scalar", "gpsimd", "sync")


class K:

    AUTO_CHAIN = ("vector", "scalar", "gpsimd")

    def __init__(self, nc):
        self.nc = nc
        self.ops = {e: [] for e in ENGINES}
        self.cnt = {e: 0 for e in ENGINES}
        self.sems = {e: nc.alloc_semaphore(f"sem_{e}") for e in ENGINES}

    def op(self, eng, fn, inc=0, chain=None):
        if chain is None:
            chain = eng in self.AUTO_CHAIN
        if chain:
            if self.cnt[eng]:
                self.wait(eng, eng, self.cnt[eng])
            inc = max(inc, 1)
        tgt = None
        if inc:
            self.cnt[eng] += inc
            tgt = self.cnt[eng]
        sem = self.sems[eng]

        def run(e, f=fn, n=inc, s=sem):
            inst = f(e)
            if n:
                inst.then_inc(s, n)
        self.ops[eng].append(run)
        return tgt

    def dma(self, q, out, in_):
        m = self.op(q, lambda e, o=out, i=in_: e.dma_start(out=o, in_=i),
                    inc=16)
        self.wait(q, q, m)
        return m

    def dma_batch(self, q, pairs):
        m = None
        for out, in_ in pairs:
            m = self.op(q, lambda e, o=out, i=in_: e.dma_start(out=o, in_=i),
                        inc=16, chain=False)
        self.wait(q, q, m)
        return m

    def wait(self, eng, on, tgt):
        if tgt is None:
            return
        assert tgt <= self.cnt[on], (eng, on, tgt, self.cnt[on])
        self.ops[eng].append(lambda e, s=self.sems[on], v=tgt: e.wait_ge(s, v))

    def waitp(self, eng, pend):
        if pend is not None:
            self.wait(eng, pend[0], pend[1])

    def finish(self):
        with self.nc.Block() as block:
            for eng in ENGINES:
                getattr(block, eng)(
                    lambda e, ops=self.ops[eng]: [f(e) for f in ops])



def make_consts():
    c = {}
    c["c_eps"] = np.array([[1e-5, 1e-6, 1e-8, 1e-12, 1.0, 0.0]], dtype=np.float32)
    c["c_ident"] = np.eye(128, dtype=np.float32)
    c["c_ones128"] = np.ones((1, 128), dtype=np.float32)
    c["c_invD"] = np.full((128, 1), 1.0 / D, dtype=np.float32)
    c["c_one_col"] = np.ones((128, 1), dtype=np.float32)
    c["c_onesrow"] = np.ones((1, TC), dtype=np.float32)
    c["c_epscol"] = np.tile(np.array([[1e-5, 1e-6, 1e-8, 1e-12, 1.0, 0.0]],
                                     dtype=np.float32), (128, 1))
    p = np.arange(128)
    c["c_maskkgt"] = np.where(p[:, None] > p[None, :], -1e30, 0.0).astype(np.float32)
    c["c_negrow"] = np.full((1, 128), -1e30, dtype=np.float32)
    c["c_ones128f"] = np.ones((1, 128), dtype=np.float32)
    c["c_identf"] = np.eye(128, dtype=np.float32)
    c["c_phicol"] = np.tile(np.array(
        [[np.pi / 2, -np.log(16.0), 0.0, 12582912.0]],
        dtype=np.float32), (128, 1))
    c["c_iotaneg"] = np.tile((np.arange(1024, dtype=np.float64) - 1e9
                              ).astype(np.float32)[None, :], (128, 1))
    freqs = 1.0 / 10000.0 ** (np.arange(0, 64, 2, dtype=np.float64) / 64.0)
    ang = np.arange(T, dtype=np.float64)[:, None] * freqs[None, :]
    c["c_l0cos"] = np.cos(ang).astype(np.float32)
    c["c_l0sin"] = np.sin(ang).astype(np.float32)
    p = np.arange(128)
    c["c_maskcgt"] = np.where(p[None, :] > p[:, None], -1e30, 0.0).astype(np.float32)
    half_idx = np.arange(0, 64, 2, dtype=np.float64)
    invf = 1.0 / 10000.0 ** (half_idx / 64.0)
    t = np.arange(T, dtype=np.float64)
    fr = t[:, None] * invf[None, :]
    frc = np.concatenate([fr, fr], axis=1)
    base = (half_idx + 0.4 * 64.0) / (1.4 * 64.0)
    power = (t - T // 2) / 256.0
    scale = base[None, :] ** power[:, None]
    scale = np.concatenate([scale, scale], axis=1)
    cosv, sinv = np.cos(frc), np.sin(frc)
    A = np.zeros((T, 128)); B = np.zeros((T, 128))
    A[:, 0:64] = cosv * scale * 8.0
    A[:, 64:128] = cosv / scale
    sq = sinv * scale * 8.0
    si = sinv / scale
    B[:, 0:32] = -sq[:, 32:64]
    B[:, 0:32] = -sq[:, 0:32]
    B[:, 32:64] = sq[:, 32:64]
    B[:, 64:96] = -si[:, 0:32]
    B[:, 96:128] = si[:, 64 - 64 + 32:64]
    B[:, 96:128] = si[:, 32:64]
    c["c_xpA"] = A.astype(np.float32)
    c["c_xpB"] = B.astype(np.float32)
    return c


_CONSTS = None


def get_consts():
    global _CONSTS
    if _CONSTS is None:
        _CONSTS = make_consts()
    return _CONSTS


class Common:

    def __init__(self, nc, k, ctx):
        consts = get_consts()
        self.in_names = list(consts.keys())
        self.aps = {}
        for name in self.in_names:
            arr = consts[name]
            self.aps[name] = nc.dram_tensor(
                name, list(arr.shape), F32, kind="ExternalInput").ap()
        sb = lambda n, s, dt=F32: ctx.enter_context(nc.sbuf_tensor(n, s, dt))
        self.eps = sb("sb_eps", [1, 6])
        self.ident = sb("sb_ident", [128, 128], BF16)
        self.ones128 = sb("sb_ones128", [1, 128], BF16)
        self.invD = sb("sb_invD", [128, 1], BF16)
        self.onecol = sb("sb_onecol", [128, 1], BF16)
        self.onesrow = sb("sb_onesrow", [1, TC], BF16)
        self.epscol = sb("sb_epscol", [128, 6])
        self.maskkgt = sb("sb_maskkgt", [128, 128], BF16)
        self.maskcgt = sb("sb_maskcgt", [128, 128], BF16)
        self.negrow = sb("sb_negrow", [1, 128], BF16)
        self.ones128f = sb("sb_ones128f", [1, 128])
        self.identf = sb("sb_identf", [128, 128])
        self.phicol = sb("sb_phicol", [128, 4])
        k.dma("sync", self.phicol[:], self.aps["c_phicol"][:, :])
        k.dma("sync", self.identf[:], self.aps["c_identf"][:, :])
        k.dma("gpsimd", self.maskkgt[:], self.aps["c_maskkgt"][:, :])
        k.dma("gpsimd", self.maskcgt[:], self.aps["c_maskcgt"][:, :])
        k.dma("gpsimd", self.negrow[:], self.aps["c_negrow"][:, :])
        k.dma("sync", self.ones128f[:], self.aps["c_ones128f"][:, :])
        k.dma("sync", self.epscol[:], self.aps["c_epscol"][:, :])
        k.dma("sync", self.eps[:], self.aps["c_eps"][:, :])
        k.dma("gpsimd", self.ident[:], self.aps["c_ident"][:, :])
        k.dma("gpsimd", self.ones128[:], self.aps["c_ones128"][:, :])
        k.dma("gpsimd", self.invD[:], self.aps["c_invD"][:, :])
        k.dma("gpsimd", self.onecol[:], self.aps["c_one_col"][:, :])
        k.dma("gpsimd", self.onesrow[:], self.aps["c_onesrow"][:, :])
        sy, gp = k.cnt["sync"], k.cnt["gpsimd"]
        for e in ENGINES:
            k.wait(e, "sync", sy)
            k.wait(e, "gpsimd", gp)


def declare_w(nc, names_shapes):
    return {n: nc.dram_tensor(n, list(s), F32, kind="ExternalInput").ap()
            for n, s in names_shapes.items()}



def emit_ln(k, cm, ctx_unused, xdram, xnb, tag, eps_idx=0):
    nc = k.nc
    ctx = ExitStack()
    sb = lambda n, s, dt=F32: ctx.enter_context(
        nc.sbuf_tensor(tag + n, s, dt, side="right"))
    xa = sb("_xa", [128, 2, NF, TC], BF16)
    sq = sb("_sq", [128, 2, TC], BF16)
    murow = sb("_mu", [1, T])
    qrow = sb("_q", [1, T])
    rho_b = sb("_rhob", [1, T], BF16)
    nmurho_b = sb("_nmrb", [1, T], BF16)
    tmp = sb("_tmp", [1, T])
    tmpf = sb("_tmpf", [128, 2, TC])
    pS = ctx.enter_context(nc.psum_tensor(tag + "_pS", [1, TC], F32))
    pQ = ctx.enter_context(nc.psum_tensor(tag + "_pQ", [1, TC], F32))
    pR = ctx.enter_context(nc.psum_tensor(tag + "_pR", [128, TC], F32))
    pM = ctx.enter_context(nc.psum_tensor(tag + "_pM", [128, TC], F32))

    te_buf = [None, None]
    sc_buf = [None, None]
    sq_te = {}
    for tc in range(NT):
        buf = tc % 2
        k.wait("gpsimd", "tensor", te_buf[buf])
        k.wait("gpsimd", "scalar", sc_buf[buf])
        dm = k.dma_batch("gpsimd", [
            (xa[:, buf, fc, :],
             xdram[fc * 128:(fc + 1) * 128, tc * TC:(tc + 1) * TC])
            for fc in range(NF)])
        k.wait("scalar", "gpsimd", dm)
        k.wait("tensor", "gpsimd", dm)
        last_sq = None
        for fc in range(NF):
            gi = tc * NF + fc
            k.wait("scalar", "tensor", sq_te.get(gi - 2))
            m = k.op("scalar", lambda e, a=xa[:, buf, fc, :], o=sq[:, fc % 2, :]:
                     e.activation(o, a, AF.Square), inc=1)
            last_sq = m
            k.op("tensor", lambda e, a=xa[:, buf, fc, :], p=pS,
                 first=(fc == 0), last=(fc == NF - 1):
                 e.matmul(p[:, :], cm.invD[:], a, start=first, stop=last), inc=0)
            k.wait("tensor", "scalar", m)
            sq_te[gi] = k.op("tensor", lambda e, a=sq[:, fc % 2, :], p=pQ,
                             first=(fc == 0), last=(fc == NF - 1):
                             e.matmul(p[:, :], cm.invD[:], a, start=first,
                                      stop=last), inc=1)
        te_buf[buf] = k.cnt["tensor"]
        sc_buf[buf] = last_sq
        k.wait("vector", "tensor", te_buf[buf])
        k.op("vector", lambda e, p=pS, o=murow[:, tc * TC:(tc + 1) * TC]:
             e.tensor_copy(o, p[:, :]), inc=0)
        em = k.op("vector", lambda e, p=pQ, o=qrow[:, tc * TC:(tc + 1) * TC]:
                  e.tensor_copy(o, p[:, :]), inc=1)
        if tc < NT - 1:
            k.wait("tensor", "vector", em)

    k.op("vector", lambda e: e.tensor_mul(tmp[:, :], murow[:, :], murow[:, :]),
         inc=0)
    vm = k.op("vector", lambda e: e.tensor_sub(tmp[:, :], qrow[:, :], tmp[:, :]),
              inc=1)
    k.wait("scalar", "vector", vm)
    k.op("scalar", lambda e: e.activation(
        tmp[:, :], tmp[:, :], AF.Ln, bias=cm.eps[0:1, eps_idx:eps_idx + 1]), inc=0)
    sm = k.op("scalar", lambda e: e.activation(tmp[:, :], tmp[:, :], AF.Exp,
                                               scale=-0.5), inc=1)
    k.wait("vector", "scalar", sm)
    k.op("vector", lambda e: e.tensor_copy(rho_b[:, :], tmp[:, :]), inc=0)
    k.op("vector", lambda e: e.tensor_mul(tmp[:, :], murow[:, :], tmp[:, :]), inc=0)
    k.op("vector", lambda e: e.tensor_scalar_mul(tmp[:, :], tmp[:, :], -1.0), inc=0)
    rm = k.op("vector", lambda e: e.tensor_copy(nmurho_b[:, :], tmp[:, :]), inc=1)

    k.wait("tensor", "vector", rm)
    ve_buf = [None, None]
    prev_ve = None
    passa_sc = k.cnt["scalar"]
    passa_te = k.cnt["tensor"]
    for tc in range(NT):
        buf = tc % 2
        if tc < 2:
            k.wait("gpsimd", "scalar", passa_sc)
            k.wait("gpsimd", "tensor", passa_te)
        tm = None
        k.wait("tensor", "vector", prev_ve)
        k.op("tensor", lambda e, p=pR, r=rho_b[:, tc * TC:(tc + 1) * TC]:
             e.matmul(p[:, :], cm.ones128[:, :], r, start=True, stop=True), inc=0)
        tm = k.op("tensor", lambda e, p=pM, r=nmurho_b[:, tc * TC:(tc + 1) * TC]:
                  e.matmul(p[:, :], cm.ones128[:, :], r, start=True, stop=True),
                  inc=1)
        k.wait("gpsimd", "vector", ve_buf[buf])
        dm = k.dma_batch("gpsimd", [
            (xa[:, buf, fc, :],
             xdram[fc * 128:(fc + 1) * 128, tc * TC:(tc + 1) * TC])
            for fc in range(NF)])
        k.wait("vector", "gpsimd", dm)
        k.wait("vector", "tensor", tm)
        for fc in range(NF):
            k.op("vector", lambda e, a=xa[:, buf, fc, :], p=pR,
                 o=tmpf[:, fc % 2, :]:
                 e.tensor_tensor(o, a, p[:, :], ALU.mult), inc=0)
            k.op("vector", lambda e, a=tmpf[:, fc % 2, :], p=pM,
                 o=xnb[:, fc, tc * TC:(tc + 1) * TC]:
                 e.tensor_tensor(o, a, p[:, :], ALU.add), inc=(1 if fc == NF - 1 else 0))
        ve_buf[buf] = k.cnt["vector"]
        prev_ve = k.cnt["vector"]
    marks = {e: k.cnt[e] for e in ENGINES}
    ctx.close()
    for e in ENGINES:
        for o in ("tensor", "vector", "scalar", "gpsimd"):
            if o != e:
                k.wait(e, o, marks[o])
    return k.cnt["vector"]



def emit_ln_sbuf_affine(k, cm, ctx_unused, xnb, gamma_dram, beta_dram, tag,
                        eps_idx=0):
    nc = k.nc
    ctx = ExitStack()
    sb = lambda n, s, dt=F32: ctx.enter_context(
        nc.sbuf_tensor(tag + n, s, dt, side="right"))
    sq = sb("_sq", [128, 2, TC], BF16)
    murow = sb("_mu", [1, T])
    qrow = sb("_q", [1, T])
    rho_b = sb("_rhob", [1, T], BF16)
    nmurho_b = sb("_nmrb", [1, T], BF16)
    tmp = sb("_tmp", [1, T])
    tmpf = sb("_tmpf", [128, 2, TC])
    gb = sb("_gb", [1, D], BF16)
    bb = sb("_bb", [1, D], BF16)
    pS = ctx.enter_context(nc.psum_tensor(tag + "_pS", [1, TC], F32))
    pQ = ctx.enter_context(nc.psum_tensor(tag + "_pQ", [1, TC], F32))
    pR = ctx.enter_context(nc.psum_tensor(tag + "_pR", [128, TC], F32))
    pM = ctx.enter_context(nc.psum_tensor(tag + "_pM", [128, TC], F32))
    gm = k.dma("gpsimd", gb[:, :], gamma_dram[:, :])
    gm = k.dma("gpsimd", bb[:, :], beta_dram[:, :])
    k.wait("tensor", "gpsimd", gm)

    sq_te = {}
    for tc in range(NT):
        last_sq = None
        for fc in range(NF):
            gi = tc * NF + fc
            k.wait("scalar", "tensor", sq_te.get(gi - 2))
            m = k.op("scalar", lambda e, a=xnb[:, fc, tc * TC:(tc + 1) * TC],
                     o=sq[:, fc % 2, :]:
                     e.activation(o, a, AF.Square), inc=1)
            last_sq = m
            k.op("tensor", lambda e, a=xnb[:, fc, tc * TC:(tc + 1) * TC], p=pS,
                 first=(fc == 0), last=(fc == NF - 1):
                 e.matmul(p[:, :], cm.invD[:], a, start=first, stop=last), inc=0)
            k.wait("tensor", "scalar", m)
            sq_te[gi] = k.op("tensor", lambda e, a=sq[:, fc % 2, :], p=pQ,
                             first=(fc == 0), last=(fc == NF - 1):
                             e.matmul(p[:, :], cm.invD[:], a, start=first,
                                      stop=last), inc=1)
        te_m = k.cnt["tensor"]
        k.wait("vector", "tensor", te_m)
        k.op("vector", lambda e, p=pS, o=murow[:, tc * TC:(tc + 1) * TC]:
             e.tensor_copy(o, p[:, :]), inc=0)
        em = k.op("vector", lambda e, p=pQ, o=qrow[:, tc * TC:(tc + 1) * TC]:
                  e.tensor_copy(o, p[:, :]), inc=1)
        if tc < NT - 1:
            k.wait("tensor", "vector", em)

    k.op("vector", lambda e: e.tensor_mul(tmp[:, :], murow[:, :], murow[:, :]),
         inc=0)
    vm = k.op("vector", lambda e: e.tensor_sub(tmp[:, :], qrow[:, :], tmp[:, :]),
              inc=1)
    k.wait("scalar", "vector", vm)
    k.op("scalar", lambda e: e.activation(
        tmp[:, :], tmp[:, :], AF.Ln, bias=cm.eps[0:1, eps_idx:eps_idx + 1]), inc=0)
    sm = k.op("scalar", lambda e: e.activation(tmp[:, :], tmp[:, :], AF.Exp,
                                               scale=-0.5), inc=1)
    k.wait("vector", "scalar", sm)
    k.op("vector", lambda e: e.tensor_copy(rho_b[:, :], tmp[:, :]), inc=0)
    k.op("vector", lambda e: e.tensor_mul(tmp[:, :], murow[:, :], tmp[:, :]), inc=0)
    k.op("vector", lambda e: e.tensor_scalar_mul(tmp[:, :], tmp[:, :], -1.0), inc=0)
    rm = k.op("vector", lambda e: e.tensor_copy(nmurho_b[:, :], tmp[:, :]), inc=1)
    k.wait("tensor", "vector", rm)

    prev_ve = None
    for tc in range(NT):
        for fc in range(NF):
            k.wait("tensor", "vector", prev_ve)
            k.op("tensor", lambda e, p=pR, g=gb[0:1, fc * 128:(fc + 1) * 128],
                 r=rho_b[:, tc * TC:(tc + 1) * TC]:
                 e.matmul(p[:, :], g, r, start=True, stop=True), inc=0)
            k.op("tensor", lambda e, p=pM, g=gb[0:1, fc * 128:(fc + 1) * 128],
                 r=nmurho_b[:, tc * TC:(tc + 1) * TC]:
                 e.matmul(p[:, :], g, r, start=True, stop=False), inc=0)
            tm = k.op("tensor", lambda e, p=pM, b=bb[0:1, fc * 128:(fc + 1) * 128]:
                      e.matmul(p[:, :], b, cm.onesrow[:, :], start=False,
                               stop=True), inc=1)
            k.wait("vector", "tensor", tm)
            k.op("vector", lambda e, a=xnb[:, fc, tc * TC:(tc + 1) * TC], p=pR,
                 o=tmpf[:, fc % 2, :]:
                 e.tensor_tensor(o, a, p[:, :], ALU.mult), inc=1)
            k.op("vector", lambda e, a=tmpf[:, fc % 2, :], p=pM,
                 o=xnb[:, fc, tc * TC:(tc + 1) * TC]:
                 e.tensor_tensor(o, a, p[:, :], ALU.add), inc=1)
            prev_ve = k.cnt["vector"]
    marks = {e: k.cnt[e] for e in ENGINES}
    ctx.close()
    for e in ENGINES:
        for o in ("tensor", "vector", "scalar", "gpsimd"):
            if o != e:
                k.wait(e, o, marks[o])
    return k.cnt["vector"]



class PsumRot:

    def __init__(self, k, ctx, tag, n=4, shape=(128, TC)):
        self.k = k
        self.banks = [ctx.enter_context(
            k.nc.psum_tensor(f"{tag}_rot{i}", list(shape), F32)) for i in range(n)]
        self.pend = [None] * n
        self.i = 0

    def next(self):
        i = self.i
        self.i = (self.i + 1) % len(self.banks)
        p = self.banks[i]
        if self.pend[i] is not None:
            self.k.wait("tensor", self.pend[i][0], self.pend[i][1])

        def release(em, idx=i):
            self.pend[idx] = em
        return p, release


def emit_mm_acc(k, lhs_slices, rhs_slices, psum, final_inc=1):
    n = len(lhs_slices)
    for i, (w, r) in enumerate(zip(lhs_slices, rhs_slices)):
        k.op("tensor", lambda e, ww=w, rr=r, p=psum, first=(i == 0),
             last=(i == n - 1): e.matmul(p[:, :], ww, rr, start=first, stop=last),
             inc=(final_inc if i == n - 1 else 0))
    return k.cnt["tensor"]



def fourier_consts():
    j = np.arange(D)
    ang = (-2.0 * np.pi / D) * np.outer(j, j % D).astype(np.float64)
    return (np.cos(ang).astype(np.float32), np.sin(ang).astype(np.float32))


class NCW:

    def __init__(self, nc):
        self._nc = nc
        self._names = {}

    def _uniq(self, name):
        i = self._names.get(name, 0)
        self._names[name] = i + 1
        return name if i == 0 else f"{name}_u{i}"

    def sbuf_tensor(self, name, *a, **kw):
        return self._nc.sbuf_tensor(self._uniq(name), *a, **kw)

    def psum_tensor(self, name, *a, **kw):
        return self._nc.psum_tensor(self._uniq(name), *a, **kw)

    def __getattr__(self, attr):
        return getattr(self._nc, attr)


def _block_env(env):
    if env is not None:
        nc, k, ctx_outer, cm, xt, xo = env
        return nc, k, ExitStack(), cm, xt, xo, False
    nc = bass.Bass("TRN2", target_bir_lowering=False, debug=False)
    k = K(nc)
    ctx = ExitStack()
    cm = Common(nc, k, ctx)
    xt = nc.dram_tensor("xt", [D, T], F32, kind="ExternalInput").ap()
    xo = nc.dram_tensor("xo", [D, T], F32, kind="ExternalOutput").ap()
    return nc, k, ctx, cm, xt, xo, True


def _block_close(k, ctx):
    m = {e: k.cnt[e] for e in ENGINES}
    ctx.close()
    for e in ENGINES:
        for o in ENGINES:
            if o != e:
                k.wait(e, o, m[o])


def build_L2f(env=None):
    nc, k, ctx, cm, xt, xo, standalone = _block_env(env)
    W = declare_w(nc, {
        "l2f_wr": (D, D), "l2f_wi": (D, D), "l2f_bias": (1, D),
        "c_Fr": (D, D), "c_Fi": (D, D),
    })
    sb = lambda n, s, dt=F32: ctx.enter_context(nc.sbuf_tensor(n, s, dt))
    xnb = sb("xnb", [128, NF, T], BF16)
    Mb = sb("Mb", [128, NF, D], BF16)
    biasb = sb("biasb", [1, D], BF16)
    biasf = sb("biasf", [1, D])
    xres = sb("xres", [128, 2, TC])
    rot = PsumRot(k, ctx, "l2f")
    ctx2 = ExitStack()
    sb2 = lambda n, s, dt=F32: ctx2.enter_context(nc.sbuf_tensor(n, s, dt))
    Frb = sb2("Frb", [128, NF, D], BF16)
    Fib = sb2("Fib", [128, NF, D], BF16)
    wrb = sb2("wrb", [128, NF, D], BF16)
    wip = sb2("wip", [128, NF, D], BF16)
    ArT = sb2("ArT", [128, NF, D], BF16)
    AiT = sb2("AiT", [128, NF, D], BF16)

    k.dma_batch("gpsimd", [
        (b_[:, fc, :], W[wn][fc * 128:(fc + 1) * 128, :])
        for fc in range(NF)
        for b_, wn in ((Frb, "c_Fr"), (Fib, "c_Fi"),
                       (wrb, "l2f_wr"), (wip, "l2f_wi"))])
    mload = k.cnt["gpsimd"]
    mb = k.dma("sync", biasf[:, :], W["l2f_bias"][:, :])
    k.wait("vector", "gpsimd", mload)
    k.wait("vector", "sync", mb)
    nw = k.op("vector", lambda e: e.tensor_copy(biasb[:, :], biasf[:, :]), inc=1)

    def mm_pair(dst, l1, r1, l2, r2, scale):
        last = None
        for oc in range(NF):
            for ic in range(2):
                p, release = rot.next()
                lhs, rhs = [], []
                for kc in range(NF):
                    lhs += [l1[:, kc, oc * 128:(oc + 1) * 128],
                            l2[:, kc, oc * 128:(oc + 1) * 128]]
                    rhs += [r1[:, kc, ic * 512:(ic + 1) * 512],
                            r2[:, kc, ic * 512:(ic + 1) * 512]]
                tem = emit_mm_acc(k, lhs, rhs, p)
                k.wait("scalar", "tensor", tem)
                em = k.op("scalar", lambda e, pp=p,
                          o=dst[:, oc, ic * 512:(ic + 1) * 512], s=scale:
                          e.activation(o, pp[:, :], AF.Copy, scale=s), inc=1)
                release(("scalar", em))
                last = em
        return last

    k.wait("tensor", "gpsimd", mload)
    m2 = mm_pair(AiT, wrb, Fib, wip, Frb, 1.0)
    k.wait("vector", "tensor", k.cnt["tensor"])
    ng = None
    for fc in range(NF):
        ng = k.op("vector", lambda e, a=wip[:, fc, :]:
                  e.tensor_scalar_mul(a, a, -1.0), inc=(1 if fc == NF - 1 else 0))
    k.wait("tensor", "vector", ng)
    m1 = mm_pair(ArT, wrb, Frb, wip, Fib, 1.0)
    m3 = mm_pair(Mb, ArT, Frb, AiT, Fib, 1.0 / D)
    te_done = k.cnt["tensor"]
    sc_done = k.cnt["scalar"]
    ctx2.close()
    for e in ("gpsimd", "vector", "sync"):
        k.wait(e, "tensor", te_done)
        k.wait(e, "scalar", sc_done)

    ln_mark = emit_ln(k, cm, ctx, xt, xnb, "ln2f")
    k.wait("tensor", "vector", nw)
    k.wait("tensor", "vector", ln_mark)

    pend_store = [None, None]
    call = [0]

    def out_evac(oc, tc, p, release):
        tm = k.op("tensor", lambda e, pp=p, b=biasb[0:1, oc * 128:(oc + 1) * 128]:
                  e.matmul(pp[:, :], b, cm.onesrow[:, :], start=False, stop=True),
                  inc=1)
        buf = call[0] % 2
        call[0] += 1
        k.wait("sync", "sync", pend_store[buf])
        dm = k.dma("sync", xres[:, buf, :],
                   xt[oc * 128:(oc + 1) * 128, tc * TC:(tc + 1) * TC])
        k.wait("vector", "sync", dm)
        k.wait("vector", "tensor", tm)
        vm = k.op("vector", lambda e, pp=p, xr=xres[:, buf, :]:
                  e.tensor_tensor(xr, xr, pp[:, :], ALU.add), inc=1)
        k.wait("sync", "vector", vm)
        sm = k.dma("sync", xo[oc * 128:(oc + 1) * 128, tc * TC:(tc + 1) * TC],
                   xres[:, buf, :])
        pend_store[buf] = sm
        release(("vector", vm))

    for oc in range(NF):
        for tc in range(NT):
            p, release = rot.next()
            lhs = [Mb[:, kc, oc * 128:(oc + 1) * 128] for kc in range(NF)]
            rhs = [xnb[:, kc, tc * TC:(tc + 1) * TC] for kc in range(NF)]
            emit_mm_acc(k, lhs, rhs, p, final_inc=0)
            out_evac(oc, tc, p, release)

    k.wait("sync", "sync", k.cnt["sync"])
    if not standalone:
        return _block_close(k, ctx)
    k.finish()
    ctx.close()
    in_names = (["xt", "l2f_wr", "l2f_wi", "l2f_bias", "c_Fr", "c_Fi"]
                + cm.in_names)
    return nc, in_names



def emit_proj_evac(k, cm, rot, wb, xnb, bias_b, evac, scale_done=None):
    for oc in range(NF):
        for tc in range(NT):
            p, release = rot.next()
            for kc in range(NF):
                last = (kc == NF - 1 and bias_b is None)
                tem = k.op("tensor",
                           lambda e, w=wb[:, kc, oc * 128:(oc + 1) * 128],
                           r=xnb[:, kc, tc * TC:(tc + 1) * TC], pp=p,
                           first=(kc == 0), la=last:
                           e.matmul(pp[:, :], w, r, start=first, stop=la),
                           inc=(1 if last else 0))
            if bias_b is not None:
                tem = k.op("tensor",
                           lambda e, pp=p, b=bias_b[0:1, oc * 128:(oc + 1) * 128]:
                           e.matmul(pp[:, :], b, cm.onesrow[:, :], start=False,
                                    stop=True), inc=1)
            evac(oc, tc, p, tem, release)


def emit_out_proj_dram(k, cm, ctx, wdram, bias_dram, srcdram, xdram, xodram,
                       tag, nk=NF):
    nc = k.nc
    sb = lambda n, s, dt=F32: ctx.enter_context(nc.sbuf_tensor(tag + n, s, dt))
    wb = sb("_wb", [128, nk, D], BF16)
    biasb = sb("_biasb", [1, D], BF16)
    srcb = sb("_srcb", [128, 2, nk, TC], BF16)
    xres = sb("_xres", [128, 2, TC])
    rot = PsumRot(k, ctx, tag + "r")
    wm = k.dma_batch("gpsimd", [
        (wb[:, kc, :], wdram[kc * 128:(kc + 1) * 128, :])
        for kc in range(nk)
    ] + ([(biasb[:, :], bias_dram[:, :])] if bias_dram is not None else []))
    k.wait("tensor", "gpsimd", wm)
    pend_store = [None, None]
    srcb_pend = [None, None]
    call = [0]
    for tc in range(NT):
        sbuf_i = tc % 2
        k.wait("gpsimd", "tensor", srcb_pend[sbuf_i])
        sm = k.dma_batch("gpsimd", [
            (srcb[:, sbuf_i, kc, :],
             srcdram[kc * 128:(kc + 1) * 128, tc * TC:(tc + 1) * TC])
            for kc in range(nk)])
        k.wait("tensor", "gpsimd", sm)
        for oc in range(NF):
            p, release = rot.next()
            for kc in range(nk):
                k.op("tensor", lambda e, w=wb[:, kc, oc * 128:(oc + 1) * 128],
                     r=srcb[:, sbuf_i, kc, :], pp=p, first=(kc == 0),
                     last=(kc == nk - 1 and bias_dram is None):
                     e.matmul(pp[:, :], w, r, start=first, stop=last),
                     inc=(1 if (kc == nk - 1 and bias_dram is None) else 0))
            if bias_dram is not None:
                tm = k.op("tensor",
                          lambda e, pp=p, b=biasb[0:1, oc * 128:(oc + 1) * 128]:
                          e.matmul(pp[:, :], b, cm.onesrow[:, :], start=False,
                                   stop=True), inc=1)
            else:
                tm = k.cnt["tensor"]
            buf = call[0] % 2
            call[0] += 1
            k.wait("sync", "sync", pend_store[buf])
            dm = k.dma("sync", xres[:, buf, :],
                       xdram[oc * 128:(oc + 1) * 128, tc * TC:(tc + 1) * TC])
            k.wait("vector", "sync", dm)
            k.wait("vector", "tensor", tm)
            vm = k.op("vector", lambda e, pp=p, xr=xres[:, buf, :]:
                      e.tensor_tensor(xr, xr, pp[:, :], ALU.add), inc=1)
            k.wait("sync", "vector", vm)
            pend_store[buf] = k.dma(
                "sync", xodram[oc * 128:(oc + 1) * 128, tc * TC:(tc + 1) * TC],
                xres[:, buf, :])
            release(("vector", vm))
        srcb_pend[sbuf_i] = k.cnt["tensor"]



def build_L1a(env=None):
    nc, k, ctx, cm, xt, xo, standalone = _block_env(env)
    W = declare_w(nc, {
        "l1a_wqT": (D, D), "l1a_wkT": (D, D), "l1a_wvT": (D, D),
        "l1a_woT": (D, D), "l1a_bq": (1, D), "l1a_bk": (1, D),
        "l1a_bv": (1, D), "l1a_bo": (1, D),
    })
    qd = nc.dram_tensor("l1a_qd", [D, T], BF16, kind="Internal").ap()
    kd = nc.dram_tensor("l1a_kd", [D, T], BF16, kind="Internal").ap()
    otd = nc.dram_tensor("l1a_otd", [D, T], BF16, kind="Internal").ap()

    sb = lambda n, s, dt=F32: ctx.enter_context(nc.sbuf_tensor(n, s, dt))

    ctxA = ExitStack()
    sbA = lambda n, s, dt=F32: ctxA.enter_context(
        nc.sbuf_tensor(n, s, dt, side="right"))
    xnb = sbA("xnb", [128, NF, T], BF16)

    ln_mark = emit_ln(k, cm, ctx, xt, xnb, "ln1a")

    vtok = sb("vtok", [128, NTT, 16, 65], BF16)
    k2cols = sb("k2cols", [128, NTT, 16])
    mcols = sb("mcols", [128, NTT, 16])
    wqb = sbA("wqb", [128, NF, D], BF16)
    wkb = sbA("wkb", [128, NF, D], BF16)
    wvb = sbA("wvb", [128, NF, D], BF16)
    browsb = sbA("browsb", [1, 3 * D], BF16)
    spill = sbA("spill", [128, 4, TC], BF16)
    sqt = sbA("sqt", [128, D], BF16)

    bm = k.dma_batch("gpsimd", [
        (wb_[:, kc, :], W[wn][kc * 128:(kc + 1) * 128, :])
        for kc in range(NF)
        for wb_, wn in ((wqb, "l1a_wqT"), (wkb, "l1a_wkT"), (wvb, "l1a_wvT"))
    ] + [
        (browsb[0:1, 0:D], W["l1a_bq"][:, :]),
        (browsb[0:1, D:2 * D], W["l1a_bk"][:, :]),
        (browsb[0:1, 2 * D:3 * D], W["l1a_bv"][:, :]),
    ])
    bqr = browsb[0:1, 0:D]
    bkr = browsb[0:1, D:2 * D]
    bvr = browsb[0:1, 2 * D:3 * D]

    rot = PsumRot(k, ctxA, "l1a", 2)
    k.wait("tensor", "gpsimd", bm)
    k.wait("tensor", "vector", ln_mark)

    spill_pend = [None] * 4
    spill_call = [0]

    def spill_evac(dst_dram, scale):
        def evac(oc, tc, p, tem, release):
            i = spill_call[0] % 4
            spill_call[0] += 1
            k.wait("scalar", "sync", spill_pend[i])
            k.wait("scalar", "tensor", tem)
            em = k.op("scalar", lambda e, pp=p, o=spill[:, i, :], s=scale:
                      e.activation(o, pp[:, :], AF.Copy, scale=s), inc=1)
            k.wait("sync", "scalar", em)
            dmm = k.dma("sync", dst_dram[oc * 128:(oc + 1) * 128,
                                         tc * TC:(tc + 1) * TC], spill[:, i, :])
            spill_pend[i] = dmm
            release(("scalar", em))
        return evac

    emit_proj_evac(k, cm, rot, wqb, xnb, bqr, spill_evac(qd, 1.0))
    emit_proj_evac(k, cm, rot, wkb, xnb, bkr, spill_evac(kd, -2.0))

    pV = ctxA.enter_context(nc.psum_tensor("l1a_pV", [128, D], F32))
    pKt = ctxA.enter_context(nc.psum_tensor("l1a_pKt", [128, D], F32))
    pv_pend = [None]
    pk_pend = [None]
    for tt in range(NTT):
        t0 = tt * 128
        k.wait("tensor", "vector", pv_pend[0])
        for kc in range(NF):
            for half in range(2):
                k.op("tensor", lambda e, l=xnb[:, kc, t0:t0 + 128],
                     r=wvb[:, kc, half * 512:(half + 1) * 512],
                     pp=pV[:, bass.ts(half, 512)], first=(kc == 0):
                     e.matmul(pp, l, r, start=first, stop=False), inc=0)
        for half in range(2):
            k.op("tensor", lambda e, pp=pV[:, bass.ts(half, 512)],
                 b=bvr[0:1, half * 512:(half + 1) * 512]:
                 e.matmul(pp, cm.ones128[:, :], b, start=False, stop=True), inc=0)
        k.wait("tensor", "scalar", pk_pend[0])
        for kc in range(NF):
            for half in range(2):
                k.op("tensor", lambda e, l=xnb[:, kc, t0:t0 + 128],
                     r=wkb[:, kc, half * 512:(half + 1) * 512],
                     pp=pKt[:, bass.ts(half, 512)], first=(kc == 0):
                     e.matmul(pp, l, r, start=first, stop=False), inc=0)
        for half in range(2):
            k.op("tensor", lambda e, pp=pKt[:, bass.ts(half, 512)],
                 b=bkr[0:1, half * 512:(half + 1) * 512]:
                 e.matmul(pp, cm.ones128[:, :], b, start=False, stop=True),
                 inc=(1 if half == 1 else 0))
        tmk = k.cnt["tensor"]
        k.wait("scalar", "tensor", tmk)
        sm = k.op("scalar", lambda e: e.activation(sqt[:, :], pKt[:, :], AF.Square),
                  inc=1)
        pk_pend[0] = sm
        k.wait("vector", "scalar", sm)
        k.op("vector", lambda e, o=k2cols[:, tt, :]:
             e.reduce_sum(o, sqt[:].rearrange("p (h d) -> p h d", h=16), axis=AX.X),
             inc=1)
        vm0 = k.cnt["vector"]
        k.wait("scalar", "vector", vm0)
        sm2 = k.op("scalar", lambda e, o=mcols[:, tt, :], a=k2cols[:, tt, :]:
                   e.activation(o, a, AF.Sqrt), inc=1)
        k.wait("vector", "scalar", sm2)
        for h in range(16):
            k.op("vector", lambda e, o=vtok[:, tt, h, 0:64],
                 a=pV[:, 64 * h:64 * h + 64], s=mcols[:, tt, h:h + 1]:
                 e.tensor_scalar_mul(o, a, s), inc=0)
            k.op("vector", lambda e, o=vtok[:, tt, h, 64:65],
                 a=mcols[:, tt, h:h + 1]:
                 e.tensor_copy(o, a), inc=(1 if h == 15 else 0))
        pv_pend[0] = k.cnt["vector"]

    teA, scA, veA, gpA, syA = (k.cnt["tensor"], k.cnt["scalar"],
                               k.cnt["vector"], k.cnt["gpsimd"], k.cnt["sync"])
    ctxA.close()
    for e in ENGINES:
        k.wait(e, "tensor", teA)
        k.wait(e, "scalar", scA)
        k.wait(e, "vector", veA)
        k.wait(e, "sync", syA)

    ctxB = ExitStack()
    sbB = lambda n, s, dt=F32: ctxB.enter_context(
        nc.sbuf_tensor(n, s, dt, side="right"))
    qh = sbB("qh", [65, 2, T], BF16)
    kh = sbB("kh", [65, 2, T], BF16)
    sqh = sbB("sqh", [64, T], BF16)
    q2row = sbB("q2row", [1, T], BF16)
    stile = sbB("stile", [128, 2, 2 * TC])
    ltile = sbB("ltile", [128, 2, 2 * TC])
    util = sbB("util", [128, 2, 2 * TC], BF16)
    rrow = sbB("rrow", [1, TC])
    rrowb = sbB("rrowb", [1, TC], BF16)
    repb = sbB("repb", [64, 2, TC], BF16)
    oT = sbB("oT", [64, 2, TC], BF16)
    pD = [ctxB.enter_context(nc.psum_tensor(f"l1a_pD{i}", [128, 2 * TC], F32))
          for i in range(2)]
    pN = [ctxB.enter_context(nc.psum_tensor(f"l1a_pN{i}", [65, TC], F32))
          for i in range(2)]
    pRep = ctxB.enter_context(nc.psum_tensor("l1a_pRep", [64, TC], F32))
    pQ2 = ctxB.enter_context(nc.psum_tensor("l1a_pQ2", [1, TC], F32))

    head_pend = [None, None]
    pd_pend = [None, None]
    u_pend = {}
    pn_pend = [None, None]
    q2_pend = [None]
    rep_pend = [None]
    o_pend = [None, None]
    sq_pend = [None]

    vkm = k.op("vector", lambda e: e.memset(kh[64:65, :, :], 1.0), inc=1)
    k.wait("tensor", "vector", vkm)

    for h in range(16):
        hb = h % 2
        fcq = h // 2
        rowbase = 64 * (h % 2)
        k.wait("gpsimd", "tensor", head_pend[hb])
        dmq = k.dma("gpsimd", qh[0:64, hb, :],
                    qd[64 * h:64 * h + 64, :])
        dmk = k.dma("gpsimd", kh[0:64, hb, :],
                    kd[64 * h:64 * h + 64, :])
        k.wait("scalar", "gpsimd", dmq)
        sm = k.op("scalar", lambda e, a=qh[0:64, hb, :]:
                  e.activation(sqh[:, :], a, AF.Square), inc=1)
        k.wait("tensor", "scalar", sm)
        k.wait("tensor", "vector", q2_pend[0])
        k.wait("vector", "tensor", head_pend[hb]) if head_pend[hb] else None
        for tc in range(NT):
            k.op("tensor", lambda e, a=sqh[:, tc * TC:(tc + 1) * TC]:
                 e.matmul(pQ2[:, :], cm.onecol[0:64, :], a, start=True, stop=True),
                 inc=1)
            tmq = k.cnt["tensor"]
            k.wait("vector", "tensor", tmq)
            vq = k.op("vector", lambda e, o=qh[64:65, hb,
                                               tc * TC:(tc + 1) * TC]:
                      e.tensor_copy(o, pQ2[:, :]), inc=1)
            if tc < NT - 1:
                k.wait("tensor", "vector", vq)
        q2m = k.cnt["vector"]
        q2_pend[0] = q2m
        k.wait("tensor", "vector", q2m)
        k.wait("tensor", "gpsimd", dmk)
        for qc in range(NT):
            ni = qc % 2
            k.wait("tensor", pn_pend[ni][0], pn_pend[ni][1]) if pn_pend[ni] else None
            d2_mark = {}

            def emit_d2(kp, qc=qc):
                di = kp % 2
                k.wait("tensor", "scalar", pd_pend[di])
                for sub in range(2):
                    kt = kp * 2 + sub
                    k.op("tensor", lambda e, l=kh[:, hb, kt * 128:(kt + 1) * 128],
                         r=qh[:, hb, qc * TC:(qc + 1) * TC],
                         pp=pD[di][:, bass.ts(sub, TC)]:
                         e.matmul(pp, l, r, start=True, stop=True), inc=1)
                d2_mark[kp] = k.cnt["tensor"]

            emit_d2(0)
            for kp in range(16):
                di = kp % 2
                if kp + 1 < 16:
                    emit_d2(kp + 1)
                k.wait("scalar", "tensor", d2_mark[kp])
                for sub in range(2):
                    kt = kp * 2 + sub
                    k.op("scalar", lambda e, pp=pD[di][:, bass.ts(sub, TC)],
                         o=stile[:, di, bass.ts(sub, TC)],
                         b=k2cols[:, kt, h:h + 1]:
                         e.activation(o, pp, AF.Ln, bias=b),
                         inc=(1 if sub == 1 else 0))
                pd_pend[di] = k.cnt["scalar"]
                k.wait("scalar", "tensor", u_pend.get((h, qc, kp - 2), None))                     if (h, qc, kp - 2) in u_pend else None
                um = k.op("scalar", lambda e, a=stile[:, di, :], o=util[:, di, :]:
                          e.activation(o, a, AF.Exp, scale=-1.0), inc=1)
                k.wait("tensor", "scalar", um)
                for sub in range(2):
                    kt = kp * 2 + sub
                    u_pend[(h, qc, kp)] = k.op(
                        "tensor", lambda e, l=vtok[:, kt, h, :],
                        r=util[:, di, bass.ts(sub, TC)], pp=pN[ni],
                        first=(kp == 0 and sub == 0),
                        last=(kp == 15 and sub == 1):
                        e.matmul(pp[:, :], l, r, start=first, stop=last),
                        inc=(1 if sub == 1 else 0))
            tmn = k.cnt["tensor"]
            k.wait("vector", "tensor", tmn)
            k.wait("vector", "sync", o_pend[ni])
            k.op("vector", lambda e, pp=pN[ni]:
                 e.reciprocal(rrow[:, :], pp[64:65, :]), inc=0)
            vm1 = k.op("vector", lambda e:
                       e.tensor_copy(rrowb[:, :], rrow[:, :]), inc=1)
            k.wait("tensor", "vector", vm1)
            k.wait("tensor", "scalar", rep_pend[0])
            tmr = k.op("tensor", lambda e:
                       e.matmul(pRep[:, :], cm.ones128[0:1, 0:64], rrowb[:, :],
                                start=True, stop=True), inc=1)
            k.wait("scalar", "tensor", tmr)
            smr = k.op("scalar", lambda e, o=repb[:, ni, :]:
                       e.activation(o, pRep[:, :], AF.Copy), inc=1)
            rep_pend[0] = smr
            k.wait("vector", "scalar", smr)
            vm2 = k.op("vector", lambda e, pp=pN[ni], o=oT[:, ni, :],
                       rr=repb[:, ni, :]:
                       e.tensor_tensor(o, pp[0:64, :], rr, ALU.mult), inc=1)
            pn_pend[ni] = ("vector", vm2)
            k.wait("sync", "vector", vm2)
            o_pend[ni] = k.dma("sync", otd[64 * h:64 * h + 64,
                                           qc * TC:(qc + 1) * TC], oT[:, ni, :])
        head_pend[hb] = k.cnt["tensor"]
        u_pend.clear()

    te2, sc2, ve2, sy2 = (k.cnt["tensor"], k.cnt["scalar"], k.cnt["vector"],
                          k.cnt["sync"])
    ctxB.close()
    for e in ENGINES:
        k.wait(e, "tensor", te2)
        k.wait(e, "scalar", sc2)
        k.wait(e, "vector", ve2)
        k.wait(e, "sync", sy2)

    emit_out_proj_dram(k, cm, ctx, W["l1a_woT"], W["l1a_bo"], otd, xt, xo, "l1aop")

    k.wait("sync", "sync", k.cnt["sync"])
    if not standalone:
        return _block_close(k, ctx)
    k.finish()
    ctx.close()
    in_names = ["xt"] + list(W.keys()) + cm.in_names
    in_names.remove("l1a_qd") if "l1a_qd" in in_names else None
    return nc, in_names


def prep_weights_l1a(inp):
    return {
        "l1a_wqT": np.ascontiguousarray(inp["L1_attn_Wq"].T),
        "l1a_wkT": np.ascontiguousarray(inp["L1_attn_Wk"].T),
        "l1a_wvT": np.ascontiguousarray(inp["L1_attn_Wv"].T),
        "l1a_woT": np.ascontiguousarray(inp["L1_attn_Wo"].T),
        "l1a_bq": inp["L1_attn_bq"].reshape(1, D).astype(np.float32),
        "l1a_bk": inp["L1_attn_bk"].reshape(1, D).astype(np.float32),
        "l1a_bv": inp["L1_attn_bv"].reshape(1, D).astype(np.float32),
        "l1a_bo": inp["L1_attn_bo"].reshape(1, D).astype(np.float32),
    }



def build_L3a(env=None):
    nc, k, ctx, cm, xt, xo, standalone = _block_env(env)
    W = declare_w(nc, {
        "l3a_wq": (D, D), "l3a_wk": (D, D), "l3a_wv": (D, D), "l3a_wo": (D, D),
        "l3a_nullkpad": (D, 128), "l3a_nullvrow": (1, 16 * 65),
        "l3a_ls": (1, 16),
    })
    qd = nc.dram_tensor("l3a_qd", [D, T], BF16, kind="Internal").ap()
    kd = nc.dram_tensor("l3a_kd", [D, T + 128], BF16, kind="Internal").ap()
    otd = nc.dram_tensor("l3a_otd", [D, T], BF16, kind="Internal").ap()
    mqd = nc.dram_tensor("l3a_mqd", [16, T], BF16, kind="Internal").ap()

    sb = lambda n, s, dt=F32: ctx.enter_context(nc.sbuf_tensor(n, s, dt))

    lsr = sb("lsr", [1, 16])
    crow = sb("crow", [1, 16])
    c_all = sb("c_all", [128, 16])
    ctx0 = ExitStack()
    pC = ctx0.enter_context(nc.psum_tensor("l3a_pC", [128, 16], F32))
    lm = k.dma("sync", lsr[:, :], W["l3a_ls"][:, :])
    k.wait("scalar", "sync", lm)
    sm = k.op("scalar", lambda e: e.activation(crow[:, :], lsr[:, :], AF.Exp),
              inc=1)
    k.wait("vector", "scalar", sm)
    vm = k.op("vector", lambda e: e.tensor_scalar_mul(crow[:, :], crow[:, :],
                                                      0.125), inc=1)
    k.wait("tensor", "vector", vm)
    tm0 = k.op("tensor", lambda e: e.matmul(pC[:, :], cm.ones128f[:, :],
                                            crow[:, :], start=True, stop=True),
               inc=1)
    k.wait("vector", "tensor", tm0)
    cam = k.op("vector", lambda e: e.tensor_copy(c_all[:, :], pC[:, :]), inc=1)
    ctx0.close()

    ctxA = ExitStack()
    sbA = lambda n, s, dt=F32: ctxA.enter_context(
        nc.sbuf_tensor(n, s, dt, side="right"))
    xnb = sbA("xnb", [128, NF, T], BF16)
    k.wait("tensor", "vector", cam)
    ln_mark = emit_ln(k, cm, ctx, xt, xnb, "ln3a")

    vtok = sb("vtok", [128, NTT + 1, 16, 65], BF16)
    wqb = sbA("wqb", [128, NF, D], BF16)
    wkb = sbA("wkb", [128, NF, D], BF16)
    wvb = sbA("wvb", [128, NF, D], BF16)
    spill = sbA("spill", [128, 4, TC], BF16)
    k.dma_batch("gpsimd", [
        (wb_[:, kc, :], W[wn][kc * 128:(kc + 1) * 128, :])
        for kc in range(NF)
        for wb_, wn in ((wqb, "l3a_wq"), (wkb, "l3a_wk"), (wvb, "l3a_wv"))
    ] + [
        (kd[:, T:T + 128], W["l3a_nullkpad"][:, :]),
    ])
    nvm = k.op("vector", lambda e: e.memset(vtok[:, NTT, :, :], 0.0), inc=1)
    k.wait("gpsimd", "vector", nvm)
    wm = k.dma("gpsimd", vtok[0:1, NTT, :, :],
               W["l3a_nullvrow"][:, :].rearrange("o (h e) -> o h e", h=16))
    rot = PsumRot(k, ctxA, "l3a", 2)
    k.wait("tensor", "gpsimd", wm)
    k.wait("tensor", "vector", ln_mark)
    k.wait("tensor", "vector", cam)

    spill_pend = [None] * 4
    spill_call = [0]

    def spill_evac(dst_dram, scaled):
        def evac(oc, tc, p, tem, release):
            i = spill_call[0] % 4
            spill_call[0] += 1
            k.wait("scalar", "sync", spill_pend[i])
            k.wait("scalar", "tensor", tem)
            if scaled:
                k.op("scalar", lambda e, pp=p, o=spill[0:64, i, :],
                     s=c_all[0:64, 2 * oc:2 * oc + 1]:
                     e.activation(o, pp[0:64, :], AF.Copy, scale=s), inc=1)
                em = k.op("scalar", lambda e, pp=p, o=spill[64:128, i, :],
                          s=c_all[64:128, 2 * oc + 1:2 * oc + 2]:
                          e.activation(o, pp[64:128, :], AF.Copy, scale=s), inc=1)
            else:
                em = k.op("scalar", lambda e, pp=p, o=spill[:, i, :]:
                          e.activation(o, pp[:, :], AF.Copy), inc=1)
            k.wait("sync", "scalar", em)
            dmm = k.dma("sync", dst_dram[oc * 128:(oc + 1) * 128,
                                         tc * TC:(tc + 1) * TC], spill[:, i, :])
            spill_pend[i] = dmm
            release(("scalar", em))
        return evac

    emit_proj_evac(k, cm, rot, wqb, xnb, None, spill_evac(qd, True))
    emit_proj_evac(k, cm, rot, wkb, xnb, None, spill_evac(kd, False))

    pV = ctxA.enter_context(nc.psum_tensor("l3a_pV", [128, D], F32))
    pv_pend = [None]
    for tt in range(NTT):
        t0 = tt * 128
        k.wait("tensor", "vector", pv_pend[0])
        for kc in range(NF):
            for half in range(2):
                k.op("tensor", lambda e, l=xnb[:, kc, t0:t0 + 128],
                     r=wvb[:, kc, half * 512:(half + 1) * 512],
                     pp=pV[:, bass.ts(half, 512)], first=(kc == 0),
                     last=(kc == NF - 1):
                     e.matmul(pp, l, r, start=first, stop=last),
                     inc=(1 if (kc == NF - 1 and half == 1) else 0))
        tmk = k.cnt["tensor"]
        k.wait("vector", "tensor", tmk)
        k.op("vector", lambda e, o=vtok[:, tt, :, 0:64],
             a=pV[:].rearrange("p (h e) -> p h e", h=16):
             e.tensor_copy(o, a), inc=1)
        k.op("vector", lambda e, o=vtok[:, tt, :, 64:65]:
             e.memset(o, 1.0), inc=1)
        pv_pend[0] = k.cnt["vector"]

    teA, scA, veA, syA, gpA = (k.cnt["tensor"], k.cnt["scalar"],
                               k.cnt["vector"], k.cnt["sync"], k.cnt["gpsimd"])
    ctxA.close()
    for e in ENGINES:
        k.wait(e, "tensor", teA)
        k.wait(e, "scalar", scA)
        k.wait(e, "vector", veA)
        k.wait(e, "sync", syA)

    ctxB = ExitStack()
    sbB = lambda n, s, dt=F32: ctxB.enter_context(
        nc.sbuf_tensor(n, s, dt, side="right"))
    qh = sbB("qh", [65, 2, T], BF16)
    kh = sbB("kh", [65, 2, T + 128], BF16)
    mqc = sbB("mqc", [128, NTT])
    mqt = sbB("mqt", [128, 12])
    mqtr = sbB("mqtr", [32, 128], BF16)
    util = sbB("util", [128, 2, 2 * TC], BF16)
    rrow = sbB("rrow", [1, TC])
    rrowb = sbB("rrowb", [1, TC], BF16)
    repb = sbB("repb", [64, 2, TC], BF16)
    oT = sbB("oT", [64, 2, TC], BF16)
    pD = [ctxB.enter_context(nc.psum_tensor(f"l3a_pD{i}", [128, 2 * TC], F32))
          for i in range(2)]
    pN = [ctxB.enter_context(nc.psum_tensor(f"l3a_pN{i}", [65, TC], F32))
          for i in range(2)]
    pRep = ctxB.enter_context(nc.psum_tensor("l3a_pRep", [64, TC], F32))
    pMq = ctxB.enter_context(nc.psum_tensor("l3a_pMq", [32, 128], F32))

    head_pend = [None, None]
    pd_pend = [None, None]
    st_pend = [None] * 4
    pn_pend = [None, None]
    rep_pend = [None]
    o_pend = [None, None]
    mq_pend = [None]

    vkm = k.op("vector", lambda e: e.memset(kh[64:65, :, :], 1.0), inc=1)
    k.wait("tensor", "vector", vkm)
    k.wait("gpsimd", "vector", vkm)
    for h in range(16):
        hb = h % 2
        k.wait("gpsimd", "tensor", head_pend[hb])
        dmq = k.dma("gpsimd", qh[0:64, hb, :], qd[64 * h:64 * h + 64, :])
        dmk = k.dma("gpsimd", kh[0:64, hb, :], kd[64 * h:64 * h + 64, :])
        k.wait("tensor", "gpsimd", dmk)
        k.wait("tensor", "vector", mq_pend[0])
        k.waitp("tensor", pd_pend[0])
        k.waitp("tensor", pd_pend[1])
        for qt in range(NTT):
            nkc = qt // 4 + 1
            for kci in range(nkc + 1):
                sl = kci % 4
                pp = pD[sl // 2][:, (sl % 2) * TC:(sl % 2) * TC + TC]
                k.waitp("tensor", st_pend[sl])
                if kci < nkc:
                    src_lo = kci * 512
                    wdt = 512
                else:
                    src_lo = T
                    wdt = 128
                k.op("tensor", lambda e, l=qh[0:64, hb, qt * 128:(qt + 1) * 128],
                     r=kh[0:64, hb, src_lo:src_lo + wdt], ppp=pp[:, 0:wdt]:
                     e.matmul(ppp, l, r, start=True, stop=True), inc=1)
                tms = k.cnt["tensor"]
                k.wait("vector", "tensor", tms)
                vlast = k.op("vector", lambda e, ppp=pp[:, 0:wdt],
                             o=mqt[:, kci:kci + 1]:
                             e.reduce_max(o, ppp, axis=AX.X), inc=1)
                st_pend[sl] = ("vector", vlast)
            k.op("vector", lambda e, o=mqc[:, qt:qt + 1],
                 a=mqt[:, 0:nkc + 1]:
                 e.reduce_max(o, a, axis=AX.X, negate=True), inc=1)
        k.wait("tensor", "vector", k.cnt["vector"])
        tmt = k.op("tensor", lambda e: e.transpose(pMq[:, :], mqc[:, :],
                                                   cm.identf[:, :]), inc=1)
        k.wait("scalar", "tensor", tmt)
        smt = k.op("scalar", lambda e: e.activation(mqtr[:, :], pMq[:, :],
                                                    AF.Copy), inc=1)
        k.wait("sync", "scalar", smt)
        k.dma("sync",
              mqd[h:h + 1, :].rearrange("o (a b) -> (o a) b", b=128),
              mqtr[:, :])
        mrm = k.dma("sync", qh[64:65, hb, :], mqd[h:h + 1, :])
        k.wait("tensor", "sync", mrm)

        for qc in range(NT):
            ni = qc % 2
            if pn_pend[ni] is not None:
                k.wait("tensor", pn_pend[ni][0], pn_pend[ni][1])
            kts = list(range((qc + 1) * 4)) + [NTT]
            nkp = (len(kts) + 1) // 2
            s_mark = {}

            def emit_s(kp, qc=qc, hb=hb):
                di = kp % 2
                k.waitp("tensor", pd_pend[di])
                subs = kts[2 * kp:2 * kp + 2]
                for si, kt in enumerate(subs):
                    klo = kt * 128 if kt < NTT else T
                    masks = []
                    for qsub in range(4):
                        qt = qc * 4 + qsub
                        if kt != NTT and kt > qt:
                            masks.append((qsub, cm.ones128, cm.negrow))
                        elif kt != NTT and kt == qt:
                            masks.append((qsub, cm.ident, cm.maskkgt))
                    k.op("tensor", lambda e, l=kh[:, hb, klo:klo + 128],
                         r=qh[:, hb, qc * TC:(qc + 1) * TC],
                         ppp=pD[di][:, bass.ts(si, TC)], last=(not masks):
                         e.matmul(ppp, l, r, start=True, stop=last), inc=1)
                    for mi, (qsub, ml, mr) in enumerate(masks):
                        k.op("tensor", lambda e, ml=ml, mr=mr,
                             ppp=pD[di][:, si * TC + qsub * 128:
                                        si * TC + qsub * 128 + 128],
                             last=(mi == len(masks) - 1):
                             e.matmul(ppp, ml[:, :], mr[:, :], start=False,
                                      stop=last),
                             inc=(1 if mi == len(masks) - 1 else 0))
                s_mark[kp] = k.cnt["tensor"]

            first_u = True
            emit_s(0)
            for kp in range(nkp):
                di = kp % 2
                if kp + 1 < nkp:
                    emit_s(kp + 1)
                subs = kts[2 * kp:2 * kp + 2]
                k.wait("scalar", "tensor", s_mark[kp])
                wdt2 = len(subs) * TC
                um = k.op("scalar", lambda e, a=pD[di][:, 0:wdt2],
                          o=util[:, di, 0:wdt2]:
                          e.activation(o, a, AF.Exp), inc=1)
                pd_pend[di] = ("scalar", um)
                k.wait("tensor", "scalar", um)
                for si, kt in enumerate(subs):
                    k.op("tensor", lambda e, l=vtok[:, kt, h, :],
                         r=util[:, di, bass.ts(si, TC)], pp=pN[ni],
                         first=first_u,
                         last=(kp == nkp - 1 and si == len(subs) - 1):
                         e.matmul(pp[:, :], l, r, start=first, stop=last),
                         inc=(1 if si == len(subs) - 1 else 0))
                    first_u = False
            tmn = k.cnt["tensor"]
            k.wait("vector", "tensor", tmn)
            k.wait("vector", "sync", o_pend[ni])
            k.op("vector", lambda e, pp=pN[ni]:
                 e.reciprocal(rrow[:, :], pp[64:65, :]), inc=1)
            vm1 = k.op("vector", lambda e:
                       e.tensor_copy(rrowb[:, :], rrow[:, :]), inc=1)
            k.wait("tensor", "vector", vm1)
            k.wait("tensor", "scalar", rep_pend[0])
            tmr = k.op("tensor", lambda e:
                       e.matmul(pRep[:, :], cm.ones128[0:1, 0:64], rrowb[:, :],
                                start=True, stop=True), inc=1)
            k.wait("scalar", "tensor", tmr)
            smr = k.op("scalar", lambda e, o=repb[:, ni, :]:
                       e.activation(o, pRep[:, :], AF.Copy), inc=1)
            rep_pend[0] = smr
            k.wait("vector", "scalar", smr)
            vm2 = k.op("vector", lambda e, pp=pN[ni], o=oT[:, ni, :],
                       rr=repb[:, ni, :]:
                       e.tensor_tensor(o, pp[0:64, :], rr, ALU.mult), inc=1)
            pn_pend[ni] = ("vector", vm2)
            k.wait("sync", "vector", vm2)
            o_pend[ni] = k.dma("sync", otd[64 * h:64 * h + 64,
                                           qc * TC:(qc + 1) * TC], oT[:, ni, :])
        head_pend[hb] = k.cnt["tensor"]
        mq_pend[0] = k.cnt["vector"]

    te2, sc2, ve2, sy2 = (k.cnt["tensor"], k.cnt["scalar"], k.cnt["vector"],
                          k.cnt["sync"])
    ctxB.close()
    for e in ENGINES:
        k.wait(e, "tensor", te2)
        k.wait(e, "scalar", sc2)
        k.wait(e, "vector", ve2)
        k.wait(e, "sync", sy2)

    emit_out_proj_dram(k, cm, ctx, W["l3a_wo"], None, otd, xt, xo, "l3aop")
    k.wait("sync", "sync", k.cnt["sync"])
    if not standalone:
        return _block_close(k, ctx)
    k.finish()
    ctx.close()
    in_names = ["xt"] + list(W.keys()) + cm.in_names
    return nc, in_names


def prep_weights_l3a(inp):
    nullk = np.asarray(inp["L3_attn_null_k"]).reshape(16, 64)
    nullv = np.asarray(inp["L3_attn_null_v"]).reshape(16, 64)
    nkpad = np.zeros((D, 128), dtype=np.float32)
    nkpad[:, 0] = nullk.reshape(D)
    nvrow = np.zeros((1, 16 * 65), dtype=np.float32)
    for h in range(16):
        nvrow[0, h * 65:h * 65 + 64] = nullv[h]
        nvrow[0, h * 65 + 64] = 1.0
    return {
        "l3a_wq": np.ascontiguousarray(inp["L3_attn_Wq"]),
        "l3a_wk": np.ascontiguousarray(inp["L3_attn_Wk"]),
        "l3a_wv": np.ascontiguousarray(inp["L3_attn_Wv"]),
        "l3a_wo": np.ascontiguousarray(inp["L3_attn_Wo"]),
        "l3a_nullkpad": nkpad,
        "l3a_nullvrow": nvrow,
        "l3a_ls": np.asarray(inp["L3_attn_logit_scale"]).reshape(1, 16).astype(np.float32),
    }



TWOPI = 6.283185307179586
INV2PI = 0.15915494309189535
RMAGIC = 12582912.0


def build_L2a(env=None):
    nc, k, ctx, cm, xt, xo, standalone = _block_env(env)
    W = declare_w(nc, {
        "l2a_wqT": (D, D), "l2a_wkT": (D, D), "l2a_wvT": (D, D),
        "l2a_woT": (D, D), "l2a_rfT": (64, 256),
    })
    qd = nc.dram_tensor("l2a_qd", [D, T], BF16, kind="Internal").ap()
    kd = nc.dram_tensor("l2a_kd", [D, T], BF16, kind="Internal").ap()
    otd = nc.dram_tensor("l2a_otd", [D, T], BF16, kind="Internal").ap()

    sb = lambda n, s, dt=F32: ctx.enter_context(nc.sbuf_tensor(n, s, dt))

    ctxA = ExitStack()
    sbA = lambda n, s, dt=F32: ctxA.enter_context(
        nc.sbuf_tensor(n, s, dt, side="right"))
    xnb = sbA("xnb", [128, NF, T], BF16)
    ln_mark = emit_ln(k, cm, ctx, xt, xnb, "ln2a")

    vtok = sb("vtok", [128, NTT, 16, 65], BF16)
    rfgq = sb("rfgq", [65, 4, 128], BF16)
    rfgk = sb("rfgk", [65, 512], BF16)
    wqb = sbA("wqb", [128, NF, D], BF16)
    wkb = sbA("wkb", [128, NF, D], BF16)
    wvb = sbA("wvb", [128, NF, D], BF16)
    spill = sbA("spill", [128, 4, TC], BF16)
    sqt = sbA("sqt", [128, D], BF16)
    k2c = sbA("k2c", [128, 16])
    kscol = sbA("kscol", [128, 16])
    wm = k.dma_batch("gpsimd", [
        (wb_[:, kc, :], W[wn][kc * 128:(kc + 1) * 128, :])
        for kc in range(NF)
        for wb_, wn in ((wqb, "l2a_wqT"), (wkb, "l2a_wkT"), (wvb, "l2a_wvT"))])
    rff = ctxA.enter_context(nc.sbuf_tensor("l2a_rff", [64, 256], F32,
                                             side="right"))
    wms = k.dma("sync", rff[:, :], W["l2a_rfT"][:, :])
    k.wait("vector", "sync", wms)
    for v in range(4):
        mcv = v % 2
        k.op("vector", lambda e, o=rfgq[0:64, v, :],
             a=rff[:, mcv * 128:(mcv + 1) * 128]:
             e.tensor_scalar_mul(o, a, -INV2PI), inc=1)
        k.op("vector", lambda e, o=rfgq[64:65, v, :], s=(0.25 if v < 2
                                                         else 0.5):
             e.memset(o, s), inc=1)
    k.op("vector", lambda e, o=rfgk[0:64, 0:256]:
         e.tensor_scalar_mul(o, rff[:, :], -INV2PI), inc=1)
    k.op("vector", lambda e, o=rfgk[0:64, 256:512]:
         e.tensor_scalar_mul(o, rff[:, :], -INV2PI), inc=1)
    k.op("vector", lambda e, o=rfgk[64:65, 0:256]: e.memset(o, 0.25), inc=1)
    wmv = k.op("vector", lambda e, o=rfgk[64:65, 256:512]:
               e.memset(o, 0.5), inc=1)
    k.wait("tensor", "vector", wmv)
    rot = PsumRot(k, ctxA, "l2a", 2)
    k.wait("tensor", "gpsimd", wm)
    k.wait("tensor", "vector", ln_mark)

    spill_pend = [None] * 4
    spill_call = [0]

    def spill_evac(dst_dram):
        def evac(oc, tc, p, tem, release):
            i = spill_call[0] % 4
            spill_call[0] += 1
            k.wait("scalar", "sync", spill_pend[i])
            k.wait("scalar", "tensor", tem)
            em = k.op("scalar", lambda e, pp=p, o=spill[:, i, :]:
                      e.activation(o, pp[:, :], AF.Copy), inc=1)
            k.wait("sync", "scalar", em)
            dmm = k.dma("sync", dst_dram[oc * 128:(oc + 1) * 128,
                                         tc * TC:(tc + 1) * TC], spill[:, i, :])
            spill_pend[i] = dmm
            release(("scalar", em))
        return evac

    emit_proj_evac(k, cm, rot, wqb, xnb, None, spill_evac(qd))
    emit_proj_evac(k, cm, rot, wkb, xnb, None, spill_evac(kd))

    pV = ctxA.enter_context(nc.psum_tensor("l2a_pV", [128, D], F32))
    pKt = ctxA.enter_context(nc.psum_tensor("l2a_pKt", [128, D], F32))
    pv_pend = [None]
    pk_pend = [None]
    for tt in range(NTT):
        t0 = tt * 128
        k.wait("tensor", "vector", pv_pend[0])
        for kc in range(NF):
            for half in range(2):
                k.op("tensor", lambda e, l=xnb[:, kc, t0:t0 + 128],
                     r=wvb[:, kc, half * 512:(half + 1) * 512],
                     pp=pV[:, bass.ts(half, 512)], first=(kc == 0),
                     last=(kc == NF - 1):
                     e.matmul(pp, l, r, start=first, stop=last), inc=0)
        k.waitp("tensor", pk_pend[0])
        for kc in range(NF):
            for half in range(2):
                k.op("tensor", lambda e, l=xnb[:, kc, t0:t0 + 128],
                     r=wkb[:, kc, half * 512:(half + 1) * 512],
                     pp=pKt[:, bass.ts(half, 512)], first=(kc == 0),
                     last=(kc == NF - 1):
                     e.matmul(pp, l, r, start=first, stop=last),
                     inc=(1 if (kc == NF - 1 and half == 1) else 0))
        tmk = k.cnt["tensor"]
        k.wait("scalar", "tensor", tmk)
        sm = k.op("scalar", lambda e: e.activation(sqt[:, :], pKt[:, :],
                                                   AF.Square), inc=1)
        pk_pend[0] = ("scalar", sm)
        k.wait("vector", "scalar", sm)
        k.op("vector", lambda e:
             e.reduce_sum(k2c[:, :], sqt[:].rearrange("p (h d) -> p h d", h=16),
                          axis=AX.X), inc=1)
        vm0 = k.cnt["vector"]
        k.wait("scalar", "vector", vm0)
        sm2 = k.op("scalar", lambda e:
                   e.activation(kscol[:, :], k2c[:, :], AF.Exp, scale=-0.5,
                                bias=cm.phicol[:, 1:2]), inc=1)
        k.wait("vector", "scalar", sm2)
        for h in range(16):
            k.op("vector", lambda e, o=vtok[:, tt, h, 0:64],
                 a=pV[:, 64 * h:64 * h + 64], s=kscol[:, h:h + 1]:
                 e.tensor_scalar_mul(o, a, s), inc=0)
            k.op("vector", lambda e, o=vtok[:, tt, h, 64:65],
                 a=kscol[:, h:h + 1]:
                 e.tensor_copy(o, a), inc=(1 if h == 15 else 0))
        pv_pend[0] = k.cnt["vector"]

    teA, scA, veA, syA = (k.cnt["tensor"], k.cnt["scalar"], k.cnt["vector"],
                          k.cnt["sync"])
    ctxA.close()
    for e in ENGINES:
        k.wait(e, "tensor", teA)
        k.wait(e, "scalar", scA)
        k.wait(e, "vector", veA)
        k.wait(e, "sync", syA)

    ctxB = ExitStack()
    sbB = lambda n, s, dt=F32: ctxB.enter_context(
        nc.sbuf_tensor(n, s, dt, side="right"))
    qh = sbB("qh", [65, 2, T], BF16)
    kh = sbB("kh", [65, 2, T], BF16)
    sqh = sbB("sqh", [64, T], BF16)
    sintb = sbB("sintb", [128, 2, TC], BF16)
    q2row = sbB("q2row", [1, T])
    qscrow = sbB("qscrow", [1, T], BF16)
    phiq = sbB("phiq", [128, 4, T], BF16)
    ftile = sbB("ftile", [128, 4, TC])
    fint = sbB("fint", [128, TC], I32)
    KVb = sbB("KVb", [128, 4, 65], BF16)
    phik = sbB("phik", [128, 2, 512], BF16)
    rrow = sbB("rrow", [1, TC])
    rrowb = sbB("rrowb", [1, TC], BF16)
    repb = sbB("repb", [64, 2, TC], BF16)
    oT = sbB("oT", [64, 2, TC], BF16)
    pP = [ctxB.enter_context(nc.psum_tensor(f"l2a_pP{i}", [128, TC], F32))
          for i in range(2)]
    pQ2 = ctxB.enter_context(nc.psum_tensor("l2a_pQ2", [1, TC], F32))
    pRp = ctxB.enter_context(nc.psum_tensor("l2a_pRp", [128, TC], F32))
    pKV = ctxB.enter_context(nc.psum_tensor("l2a_pKV", [65, TC], F32))
    pKVt = ctxB.enter_context(nc.psum_tensor("l2a_pKVt", [128, 65], BF16))
    pN = [ctxB.enter_context(nc.psum_tensor(f"l2a_pN{i}", [65, TC], F32))
          for i in range(2)]
    pRep = pRp

    head_pend = [None, None]
    q2_pend = [None]
    pp_pend = [None, None]
    prp_pend = [None]
    phik_pend = [None, None]
    pn_pend = [None, None]
    rep_pend = [None]
    o_pend = [None, None]
    kvb_pend = [None]
    rr_pend = [None]

    ks_last = [None, None]
    ds_last = [None, None]
    fs_call = [0]

    def fs_head(pp, src_slice, tem, w=TC):
        si = fs_call[0] % 2
        fs_call[0] += 1
        k.wait("scalar", "tensor", tem)
        k.waitp("scalar", ks_last[si])
        sm1 = k.op("scalar", lambda e, a=src_slice, o=ftile[:, si, 0:w]:
                   e.activation(o, a, AF.Copy, bias=RMAGIC), inc=1)
        k.wait("vector", "scalar", sm1)
        k.op("vector", lambda e, o=ftile[:, si, 0:w], a=ftile[:, si, 0:w]:
             e.tensor_scalar_add(o, a, -RMAGIC), inc=1)
        k.waitp("vector", ds_last[si])
        subm = k.op("vector", lambda e, o=ftile[:, 2 + si, 0:w], a=src_slice,
                    b=ftile[:, si, 0:w]:
                    e.tensor_tensor(o, a, b, ALU.subtract), inc=1)
        ks_last[si] = ("vector", subm)
        vm = k.op("vector", lambda e, o=ftile[:, 2 + si, 0:w]:
                  e.tensor_scalar(o, o, 0.49999997, -0.49999997, ALU.min,
                                  ALU.max), inc=1)
        return si, ("vector", subm), vm

    def fs_tail(si, vm, dst, w=TC, dst_pend=None):
        k.wait("scalar", "vector", vm)
        k.waitp("scalar", dst_pend)
        sm = k.op("scalar", lambda e, o=dst, a=ftile[:, 2 + si, 0:w]:
                  e.activation(o, a, AF.Sin, scale=TWOPI), inc=1)
        ds_last[si] = ("scalar", sm)
        return sm

    vam = k.op("vector", lambda e: e.memset(qh[64:65, :, :], 1.0), inc=1)
    vam = k.op("vector", lambda e: e.memset(kh[64:65, :, :], 1.0), inc=1)
    k.wait("tensor", "vector", vam)
    k.wait("gpsimd", "vector", vam)
    for h in range(16):
        hb = h % 2
        k.wait("gpsimd", "tensor", head_pend[hb])
        dmq = k.dma("gpsimd", qh[0:64, hb, :], qd[64 * h:64 * h + 64, :])
        dmk = k.dma("gpsimd", kh[0:64, hb, :], kd[64 * h:64 * h + 64, :])
        k.wait("scalar", "gpsimd", dmq)
        sm = k.op("scalar", lambda e, a=qh[0:64, hb, :]:
                  e.activation(sqh[:, :], a, AF.Square), inc=1)
        k.wait("tensor", "scalar", sm)
        k.wait("tensor", "vector", q2_pend[0])
        for tc in range(NT):
            k.op("tensor", lambda e, a=sqh[:, tc * TC:(tc + 1) * TC]:
                 e.matmul(pQ2[:, :], cm.onecol[0:64, :], a, start=True,
                          stop=True), inc=1)
            tmq = k.cnt["tensor"]
            k.wait("vector", "tensor", tmq)
            vq = k.op("vector", lambda e, o=q2row[0:1, tc * TC:(tc + 1) * TC]:
                      e.tensor_copy(o, pQ2[:, :]), inc=1)
            if tc < NT - 1:
                k.wait("tensor", "vector", vq)
        k.wait("scalar", "vector", k.cnt["vector"])
        sm = k.op("scalar", lambda e:
                  e.activation(qscrow[:, :], q2row[:, :], AF.Exp, scale=-0.5,
                               bias=cm.phicol[0:1, 1:2]), inc=1)
        q2_pend[0] = k.cnt["vector"]

        st_last = [None, None]
        tail = [None]

        def flush_tail():
            if tail[0] is None:
                return
            si, vm, tcp, vp, trp = tail[0]
            tail[0] = None
            smx = fs_tail(si, vm, sintb[:, si, :], dst_pend=st_last[si])
            k.wait("vector", "scalar", smx)
            k.wait("vector", "tensor", trp)
            vmx = k.op("vector", lambda e, a=sintb[:, si, :],
                       o=phiq[:, vp, tcp * TC:(tcp + 1) * TC]:
                       e.tensor_tensor(o, a, pRp[:, :], ALU.mult), inc=1)
            st_last[si] = ("vector", vmx)

        trp = None
        for tc in range(NT):
            flush_tail()
            prp_pend[0] = ("vector", k.cnt["vector"])
            k.wait("tensor", "scalar", sm)
            k.waitp("tensor", prp_pend[0])
            k.waitp("tensor", rep_pend[0])
            trp = k.op("tensor", lambda e, r=qscrow[0:1, tc * TC:(tc + 1) * TC]:
                       e.matmul(pRp[:, :], cm.ones128[:, :], r, start=True,
                                stop=True), inc=1)
            for v in range(4):
                pi2 = v % 2
                k.waitp("tensor", pp_pend[pi2])
                tem = k.op("tensor", lambda e, l=rfgq[:, v, :],
                           r=qh[:, hb, tc * TC:(tc + 1) * TC], pp=pP[pi2]:
                           e.matmul(pp[:, :], l, r, start=True, stop=True),
                           inc=1)
                si, rel, vm = fs_head(pP[pi2], pP[pi2][:, :], tem)
                pp_pend[pi2] = rel
                flush_tail()
                tail[0] = (si, vm, tc, v, trp)
        flush_tail()
        prp_pend[0] = ("vector", k.cnt["vector"])

        k.wait("tensor", "gpsimd", dmk)
        k.waitp("tensor", kvb_pend[0])
        ktail = [None]

        def kflush():
            if ktail[0] is None:
                return
            si, vm, ttp = ktail[0]
            ktail[0] = None
            smx = fs_tail(si, vm, phik[:, si, :], dst_pend=phik_pend[si])
            k.wait("tensor", "scalar", smx)
            um = k.op("tensor", lambda e, l=vtok[:, ttp, h, :],
                      r=phik[:, si, :], pp=pKV, first=(ttp == 0),
                      last=(ttp == NTT - 1):
                      e.matmul(pp[:, 0:512], l, r, start=first, stop=last),
                      inc=1)
            phik_pend[si] = ("tensor", um)

        for tt in range(NTT):
            pi2 = tt % 2
            k.waitp("tensor", pp_pend[pi2])
            tem = k.op("tensor", lambda e, l=kh[:, hb, tt * 128:(tt + 1) * 128],
                       r=rfgk[:, :], pp=pP[pi2]:
                       e.matmul(pp[:, :], l, r, start=True, stop=True),
                       inc=1)
            si, rel, vm = fs_head(pP[pi2], pP[pi2][:, :], tem)
            pp_pend[pi2] = rel
            kflush()
            ktail[0] = (si, vm, tt)
        kflush()
        tkv = k.cnt["tensor"]
        k.wait("scalar", "tensor", tkv)
        smkv = k.op("scalar", lambda e: e.activation(phik[0:65, 0, :], pKV[:, 0:512],
                                                     AF.Copy), inc=1)
        k.wait("tensor", "scalar", smkv)
        for fc4 in range(4):
            k.op("tensor", lambda e, a=phik[0:65, 0, fc4 * 128:(fc4 + 1) * 128]:
                 e.transpose(pKVt[:, 0:65], a, cm.ident[0:65, 0:65]), inc=1)
            tmt = k.cnt["tensor"]
            k.wait("vector", "tensor", tmt)
            vkv = k.op("vector", lambda e, o=KVb[:, fc4, :]:
                       e.tensor_copy(o, pKVt[0:128, 0:65]), inc=1)
            if fc4 < 3:
                k.wait("tensor", "vector", vkv)
        kvb_pend[0] = ("vector", k.cnt["vector"])
        k.wait("tensor", "vector", k.cnt["vector"])

        for qc in range(NT):
            ni = qc % 2
            k.waitp("tensor", pn_pend[ni])
            for fc4 in range(4):
                k.op("tensor", lambda e, l=KVb[:, fc4, :],
                     r=phiq[:, fc4, qc * TC:(qc + 1) * TC], pp=pN[ni],
                     first=(fc4 == 0), last=(fc4 == 3):
                     e.matmul(pp[:, :], l, r, start=first, stop=last),
                     inc=(1 if fc4 == 3 else 0))
            tmn = k.cnt["tensor"]
            k.wait("vector", "tensor", tmn)
            k.wait("vector", "sync", o_pend[ni])
            k.waitp("vector", rr_pend[0])
            vmx = k.op("vector", lambda e, pp=pN[ni]:
                       e.tensor_scalar_max(rrow[:, :], pp[64:65, :], 1e-6),
                       inc=1)
            k.wait("scalar", "vector", vmx)
            k.wait("scalar", "tensor", tmn)
            k.op("scalar", lambda e:
                 e.activation(rrow[:, :], rrow[:, :], AF.Ln), inc=1)
            vm1 = k.op("scalar", lambda e:
                       e.activation(rrowb[:, :], rrow[:, :], AF.Exp,
                                    scale=-1.0), inc=1)
            rr_pend[0] = ("scalar", vm1)
            k.wait("tensor", "scalar", vm1)
            k.waitp("tensor", rep_pend[0])
            k.waitp("tensor", prp_pend[0])
            tmr = k.op("tensor", lambda e:
                       e.matmul(pRep[0:64, :], cm.ones128[0:1, 0:64], rrowb[:, :],
                                start=True, stop=True), inc=1)
            k.wait("scalar", "tensor", tmr)
            smr = k.op("scalar", lambda e, o=repb[:, ni, :]:
                       e.activation(o, pRep[0:64, :], AF.Copy), inc=1)
            rep_pend[0] = ("scalar", smr)
            prp_pend[0] = ("scalar", smr)
            k.wait("vector", "scalar", smr)
            vm2 = k.op("vector", lambda e, pp=pN[ni], o=oT[:, ni, :],
                       rr=repb[:, ni, :]:
                       e.tensor_tensor(o, pp[0:64, :], rr, ALU.mult), inc=1)
            pn_pend[ni] = ("vector", vm2)
            k.wait("sync", "vector", vm2)
            o_pend[ni] = k.dma("sync", otd[64 * h:64 * h + 64,
                                           qc * TC:(qc + 1) * TC], oT[:, ni, :])
        head_pend[hb] = k.cnt["tensor"]

    te2, sc2, ve2, sy2 = (k.cnt["tensor"], k.cnt["scalar"], k.cnt["vector"],
                          k.cnt["sync"])
    ctxB.close()
    for e in ENGINES:
        k.wait(e, "tensor", te2)
        k.wait(e, "scalar", sc2)
        k.wait(e, "vector", ve2)
        k.wait(e, "sync", sy2)

    emit_out_proj_dram(k, cm, ctx, W["l2a_woT"], None, otd, xt, xo, "l2aop")
    k.wait("sync", "sync", k.cnt["sync"])
    if not standalone:
        return _block_close(k, ctx)
    k.finish()
    ctx.close()
    in_names = ["xt"] + list(W.keys()) + cm.in_names
    return nc, in_names


def prep_weights_l2a(inp):
    return {
        "l2a_wqT": np.ascontiguousarray(inp["L2_attn_wq"].T),
        "l2a_wkT": np.ascontiguousarray(inp["L2_attn_wk"].T),
        "l2a_wvT": np.ascontiguousarray(inp["L2_attn_wv"].T),
        "l2a_woT": np.ascontiguousarray(inp["L2_attn_wo"].T),
        "l2a_rfT": np.ascontiguousarray(inp["L2_attn_rf"].T),
    }



def build_L0a(env=None):
    nc, k, ctx, cm, xt, xo, standalone = _block_env(env)
    W = declare_w(nc, {
        "l0a_wq": (D, D), "l0a_wk": (D, D), "l0a_wv": (D, D), "l0a_wo": (D, D),
    })
    otd = nc.dram_tensor("l0a_otd", [D, T], BF16, kind="Internal").ap()
    sb = lambda n, s, dt=F32: ctx.enter_context(nc.sbuf_tensor(n, s, dt))

    ctxA = ExitStack()
    sbA = lambda n, s, dt=F32: ctxA.enter_context(
        nc.sbuf_tensor(n, s, dt, side="right"))
    xnb = sbA("xnb", [128, NF, T], BF16)
    ln_mark = emit_ln(k, cm, ctx, xt, xnb, "ln0a")

    qT = sb("l0a_qT", [128, NF, T], BF16)
    vkTb = sb("l0a_vkTb", [128, 8, 65], BF16)
    vkTacc = sb("l0a_vkTacc", [128, 8, 65])

    cosk = sbA("l0a_cosk", [128, NTT, 32], BF16)
    sink = sbA("l0a_sink", [128, NTT, 32], BF16)
    wqb = sbA("wqb", [128, NF, D], BF16)
    wkb = sbA("wkb", [128, NF, D], BF16)
    wvb = sbA("wvb", [128, NF, D], BF16)
    rope_f = sbA("rope_f", [128, 2, D])
    ktile = sbA("ktile", [128, 2, D], BF16)
    qtile = sbA("qtile", [128, 2, D], BF16)
    vaug = sbA("vaug", [128, 2, 16, 65], BF16)

    cm2 = k.dma_batch("gpsimd", [
        (wb_[:, kc, :], W[wn][kc * 128:(kc + 1) * 128, :])
        for kc in range(NF)
        for wb_, wn in ((wqb, "l0a_wq"), (wkb, "l0a_wk"), (wvb, "l0a_wv"))
    ] + [
        (cosk[:, :, :],
         cm.aps["c_l0cos"][:, :].rearrange("(a p) f -> p a f", p=128)),
        (sink[:, :, :],
         cm.aps["c_l0sin"][:, :].rearrange("(a p) f -> p a f", p=128)),
    ])
    k.wait("tensor", "gpsimd", cm2)
    k.wait("tensor", "vector", ln_mark)
    k.wait("vector", "gpsimd", cm2)

    pA = ctxA.enter_context(nc.psum_tensor("l0a_pA", [128, D], F32))
    pB = pA
    pVK = ctxA.enter_context(nc.psum_tensor("l0a_pVK", [128, 1024], F32))
    pT = ctxA.enter_context(nc.psum_tensor("l0a_pT", [128, 128], BF16))

    pa_pend = [None]
    pb_pend = pa_pend
    vz = k.op("vector", lambda e: e.memset(vkTacc[:, :, :], 0.0), inc=1)
    pt_pend = [None]
    ve_tiles = [None, None]

    def emit_rope(psrc, dst_tile, tt, relu_dst):
        cosb = bass.AP(cosk.ap().tensor, cosk[:, tt, :].offset,
                       [[NTT * 32, 128], [0, 16], [1, 32]])
        sinb = bass.AP(sink.ap().tensor, sink[:, tt, :].offset,
                       [[NTT * 32, 128], [0, 16], [1, 32]])
        x1 = bass.AP(psrc.ap().tensor, psrc.ap().offset,
                     [[psrc.shape[1], 128], [64, 16], [2, 32]])
        x2 = bass.AP(psrc.ap().tensor, psrc.ap().offset + 1,
                     [[psrc.shape[1], 128], [64, 16], [2, 32]])
        r3 = lambda i: rope_f[:, 0 if i == 2 else i, 0:512].rearrange(
            "p (h f) -> p h f", h=16)
        k.op("vector", lambda e, a=x1, b=cosb, o=r3(0):
             e.tensor_tensor(o, a, b, ALU.mult), inc=1)
        k.op("vector", lambda e, a=x2, b=sinb, o=r3(1):
             e.tensor_tensor(o, a, b, ALU.mult), inc=1)
        k.op("vector", lambda e, o=r3(2):
             e.tensor_sub(o, r3(0), r3(1)), inc=1)
        k.op("vector", lambda e, o=dst_tile[:].rearrange(
            "p (h q f) -> p (h q) f", h=16, q=2)[:, 0::2, :], a=r3(2):
             e.tensor_relu(o, a), inc=1)
        k.op("vector", lambda e, a=x1, b=sinb, o=r3(0):
             e.tensor_tensor(o, a, b, ALU.mult), inc=1)
        k.op("vector", lambda e, a=x2, b=cosb, o=r3(1):
             e.tensor_tensor(o, a, b, ALU.mult), inc=1)
        k.op("vector", lambda e, o=r3(2):
             e.tensor_add(o, r3(0), r3(1)), inc=1)
        m = k.op("vector", lambda e, o=dst_tile[:].rearrange(
            "p (h q f) -> p (h q) f", h=16, q=2)[:, 1::2, :], a=r3(2):
             e.tensor_relu(o, a), inc=1)
        return m

    for tt in range(NTT):
        t0 = tt * 128
        buf = tt % 2
        k.waitp("tensor", pa_pend[0])
        k.waitp("tensor", ve_tiles[buf])
        for kc in range(NF):
            for half in range(2):
                k.op("tensor", lambda e, l=xnb[:, kc, t0:t0 + 128],
                     r=wqb[:, kc, half * 512:(half + 1) * 512],
                     pp=pA[:, bass.ts(half, 512)], first=(kc == 0),
                     last=(kc == NF - 1):
                     e.matmul(pp, l, r, start=first, stop=last),
                     inc=(1 if (kc == NF - 1 and half == 1) else 0))
        tmq = k.cnt["tensor"]
        k.wait("vector", "tensor", tmq)
        k.waitp("vector", ve_tiles[buf])
        qm = emit_rope(pA, qtile[:, buf, :], tt, None)
        pa_pend[0] = ("vector", qm)
        k.wait("tensor", "vector", qm)
        for fc in range(NF):
            k.waitp("tensor", pt_pend[0])
            k.op("tensor", lambda e, a=qtile[:, buf, fc * 128:(fc + 1) * 128]:
                 e.transpose(pT[:, :], a, cm.ident[:, :]), inc=1)
            tmt = k.cnt["tensor"]
            k.wait("vector", "tensor", tmt)
            vmt = k.op("vector", lambda e, o=qT[:, fc, t0:t0 + 128]:
                       e.tensor_copy(o, pT[:, :]), inc=1)
            pt_pend[0] = ("vector", vmt)
        k.waitp("tensor", pb_pend[0])
        for kc in range(NF):
            for half in range(2):
                k.op("tensor", lambda e, l=xnb[:, kc, t0:t0 + 128],
                     r=wkb[:, kc, half * 512:(half + 1) * 512],
                     pp=pB[:, bass.ts(half, 512)], first=(kc == 0),
                     last=(kc == NF - 1):
                     e.matmul(pp, l, r, start=first, stop=last),
                     inc=(1 if (kc == NF - 1 and half == 1) else 0))
        tmk = k.cnt["tensor"]
        k.wait("vector", "tensor", tmk)
        km = emit_rope(pB, ktile[:, buf, :], tt, None)
        pb_pend[0] = ("vector", km)
        k.wait("tensor", "vector", km)
        for kc in range(NF):
            for half in range(2):
                k.op("tensor", lambda e, l=xnb[:, kc, t0:t0 + 128],
                     r=wvb[:, kc, half * 512:(half + 1) * 512],
                     pp=pB[:, bass.ts(half, 512)], first=(kc == 0),
                     last=(kc == NF - 1):
                     e.matmul(pp, l, r, start=first, stop=last),
                     inc=(1 if (kc == NF - 1 and half == 1) else 0))
        tmv = k.cnt["tensor"]
        k.wait("vector", "tensor", tmv)
        k.op("vector", lambda e, o=vaug[:, buf, :, 0:64],
             a=pB[:].rearrange("p (h v) -> p h v", h=16):
             e.tensor_copy(o, a), inc=1)
        vm = k.op("vector", lambda e, o=vaug[:, buf, :, 64:65]:
                  e.memset(o, 1.0), inc=1)
        pb_pend[0] = ("vector", vm)
        k.wait("tensor", "vector", vm)
        if tt == 0:
            k.wait("tensor", "vector", vz)
        for h in range(16):
            rb = 64 * (h % 2)
            k.op("tensor", lambda e, l=ktile[:, buf, 64 * h:64 * h + 64],
                 r=vaug[:, buf, h, :],
                 pp=pVK[rb:rb + 64, 128 * (h // 2):128 * (h // 2) + 65]:
                 e.matmul(pp, l, r, start=True, stop=True),
                 inc=(1 if h == 15 else 0))
        tvk = k.cnt["tensor"]
        k.wait("vector", "tensor", tvk)
        for h in range(16):
            rb = 64 * (h % 2)
            k.op("vector", lambda e, o=vkTacc[rb:rb + 64, h // 2, :],
                 a=pVK[rb:rb + 64, 128 * (h // 2):128 * (h // 2) + 65]:
                 e.tensor_tensor(o, o, a, ALU.add), inc=1)
        ve_tiles[buf] = ("vector", k.cnt["vector"])

    vkm = k.op("vector", lambda e:
               e.tensor_copy(vkTb[:, :, :], vkTacc[:, :, :]), inc=1)

    teA, scA, veA, syA = (k.cnt["tensor"], k.cnt["scalar"], k.cnt["vector"],
                          k.cnt["sync"])
    ctxA.close()
    for e in ENGINES:
        k.wait(e, "tensor", teA)
        k.wait(e, "scalar", scA)
        k.wait(e, "vector", veA)
        k.wait(e, "sync", syA)

    ctxB = ExitStack()
    sbB = lambda n, s, dt=F32: ctxB.enter_context(
        nc.sbuf_tensor(n, s, dt, side="right"))
    rrow = sbB("rrow", [1, TC])
    rrowb = sbB("rrowb", [1, TC], BF16)
    repb = sbB("repb", [64, 2, TC], BF16)
    oT = sbB("oT", [64, 2, TC], BF16)
    pN = [ctxB.enter_context(nc.psum_tensor(f"l0a_pN{i}", [65, TC], F32))
          for i in range(2)]
    pRep = ctxB.enter_context(nc.psum_tensor("l0a_pRep", [64, TC], F32))
    pn_pend = [None, None]
    rep_pend = [None]
    o_pend = [None, None]
    for h in range(16):
        fcq = h // 2
        rbase = 64 * (h % 2)
        for tc in range(NT):
            ni = (h * NT + tc) % 2
            k.waitp("tensor", pn_pend[ni])
            k.op("tensor", lambda e, l=vkTb[rbase:rbase + 64, h // 2, :],
                 r=qT[rbase:rbase + 64, fcq, tc * TC:(tc + 1) * TC], pp=pN[ni]:
                 e.matmul(pp[:, :], l, r, start=True, stop=True), inc=1)
            tmn = k.cnt["tensor"]
            k.wait("vector", "tensor", tmn)
            k.wait("vector", "sync", o_pend[ni])
            k.wait("scalar", "tensor", tmn)
            k.op("scalar", lambda e, pp=pN[ni]:
                 e.activation(rrow[:, :], pp[64:65, :], AF.Ln,
                              bias=cm.eps[0:1, 1:2]), inc=1)
            vm1 = k.op("scalar", lambda e:
                       e.activation(rrowb[:, :], rrow[:, :], AF.Exp,
                                    scale=-1.0), inc=1)
            k.wait("tensor", "scalar", vm1)
            k.waitp("tensor", rep_pend[0])
            tmr = k.op("tensor", lambda e:
                       e.matmul(pRep[:, :], cm.ones128[0:1, 0:64], rrowb[:, :],
                                start=True, stop=True), inc=1)
            k.wait("scalar", "tensor", tmr)
            smr = k.op("scalar", lambda e, o=repb[:, ni, :]:
                       e.activation(o, pRep[:, :], AF.Copy), inc=1)
            rep_pend[0] = ("scalar", smr)
            k.wait("vector", "scalar", smr)
            vm2 = k.op("vector", lambda e, pp=pN[ni], o=oT[:, ni, :],
                       rr=repb[:, ni, :]:
                       e.tensor_tensor(o, pp[0:64, :], rr, ALU.mult), inc=1)
            pn_pend[ni] = ("vector", vm2)
            k.wait("sync", "vector", vm2)
            o_pend[ni] = k.dma("sync", otd[64 * h:64 * h + 64,
                                           tc * TC:(tc + 1) * TC], oT[:, ni, :])

    te2, sc2, ve2, sy2 = (k.cnt["tensor"], k.cnt["scalar"], k.cnt["vector"],
                          k.cnt["sync"])
    ctxB.close()
    for e in ENGINES:
        k.wait(e, "tensor", te2)
        k.wait(e, "scalar", sc2)
        k.wait(e, "vector", ve2)
        k.wait(e, "sync", sy2)

    emit_out_proj_dram(k, cm, ctx, W["l0a_wo"], None, otd, xt, xo, "l0aop")
    k.wait("sync", "sync", k.cnt["sync"])
    if not standalone:
        return _block_close(k, ctx)
    k.finish()
    ctx.close()
    in_names = ["xt"] + list(W.keys()) + cm.in_names
    return nc, in_names


def prep_weights_l0a(inp):
    return {
        "l0a_wq": np.ascontiguousarray(inp["L0_attn_Wq"]),
        "l0a_wk": np.ascontiguousarray(inp["L0_attn_Wk"]),
        "l0a_wv": np.ascontiguousarray(inp["L0_attn_Wv"]),
        "l0a_wo": np.ascontiguousarray(inp["L0_attn_Wo"]),
    }



def build_L3f(env=None):
    nc, k, ctx, cm, xt, xo, standalone = _block_env(env)
    W = declare_w(nc, {
        "l3f_wg": (D, D), "l3f_bg": (1, D), "l3f_we": (8, D, D),
        "l3f_be": (8, 1, D),
    })
    sb = lambda n, s, dt=F32: ctx.enter_context(nc.sbuf_tensor(n, s, dt))

    ctxA = ExitStack()
    sbA = lambda n, s, dt=F32: ctxA.enter_context(
        nc.sbuf_tensor(n, s, dt, side="right"))
    xnb = sbA("xnb", [128, NF, T], BF16)
    ln_mark = emit_ln(k, cm, ctx, xt, xnb, "ln3f")

    acc = sb("l3f_acc", [128, NTT, D], BF16)
    ecol = sb("l3f_ecol", [128, NTT])
    pcol = sb("l3f_pcol", [128, NTT])

    ctxG = ExitStack()
    sbG = lambda n, s, dt=F32: ctxG.enter_context(
        nc.sbuf_tensor(n, s, dt, side="right"))
    wgb = sbG("wgb", [128, NF, D], BF16)
    bgb = sbG("bgb", [1, D], BF16)
    iotan = sbG("iotan", [128, D])
    gex = sbG("gex", [128, D], BF16)
    mxc = sbG("mxc", [128, 1])
    nmxc = sbG("nmxc", [128, 1])
    sec = sbG("sec", [128, 1])
    eqv = sbG("eqv", [128, D])
    idxc = sbG("idxc", [128, 4])
    wm = k.dma_batch("gpsimd", [
        (wgb[:, kc, :], W["l3f_wg"][kc * 128:(kc + 1) * 128, :])
        for kc in range(NF)] + [(bgb[:, :], W["l3f_bg"][:, :])])
    im = k.dma("sync", iotan[:, :], cm.aps["c_iotaneg"][:, :])
    k.wait("tensor", "gpsimd", wm)
    k.wait("tensor", "vector", ln_mark)
    k.wait("vector", "sync", im)

    pG = ctxG.enter_context(nc.psum_tensor("l3f_pG", [128, D], F32))
    pg_pend = [None]
    for tt in range(NTT):
        t0 = tt * 128
        k.waitp("tensor", pg_pend[0])
        for kc in range(NF):
            for half in range(2):
                k.op("tensor", lambda e, l=xnb[:, kc, t0:t0 + 128],
                     r=wgb[:, kc, half * 512:(half + 1) * 512],
                     pp=pG[:, bass.ts(half, 512)], first=(kc == 0):
                     e.matmul(pp, l, r, start=first, stop=False), inc=0)
        for half in range(2):
            k.op("tensor", lambda e, pp=pG[:, bass.ts(half, 512)],
                 b=bgb[0:1, half * 512:(half + 1) * 512]:
                 e.matmul(pp, cm.ones128[:, :], b, start=False, stop=True),
                 inc=(1 if half == 1 else 0))
        tmg = k.cnt["tensor"]
        k.wait("vector", "tensor", tmg)
        k.op("vector", lambda e: e.reduce_max(mxc[:, :], pG[:, :], axis=AX.X),
             inc=1)
        k.op("vector", lambda e: e.tensor_scalar_mul(nmxc[:, :], mxc[:, :],
                                                     -1.0), inc=1)
        vmx = k.cnt["vector"]
        k.wait("scalar", "vector", vmx)
        sme = k.op("scalar", lambda e:
                   e.activation(gex[:, :], pG[:, :], AF.Exp, bias=nmxc[:, :],
                                accum_out=sec[:, :]), inc=1)
        k.wait("vector", "scalar", sme)
        k.op("vector", lambda e, o=pcol[:, tt:tt + 1]:
             e.reciprocal(o, sec[:, :]), inc=1)
        k.op("vector", lambda e:
             e.tensor_single_scalar(eqv[:, :], pG[:, :], mxc[:, :], ALU.is_ge),
             inc=1)
        pg_pend[0] = ("vector", k.cnt["vector"])
        k.op("vector", lambda e:
             e.tensor_mul(eqv[:, :], eqv[:, :], iotan[:, :]), inc=1)
        k.op("vector", lambda e, o=idxc[:, 0:1]:
             e.tensor_reduce(o, eqv[:, :], AX.X, ALU.min), inc=1)
        k.op("vector", lambda e, o=idxc[:, 0:1]:
             e.tensor_scalar_add(o, o, 1e9), inc=1)
        k.op("vector", lambda e, o=idxc[:, 1:2], a=idxc[:, 0:1]:
             e.tensor_scalar(o, a, 0.125, -0.4375, ALU.mult, ALU.add), inc=1)
        k.op("vector", lambda e, o=idxc[:, 1:2], a=idxc[:, 1:2]:
             e.tensor_scalar_add(o, a, RMAGIC), inc=1)
        k.op("vector", lambda e, o=idxc[:, 1:2], a=idxc[:, 1:2]:
             e.tensor_scalar_add(o, a, -RMAGIC), inc=1)
        k.op("vector", lambda e, o=ecol[:, tt:tt + 1], a=idxc[:, 1:2],
             b=idxc[:, 0:1]:
             e.scalar_tensor_tensor(o, a, -8.0, b, ALU.mult, ALU.add), inc=1)

    marksG = {e: k.cnt[e] for e in ENGINES}
    ctxG.close()
    for e in ENGINES:
        for o in ("tensor", "vector", "scalar", "sync"):
            if o != e:
                k.wait(e, o, marksG[o])

    ctxB = ExitStack()
    sbB = lambda n, s, dt=F32: ctxB.enter_context(
        nc.sbuf_tensor(n, s, dt, side="right"))
    web = sbB("web", [128, 2, NF, D], BF16)
    beb = sbB("beb", [1, 2, D], BF16)
    zb = sbB("zb", [128, D], BF16)
    mcol = sbB("mcol", [128, 2])
    pZ = ctxB.enter_context(nc.psum_tensor("l3f_pZ", [128, D], F32))
    az = k.op("vector", lambda e: e.memset(acc[:, :, :], 0.0), inc=1)
    k.wait("tensor", "vector", az)
    we_pend = [None, None]
    pz_pend = [None]
    zb_pend = [None]
    for ei in range(8):
        eb = ei % 2
        k.waitp("gpsimd", we_pend[eb])
        wm = k.dma_batch("gpsimd", [
            (web[:, eb, kc, :], W["l3f_we"][ei, kc * 128:(kc + 1) * 128, :])
            for kc in range(NF)] + [(beb[0:1, eb, :], W["l3f_be"][ei, :, :])])
        k.wait("tensor", "gpsimd", wm)
        for tt in range(NTT):
            t0 = tt * 128
            k.waitp("tensor", pz_pend[0])
            for kc in range(NF):
                for half in range(2):
                    k.op("tensor", lambda e, l=xnb[:, kc, t0:t0 + 128],
                         r=web[:, eb, kc, half * 512:(half + 1) * 512],
                         pp=pZ[:, bass.ts(half, 512)], first=(kc == 0):
                         e.matmul(pp, l, r, start=first, stop=False), inc=0)
            for half in range(2):
                k.op("tensor", lambda e, pp=pZ[:, bass.ts(half, 512)],
                     b=beb[0:1, eb, half * 512:(half + 1) * 512]:
                     e.matmul(pp, cm.ones128[:, :], b, start=False, stop=True),
                     inc=(1 if half == 1 else 0))
            tmz = k.cnt["tensor"]
            k.wait("scalar", "tensor", tmz)
            k.waitp("scalar", zb_pend[0])
            smz = k.op("scalar", lambda e:
                       e.activation(zb[:, :], pZ[:, :], AF.Relu), inc=1)
            pz_pend[0] = ("scalar", smz)
            k.wait("vector", "scalar", smz)
            k.op("vector", lambda e, o=mcol[:, 0:1], a=ecol[:, tt:tt + 1],
                 ef=float(ei):
                 e.tensor_single_scalar(o, a, ef, ALU.is_equal), inc=1)
            k.op("vector", lambda e, o=mcol[:, 1:2], a=mcol[:, 0:1],
                 b=pcol[:, tt:tt + 1]:
                 e.tensor_mul(o, a, b), inc=1)
            k.op("vector", lambda e, o=acc[:, tt, :], s=mcol[:, 1:2]:
                 e.scalar_tensor_tensor(o, zb[:, :], s, o, ALU.mult, ALU.add),
                 inc=1)
            zb_pend[0] = ("vector", k.cnt["vector"])
        we_pend[eb] = ("tensor", k.cnt["tensor"])

    marksB = {e: k.cnt[e] for e in ENGINES}
    ctxB.close()
    for e in ENGINES:
        for o in ("tensor", "vector", "scalar", "sync"):
            if o != e:
                k.wait(e, o, marksB[o])
    ctxA.close()

    ctxC = ExitStack()
    sbC = lambda n, s, dt=F32: ctxC.enter_context(
        nc.sbuf_tensor(n, s, dt, side="right"))
    xres = sbC("xres", [128, 2, TC])
    pT = ctxC.enter_context(nc.psum_tensor("l3f_pT", [128, 128], BF16))
    pend_store = [None, None]
    pt_pend = [None]
    call = [0]
    for fc in range(NF):
        for tc in range(NT):
            buf = call[0] % 2
            call[0] += 1
            k.wait("sync", "sync", pend_store[buf])
            dmx = k.dma("sync", xres[:, buf, :],
                        xt[fc * 128:(fc + 1) * 128, tc * TC:(tc + 1) * TC])
            for q4 in range(4):
                tt = tc * 4 + q4
                k.waitp("tensor", pt_pend[0])
                k.op("tensor", lambda e, a=acc[:, tt, fc * 128:(fc + 1) * 128]:
                     e.transpose(pT[:, :], a, cm.ident[:, :]), inc=1)
                tmt = k.cnt["tensor"]
                k.wait("vector", "tensor", tmt)
                if q4 == 0:
                    k.wait("vector", "sync", dmx)
                vmt = k.op("vector", lambda e, o=xres[:, buf,
                                                      q4 * 128:q4 * 128 + 128]:
                           e.tensor_tensor(o, o, pT[:, :], ALU.add), inc=1)
                pt_pend[0] = ("vector", vmt)
            k.wait("sync", "vector", k.cnt["vector"])
            pend_store[buf] = k.dma(
                "sync", xo[fc * 128:(fc + 1) * 128, tc * TC:(tc + 1) * TC],
                xres[:, buf, :])

    k.wait("sync", "sync", k.cnt["sync"])
    ctxC.close()
    if not standalone:
        return _block_close(k, ctx)
    k.finish()
    ctx.close()
    in_names = ["xt"] + list(W.keys()) + cm.in_names
    return nc, in_names


def prep_weights_l3f(inp):
    return {
        "l3f_wg": np.ascontiguousarray(inp["L3_ffn_Wg"]),
        "l3f_bg": inp["L3_ffn_bg"].reshape(1, D).astype(np.float32),
        "l3f_we": np.ascontiguousarray(inp["L3_ffn_We"]),
        "l3f_be": np.ascontiguousarray(inp["L3_ffn_be"]).reshape(8, 1, D),
    }



HM = 4096


def build_L1f(env=None):
    nc, k, ctx, cm, xt, xo, standalone = _block_env(env)
    W = declare_w(nc, {
        "l1f_gw": (D, 8), "l1f_w1": (8, D, HM), "l1f_b1": (8, 128, 32),
        "l1f_w2": (8, HM, D), "l1f_b2": (8, D), "l1f_gamma": (1, D),
        "l1f_beta": (1, D),
    })
    cw_d = nc.dram_tensor("l1f_cwd", [8, T], F32, kind="Internal").ap()
    h_d = nc.dram_tensor("l1f_hd", [8, 32, 128, T], BF16, kind="Internal").ap()
    sb = lambda n, s, dt=F32: ctx.enter_context(nc.sbuf_tensor(n, s, dt))

    ctxA = ExitStack()
    sbA = lambda n, s, dt=F32: ctxA.enter_context(
        nc.sbuf_tensor(n, s, dt, side="right"))
    xnb = sbA("xnb", [128, NF, T], BF16)
    ln_mark = emit_ln(k, cm, ctx, xt, xnb, "ln1f")
    xn_d = nc.dram_tensor("l1f_xnd", [D, T], BF16, kind="Internal").ap()
    spm = None
    for fc in range(NF):
        spm = k.dma("sync", xn_d[fc * 128:(fc + 1) * 128, :], xnb[:, fc, :])
    k.wait("vector", "sync", spm)
    emit_ln_sbuf_affine(k, cm, ctx, xnb, W["l1f_gamma"], W["l1f_beta"], "ln1fi")

    ctxG = ExitStack()
    sbG = lambda n, s, dt=F32: ctxG.enter_context(
        nc.sbuf_tensor(n, s, dt, side="right"))
    gwb = sbG("gwb", [128, NF, 8], BF16)
    ex = sbG("ex", [128, 8])
    c1 = sbG("c1", [128, 6])
    cwc = sbG("cwc", [128, 8])
    cwtr = sbG("cwtr", [8, 128])
    mskd = sbG("mskd", [128, 8])
    wm = k.dma_batch("gpsimd", [
        (gwb[:, kc, :], W["l1f_gw"][kc * 128:(kc + 1) * 128, :])
        for kc in range(NF)])
    k.wait("tensor", "gpsimd", wm)
    k.wait("tensor", "vector", ln_mark)
    pG = ctxG.enter_context(nc.psum_tensor("l1f_pG", [128, 8], F32))
    pCt = ctxG.enter_context(nc.psum_tensor("l1f_pCt", [8, 128], F32))
    pg_pend = [None]
    ct_pend = [None]
    cw_store = [None]
    for tt in range(NTT):
        t0 = tt * 128
        k.waitp("tensor", pg_pend[0])
        for kc in range(NF):
            k.op("tensor", lambda e, l=xnb[:, kc, t0:t0 + 128],
                 r=gwb[:, kc, :], pp=pG, first=(kc == 0), last=(kc == NF - 1):
                 e.matmul(pp[:, :], l, r, start=first, stop=last),
                 inc=(1 if kc == NF - 1 else 0))
        tmg = k.cnt["tensor"]
        k.wait("vector", "tensor", tmg)
        k.op("vector", lambda e: e.reduce_max(c1[:, 0:1], pG[:, :], axis=AX.X,
                                              negate=True), inc=1)
        vmx = k.cnt["vector"]
        k.wait("scalar", "vector", vmx)
        sme = k.op("scalar", lambda e:
                   e.activation(ex[:, :], pG[:, :], AF.Exp, bias=c1[:, 0:1]),
                   inc=1)
        pg_pend[0] = ("scalar", sme)
        k.wait("vector", "scalar", sme)
        k.op("vector", lambda e: e.reduce_max(c1[:, 1:2], ex[:, :], axis=AX.X),
             inc=1)
        k.op("vector", lambda e:
             e.tensor_single_scalar(mskd[:, :], ex[:, :], c1[:, 1:2],
                                    ALU.is_ge), inc=1)
        k.op("vector", lambda e:
             e.scalar_tensor_tensor(mskd[:, :], mskd[:, :], -1e30, ex[:, :],
                                    ALU.mult, ALU.add), inc=1)
        k.op("vector", lambda e: e.reduce_max(c1[:, 3:4], mskd[:, :],
                                              axis=AX.X), inc=1)
        k.op("vector", lambda e:
             e.tensor_tensor(c1[:, 4:5], c1[:, 1:2], c1[:, 3:4], ALU.add),
             inc=1)
        k.op("vector", lambda e:
             e.reciprocal(c1[:, 4:5], c1[:, 4:5]), inc=1)
        k.op("vector", lambda e:
             e.tensor_single_scalar(cwc[:, :], ex[:, :], c1[:, 3:4], ALU.is_ge),
             inc=1)
        k.op("vector", lambda e:
             e.tensor_mul(cwc[:, :], cwc[:, :], ex[:, :]), inc=1)
        vcw = k.op("vector", lambda e:
                   e.tensor_scalar_mul(cwc[:, :], cwc[:, :], c1[:, 4:5]), inc=1)
        k.wait("tensor", "vector", vcw)
        k.waitp("tensor", ct_pend[0])
        tmt = k.op("tensor", lambda e:
                   e.transpose(pCt[:, :], cwc[:, :], cm.identf[:, :]), inc=1)
        k.wait("vector", "tensor", tmt)
        k.waitp("vector", cw_store[0])
        vct = k.op("vector", lambda e:
                   e.tensor_copy(cwtr[:, :], pCt[:, :]), inc=1)
        ct_pend[0] = ("vector", vct)
        k.wait("sync", "vector", vct)
        cw_store[0] = ("sync", k.dma("sync", cw_d[:, t0:t0 + 128], cwtr[:, :]))

    marksG = {e: k.cnt[e] for e in ENGINES}
    ctxG.close()
    for e in ENGINES:
        for o in ("tensor", "vector", "scalar", "sync", "gpsimd"):
            if o != e:
                k.wait(e, o, marksG[o])

    ctxB = ExitStack()
    sbB = lambda n, s, dt=F32: ctxB.enter_context(
        nc.sbuf_tensor(n, s, dt, side="right"))
    w1t = sbB("w1t", [128, 2, NF, 128], BF16)
    b1c = sbB("b1c", [128, 32])
    cwrow = sbB("cwrow", [1, T], BF16)
    cwrepb = sbB("cwrepb", [128, T], BF16)
    hb = sbB("hb", [128, 6, TC], BF16)
    hcw = sbB("hcw", [128, 6, TC], BF16)
    pH = [ctxB.enter_context(nc.psum_tensor(f"l1f_pH{i}", [128, TC], F32))
          for i in range(6)]
    pRc = ctxB.enter_context(nc.psum_tensor("l1f_pRc", [128, TC], F32))
    w1_pend = [None, None]
    phA_call = [0]
    ph_pend = [None] * 6
    hcw_pend = [None] * 6
    rc_pend = [None]
    prev_e_te = [None]
    prev_e_sc = [None]
    for ei in range(8):
        k.wait("gpsimd", "tensor", prev_e_te[0]) if prev_e_te[0] else None
        cwm = k.dma("gpsimd", cwrow[:, :], cw_d[ei:ei + 1, :])
        k.wait("tensor", "gpsimd", cwm)
        k.wait("gpsimd", "scalar", prev_e_sc[0]) if prev_e_sc[0] else None
        bm = k.dma("gpsimd", b1c[:, :], W["l1f_b1"][ei, :, :])
        for tc in range(NT):
            k.waitp("tensor", rc_pend[0])
            trc = k.op("tensor", lambda e, r=cwrow[0:1, tc * TC:(tc + 1) * TC]:
                       e.matmul(pRc[:, :], cm.ones128[:, :], r, start=True,
                                stop=True), inc=1)
            k.wait("vector", "tensor", trc)
            vrc = k.op("vector", lambda e, o=cwrepb[:, tc * TC:(tc + 1) * TC]:
                       e.tensor_copy(o, pRc[:, :]), inc=1)
            rc_pend[0] = ("vector", vrc)
        cwrep_mark = k.cnt["vector"]
        k.wait("scalar", "gpsimd", bm)
        for hc in range(32):
            wb_i = hc % 2
            k.waitp("gpsimd", w1_pend[wb_i])
            wm = k.dma_batch("gpsimd", [
                (w1t[:, wb_i, kc, :],
                 W["l1f_w1"][ei, kc * 128:(kc + 1) * 128,
                             hc * 128:(hc + 1) * 128])
                for kc in range(NF)])
            k.wait("tensor", "gpsimd", wm)
            for tch in range(NT):
                pi = phA_call[0] % 6
                phA_call[0] += 1
                k.waitp("tensor", ph_pend[pi])
                for kc in range(NF):
                    k.op("tensor", lambda e, l=w1t[:, wb_i, kc, :],
                         r=xnb[:, kc, tch * TC:(tch + 1) * TC], pp=pH[pi],
                         first=(kc == 0), last=(kc == NF - 1):
                         e.matmul(pp[:, :], l, r, start=first, stop=last),
                         inc=(1 if kc == NF - 1 else 0))
                tmh = k.cnt["tensor"]
                k.wait("scalar", "tensor", tmh)
                k.waitp("scalar", hcw_pend[pi])
                smg = k.op("scalar", lambda e, pp=pH[pi], o=hb[:, pi, :],
                           b=b1c[:, hc:hc + 1]:
                           e.activation(o, pp[:, :], AF.Gelu, bias=b), inc=1)
                ph_pend[pi] = ("scalar", smg)
                k.wait("vector", "scalar", smg)
                if ei == 0 and hc == 0 and tch < 2:
                    k.wait("vector", "vector", cwrep_mark)
                vmc = k.op("vector", lambda e, o=hcw[:, pi, :], a=hb[:, pi, :],
                           b=cwrepb[:, tch * TC:(tch + 1) * TC]:
                           e.tensor_tensor(o, a, b, ALU.mult), inc=1)
                k.wait("sync", "vector", vmc)
                sm2 = k.dma("sync", h_d[ei, hc, :, tch * TC:(tch + 1) * TC],
                            hcw[:, pi, :])
                hcw_pend[pi] = ("sync", sm2)
            w1_pend[wb_i] = ("tensor", k.cnt["tensor"])
        prev_e_te[0] = k.cnt["tensor"]
        prev_e_sc[0] = k.cnt["scalar"]

    marksA = {e: k.cnt[e] for e in ENGINES}
    ctxB.close()
    ctxA.close()
    for e in ENGINES:
        for o in ("tensor", "vector", "scalar", "sync", "gpsimd"):
            if o != e:
                k.wait(e, o, marksA[o])

    ctxC = ExitStack()
    sbC = lambda n, s, dt=F32: ctxC.enter_context(
        nc.sbuf_tensor(n, s, dt, side="right"))
    w2e = sbC("w2e", [128, 32, D], BF16)
    b2b = sbC("b2b", [8, D], BF16)
    cwt8 = sbC("cwt8", [8, T], BF16)
    ht = sbC("ht", [128, 8, TC], BF16)
    xres = sbC("xres", [128, 2, TC])
    zacc = sbC("zacc", [128, NF, T], BF16)
    pZ = [ctxC.enter_context(nc.psum_tensor(f"l1f_pZ{i}", [128, TC], F32))
          for i in range(8)]
    bm = k.dma("gpsimd", b2b[:, :], W["l1f_b2"][:, :])
    bm = k.dma("gpsimd", cwt8[:, :], cw_d[:, :])
    zm = k.op("vector", lambda e: e.memset(zacc[:, :, :], 0.0), inc=1)
    ev_pend = [None] * NF
    ht_cons = [None, None]
    w2_used = [None] * 4
    w2_mark = [None] * 4
    for ei in range(8):
        if ei == 0:
            k.wait("vector", "vector", zm)

        def issue_w2(wb, ei=ei):
            k.waitp("gpsimd", w2_used[wb])
            w2_mark[wb] = k.dma_batch("gpsimd", [
                (w2e[:, wb * 8 + j, :],
                 W["l1f_w2"][ei, (wb * 8 + j) * 128:(wb * 8 + j + 1) * 128, :])
                for j in range(8)])

        for tc in range(NT):
            tl = tc * TC
            gm = {}

            def issue_group(hg, tl=tl, ei=ei, gm=gm):
                half = hg % 2
                k.waitp("gpsimd", ht_cons[half])
                gm[hg] = k.dma_batch("gpsimd", [
                    (ht[:, half * 4 + j, :],
                     h_d[ei, hg * 4 + j, :, tl:tl + TC])
                    for j in range(4)])

            if tc == 0:
                issue_w2(0)
            issue_group(0)
            issue_group(1)
            if tc == 0:
                issue_w2(1)
            for hg in range(8):
                half = hg % 2
                k.wait("tensor", "gpsimd", gm[hg])
                if tc == 0:
                    k.wait("tensor", "gpsimd", w2_mark[hg // 2])
                for j in range(4):
                    hc = hg * 4 + j
                    hbuf = half * 4 + j
                    for oc in range(NF):
                        if hc == 0:
                            k.waitp("tensor", ev_pend[oc])
                        k.op("tensor", lambda e,
                             l=w2e[:, hc, oc * 128:(oc + 1) * 128],
                             r=ht[:, hbuf, :], pp=pZ[oc], first=(hc == 0),
                             last=(hc == 31):
                             e.matmul(pp[:, :], l, r, start=first, stop=last),
                             inc=(1 if oc == NF - 1 else 0))
                ht_cons[half] = ("tensor", k.cnt["tensor"])
                if hg + 2 < 8:
                    issue_group(hg + 2)
                if tc == 0 and hg % 2 == 0 and hg // 2 + 2 < 4:
                    issue_w2(hg // 2 + 2)
            if tc == NT - 1:
                for wb in range(4):
                    w2_used[wb] = ("tensor", k.cnt["tensor"])
            tmz = k.cnt["tensor"]
            k.wait("vector", "tensor", tmz)
            for oc in range(NF):
                k.op("vector", lambda e, o=zacc[:, oc, tl:tl + TC], pp=pZ[oc]:
                     e.tensor_tensor(o, o, pp[:, :], ALU.add), inc=1)
                ev_pend[oc] = ("vector", k.cnt["vector"])

    marksB2 = {e: k.cnt[e] for e in ENGINES}
    for e in ENGINES:
        for o in ("tensor", "vector", "scalar", "sync", "gpsimd"):
            if o != e:
                k.wait(e, o, marksB2[o])

    pend_store = [None, None]
    call = [0]
    pB2 = pZ[0]
    pb2_pend = [None]
    xnt = sbC("xnt", [128, 2, TC], BF16)
    xnt_pend = [None, None]
    for oc in range(NF):
        for tc in range(NT):
            tl = tc * TC
            buf = call[0] % 2
            call[0] += 1
            k.waitp("tensor", pb2_pend[0])
            tb2 = k.op("tensor", lambda e, l=b2b[:, oc * 128:(oc + 1) * 128],
                       r=cwt8[:, tl:tl + TC], pp=pB2:
                       e.matmul(pp[:, :], l, r, start=True, stop=True), inc=1)
            k.wait("sync", "sync", pend_store[buf])
            dmx = k.dma("sync", xres[:, buf, :],
                        xt[oc * 128:(oc + 1) * 128, tl:tl + TC])
            k.waitp("gpsimd", xnt_pend[buf])
            dmn = k.dma("gpsimd", xnt[:, buf, :],
                        xn_d[oc * 128:(oc + 1) * 128, tl:tl + TC])
            k.wait("vector", "sync", dmx)
            k.wait("vector", "gpsimd", dmn)
            k.wait("vector", "tensor", tb2)
            k.op("vector", lambda e, o=xres[:, buf, :], pp=pB2:
                 e.tensor_tensor(o, o, pp[:, :], ALU.add), inc=1)
            k.op("vector", lambda e, o=xres[:, buf, :],
                 a=zacc[:, oc, tl:tl + TC]:
                 e.tensor_tensor(o, o, a, ALU.add), inc=1)
            vmz = k.op("vector", lambda e, o=xres[:, buf, :],
                       a=xnt[:, buf, :]:
                       e.tensor_tensor(o, o, a, ALU.add), inc=1)
            pb2_pend[0] = ("vector", vmz)
            xnt_pend[buf] = ("vector", vmz)
            k.wait("sync", "vector", vmz)
            pend_store[buf] = k.dma("sync", xo[oc * 128:(oc + 1) * 128,
                                               tl:tl + TC], xres[:, buf, :])

    k.wait("sync", "sync", k.cnt["sync"])
    ctxC.close()
    if not standalone:
        return _block_close(k, ctx)
    k.finish()
    ctx.close()
    in_names = ["xt"] + list(W.keys()) + cm.in_names
    return nc, in_names


def prep_weights_l1f(inp):
    return {
        "l1f_gamma": inp["L1_ffn_gamma"].reshape(1, D).astype(np.float32),
        "l1f_beta": inp["L1_ffn_beta"].reshape(1, D).astype(np.float32),
        "l1f_gw": np.ascontiguousarray(inp["L1_ffn_gate_w"]),
        "l1f_w1": np.ascontiguousarray(inp["L1_ffn_w1"]),
        "l1f_b1": np.ascontiguousarray(
            inp["L1_ffn_b1"].reshape(8, 32, 128).transpose(0, 2, 1)),
        "l1f_w2": np.ascontiguousarray(inp["L1_ffn_w2"]),
        "l1f_b2": np.ascontiguousarray(inp["L1_ffn_b2"]),
    }



L0F_HID = 2730


def _l0f_chunks():
    out = []
    c = 0
    while c < L0F_HID:
        w = min(128, L0F_HID - c)
        out.append((c, w))
        c += w
    return out


def build_L0f(env=None):
    nc, k, ctx, cm, xt, xo, standalone = _block_env(env)
    W = declare_w(nc, {
        "l0f_wqkv": (2, D, 1536), "l0f_qks": (2, 1, 128),
        "l0f_wo": (2, 512, D), "l0f_lng": (2, 1, D), "l0f_lnb": (2, 1, D),
        "l0f_w1": (2, D, 2 * L0F_HID), "l0f_b1": (2, 1, 2 * L0F_HID),
        "l0f_w2": (2, L0F_HID, D), "l0f_b2": (2, 1, D),
    })
    xi_d = nc.dram_tensor("l0f_xid", [D, T], F32, kind="Internal").ap()
    ot_d = nc.dram_tensor("l0f_otd", [512, T], BF16, kind="Internal").ap()
    mq_d = nc.dram_tensor("l0f_mqd", [64, 512], BF16, kind="Internal").ap()
    vg_d = nc.dram_tensor("l0f_vgd", [NTT * 128, 8 * 65], BF16,
                          kind="Internal").ap()
    hn_d = nc.dram_tensor("l0f_hnd", [D, T], BF16, kind="Internal").ap()
    sb = lambda n, s, dt=F32: ctx.enter_context(nc.sbuf_tensor(n, s, dt))

    xpA = sb("l0f_xpA", [128, NTT, 128], BF16)
    xpB = sb("l0f_xpB", [128, NTT, 128], BF16)
    k.dma("gpsimd", xpA[:, :, :],
          cm.aps["c_xpA"][:, :].rearrange("(a p) f -> p a f", p=128))
    tbm = k.dma("gpsimd", xpB[:, :, :],
                cm.aps["c_xpB"][:, :].rearrange("(a p) f -> p a f", p=128))

    ctx0 = ExitStack()
    sb0 = lambda n, s, dt=F32: ctx0.enter_context(
        nc.sbuf_tensor(n, s, dt, side="right"))
    xnb = sb0("xnb", [128, NF, T], BF16)
    ln_mark = emit_ln(k, cm, ctx, xt, xnb, "ln0f")
    spm = None
    for fc in range(NF):
        spm = k.dma("gpsimd", xi_d[fc * 128:(fc + 1) * 128, :], xnb[:, fc, :])
    m0 = {e: k.cnt[e] for e in ENGINES}
    ctx0.close()
    for e in ENGINES:
        for o in ENGINES:
            if o != e:
                k.wait(e, o, m0[o])

    for li in range(2):
        _l0f_layer(k, cm, ctx, W, xi_d, ot_d, mq_d, vg_d, hn_d, xpA, xpB, li)

    ctxF = ExitStack()
    sbF = lambda n, s, dt=F32: ctxF.enter_context(
        nc.sbuf_tensor(n, s, dt, side="right"))
    xres = sbF("f_xres", [128, 2, TC])
    xadd = sbF("f_xadd", [128, 2, TC])
    pend_store = [None, None]
    buf_pend = [None, None]
    call = [0]
    for fc in range(NF):
        for tc in range(NT):
            buf = call[0] % 2
            call[0] += 1
            k.wait("sync", "sync", pend_store[buf])
            k.waitp("gpsimd", buf_pend[buf])
            dm1 = k.dma("sync", xres[:, buf, :],
                        xt[fc * 128:(fc + 1) * 128, tc * TC:(tc + 1) * TC])
            dm2 = k.dma("gpsimd", xadd[:, buf, :],
                        xi_d[fc * 128:(fc + 1) * 128, tc * TC:(tc + 1) * TC])
            k.wait("vector", "sync", dm1)
            k.wait("vector", "gpsimd", dm2)
            vm = k.op("vector", lambda e, o=xres[:, buf, :], a=xadd[:, buf, :]:
                      e.tensor_tensor(o, o, a, ALU.add), inc=1)
            buf_pend[buf] = ("vector", vm)
            k.wait("sync", "vector", vm)
            pend_store[buf] = k.dma(
                "sync", xo[fc * 128:(fc + 1) * 128, tc * TC:(tc + 1) * TC],
                xres[:, buf, :])
    k.wait("sync", "sync", k.cnt["sync"])
    if not standalone:
        ctxF.close()
        return _block_close(k, ctx)
    k.finish()
    ctxF.close()
    ctx.close()
    in_names = ["xt"] + list(W.keys()) + cm.in_names
    return nc, in_names


def _l0f_layer(k, cm, ctx, W, xi_d, ot_d, mq_d, vg_d, hn_d, xpA, xpB, li):
    nc = k.nc
    tg = f"l0fL{li}"

    def barrier():
        m = {e: k.cnt[e] for e in ENGINES}
        for e in ENGINES:
            for o in ENGINES:
                if o != e:
                    k.wait(e, o, m[o])

    ctxL = ExitStack()
    qkT = ctxL.enter_context(
        nc.sbuf_tensor(tg + "qkT", [128, 8, T], BF16, side="right"))

    ctx1 = ExitStack()
    sb1 = lambda n, s, dt=F32: ctx1.enter_context(
        nc.sbuf_tensor(tg + n, s, dt, side="right"))
    xib = sb1("xib", [128, NF, T + 1], BF16)
    wqkvb = sb1("wqkvb", [128, NF, 1536], BF16)
    qksr = sb1("qksr", [1, 128], BF16)
    screp = sb1("screp", [128, 128], BF16)
    rno = sb1("rno", [128, 3, 16])
    ropef = sb1("ropef", [128, 2, 1024])
    qkb = sb1("qkb", [128, 1024], BF16)
    vstg = sb1("vstg", [128, 2, 8, 65], BF16)
    pQKV = [ctx1.enter_context(
        nc.psum_tensor(f"{tg}_pqkv{i}", [128, 1536], F32)) for i in range(2)]
    pT = [ctx1.enter_context(
        nc.psum_tensor(f"{tg}_pT{i}", [128, 128], BF16)) for i in range(2)]
    pRp = pQKV[0][:, 0:128]

    wm2 = k.dma_batch("gpsimd", [
        p for kc in range(NF) for p in (
            (xib[:, kc, 1:T + 1], xi_d[kc * 128:(kc + 1) * 128, :]),
            (wqkvb[:, kc, :], W["l0f_wqkv"][li, kc * 128:(kc + 1) * 128, :]))
    ] + [(qksr[:, :], W["l0f_qks"][li, :, :])])
    k.wait("vector", "gpsimd", wm2)
    k.op("vector", lambda e: e.memset(xib[:, :, 0:1], 0.0), inc=1)
    zm = k.op("vector", lambda e: e.memset(vstg[:, :, :, 64:65], 1.0), inc=1)
    k.wait("tensor", "gpsimd", wm2)
    trp = k.op("tensor", lambda e:
               e.matmul(pRp, cm.ones128[:, :], qksr[:, :], start=True,
                        stop=True), inc=1)
    k.wait("vector", "tensor", trp)
    vsc = k.op("vector", lambda e: e.tensor_copy(screp[:, :], pRp),
               inc=1)
    k.wait("tensor", "vector", zm)
    k.wait("tensor", "vector", vsc)

    def shifted(kc, lo, width):
        off = 1 if kc < 4 else 0
        return xib[:, kc, lo + off:lo + off + width]

    pq_pend = [None, None]
    pt_pend = [None, None]
    vst_pend = [None, None]
    qm_prev = None
    for tt in range(NTT):
        t0 = tt * 128
        pi = tt % 2
        vb = tt % 2
        k.waitp("tensor", pq_pend[pi])
        for third in range(3):
            for kc in range(NF):
                k.op("tensor", lambda e, l=shifted(kc, t0, 128),
                     r=wqkvb[:, kc, third * 512:(third + 1) * 512],
                     pp=pQKV[pi][:, third * 512:(third + 1) * 512],
                     first=(kc == 0), last=(kc == NF - 1):
                     e.matmul(pp, l, r, start=first, stop=last),
                     inc=(1 if kc == NF - 1 else 0))
        tmq = k.cnt["tensor"]
        k.wait("scalar", "tensor", tmq)
        k.waitp("scalar", qm_prev)
        sm1 = k.op("scalar", lambda e, pp=pQKV[pi]:
                   e.activation(ropef[:, 0, :], pp[:, 0:1024], AF.Square),
                   inc=1)
        k.wait("vector", "scalar", sm1)
        k.op("vector", lambda e:
             e.reduce_sum(rno[:, 0, :], ropef[:, 0, :].rearrange(
                 "p (g d) -> p g d", g=16), axis=AX.X), inc=1)
        vr = k.cnt["vector"]
        k.wait("scalar", "vector", vr)
        sm2 = k.op("scalar", lambda e:
                   e.activation(rno[:, 1, :], rno[:, 0, :], AF.Sqrt), inc=1)
        k.wait("vector", "scalar", sm2)
        k.op("vector", lambda e:
             e.tensor_scalar_max(rno[:, 1, :], rno[:, 1, :], 1e-12), inc=1)
        k.op("vector", lambda e:
             e.reciprocal(rno[:, 2, :], rno[:, 1, :]), inc=1)
        rno_b = bass.AP(rno.ap().tensor, rno[:, 2, :].offset,
                        [[48, 128], [1, 16], [0, 64]])
        k.op("vector", lambda e, pp=pQKV[pi], b=rno_b:
             e.tensor_tensor(ropef[:, 0, :].rearrange(
                 "p (g d) -> p g d", g=16), pp[:, 0:1024], b, ALU.mult),
             inc=1)
        k.waitp("vector", vst_pend[vb])
        vmv = k.op("vector", lambda e, pp=pQKV[pi], o=vstg[:, vb, :, 0:64]:
                   e.tensor_copy(o, pp[:, 1024:1536].rearrange(
                       "p (h v) -> p h v", h=8)), inc=1)
        pq_pend[pi] = ("vector", vmv)
        k.wait("sync", "vector", vmv)
        vst_pend[vb] = ("sync",
                        k.dma("sync", vg_d[t0:t0 + 128, :],
                              vstg[:, vb, :, :].rearrange("p h v -> p (h v)")))
        for qk in range(2):
            scr_b = bass.AP(screp.ap().tensor,
                            screp.ap().offset + qk * 64,
                            [[128, 128], [0, 8], [1, 64]])
            k.op("vector", lambda e, b=scr_b,
                 o=ropef[:, 0, qk * 512:(qk + 1) * 512].rearrange(
                     "p (h c) -> p h c", h=8):
                 e.tensor_tensor(o, o, b, ALU.mult), inc=1)
        for qk in range(2):
            B_b = bass.AP(xpB.ap().tensor,
                          xpB[:, tt, :].offset + qk * 64,
                          [[NTT * 128, 128], [0, 8], [32, 2], [1, 32]])
            vsw = bass.AP(ropef.ap().tensor,
                          ropef[:, 0, :].offset + qk * 512 + 32,
                          [[2048, 128], [64, 8], [-32, 2], [1, 32]])
            k.op("vector", lambda e, b=B_b, a=vsw,
                 o=ropef[:, 1, qk * 512:(qk + 1) * 512].rearrange(
                     "p (h a c) -> p h a c", h=8, a=2):
                 e.tensor_tensor(o, a, b, ALU.mult), inc=1)
        for qk in range(2):
            A_b = bass.AP(xpA.ap().tensor,
                          xpA[:, tt, :].offset + qk * 64,
                          [[NTT * 128, 128], [0, 8], [1, 64]])
            k.op("vector", lambda e, b=A_b,
                 o=ropef[:, 0, qk * 512:(qk + 1) * 512].rearrange(
                     "p (h c) -> p h c", h=8):
                 e.tensor_tensor(o, o, b, ALU.mult), inc=1)
        qm = k.op("vector", lambda e:
                  e.tensor_tensor(qkb[:, :], ropef[:, 0, :], ropef[:, 1, :],
                                  ALU.add), inc=1)
        qm_prev = ("vector", qm)
        k.wait("tensor", "vector", qm)
        for g in range(8):
            gi = g % 2
            k.waitp("tensor", pt_pend[gi])
            ttm = k.op("tensor", lambda e, a=qkb[:, g * 128:(g + 1) * 128],
                       pp=pT[gi][:, :]:
                       e.transpose(pp, a, cm.ident[:, :]), inc=1)
            k.wait("scalar", "tensor", ttm)
            vmt = k.op("scalar", lambda e, o=qkT[:, g, t0:t0 + 128],
                       pp=pT[gi][:, :]:
                       e.activation(o, pp, AF.Copy), inc=1)
            pt_pend[gi] = ("scalar", vmt)
    barrier()
    ctx1.close()
    barrier()

    ctx2a = ExitStack()
    mqc = ctx2a.enter_context(
        nc.sbuf_tensor(tg + "mqc", [128, 256], F32, side="right"))
    mqtr = ctx2a.enter_context(
        nc.sbuf_tensor(tg + "mqtr", [128, 128], BF16, side="right"))
    pSa = [ctx2a.enter_context(
        nc.psum_tensor(f"{tg}_pSa{i}", [128, 640], F32)) for i in range(2)]
    pMq = ctx2a.enter_context(nc.psum_tensor(tg + "_pMq", [128, 128], F32))

    def hgeo(h):
        return 64 * (h % 2), h // 2, 4 + h // 2

    ps_pend = [None, None]
    cnt1 = [0]
    for h in range(8):
        rb, gq, gk = hgeo(h)
        for w in range(8):
            for qt in range(4):
                qa = 4 * w + qt
                lo = max(0, qa - 4)
                nkt = qa - lo + 1
                ci = cnt1[0] % 2
                cnt1[0] += 1
                k.waitp("tensor", ps_pend[ci])
                if nkt == 5:
                    k.op("tensor", lambda e,
                         l=qkT[rb:rb + 64, gq, qa * 128:qa * 128 + 128],
                         r=qkT[rb:rb + 64, gk, (lo + 4) * 128:(lo + 5) * 128],
                         pp=pSa[ci][:, 512:640]:
                         e.matmul(pp, l, r, start=True, stop=False), inc=1)
                    k.op("tensor", lambda e, pp=pSa[ci][:, 512:640]:
                         e.matmul(pp, cm.ident[:, :], cm.maskcgt[:, :],
                                  start=False, stop=True), inc=1)
                    k.op("tensor", lambda e,
                         l=qkT[rb:rb + 64, gq, qa * 128:qa * 128 + 128],
                         r=qkT[rb:rb + 64, gk, lo * 128:(lo + 4) * 128],
                         pp=pSa[ci][:, 0:512]:
                         e.matmul(pp, l, r, start=True, stop=False), inc=1)
                    k.op("tensor", lambda e, pp=pSa[ci][:, 0:128]:
                         e.matmul(pp, cm.ident[:, :], cm.maskkgt[:, :],
                                  start=False, stop=True), inc=1)
                else:
                    k.op("tensor", lambda e,
                         l=qkT[rb:rb + 64, gq, qa * 128:qa * 128 + 128],
                         r=qkT[rb:rb + 64, gk, lo * 128:(lo + nkt) * 128],
                         pp=pSa[ci][:, 0:nkt * 128]:
                         e.matmul(pp, l, r, start=True, stop=False), inc=1)
                    k.op("tensor", lambda e,
                         pp=pSa[ci][:, (nkt - 1) * 128:nkt * 128]:
                         e.matmul(pp, cm.ident[:, :], cm.maskcgt[:, :],
                                  start=False, stop=True), inc=1)
                tmm = k.cnt["tensor"]
                k.wait("vector", "tensor", tmm)
                c = h * 32 + w * 4 + qt
                vv = k.op("vector", lambda e, pp=pSa[ci][:, 0:nkt * 128],
                          o=mqc[:, c:c + 1]:
                          e.reduce_max(o, pp, axis=AX.X, negate=True), inc=1)
                ps_pend[ci] = ("vector", vv)
    k.wait("tensor", "vector", k.cnt["vector"])
    mqv = mq_d[:, :].rearrange("a (b c) -> (a b) c", c=128)
    smt = None
    dm0 = None
    for half in range(2):
        if half:
            k.wait("tensor", "scalar", smt)
        ttm = k.op("tensor", lambda e, a=mqc[:, half * 128:(half + 1) * 128]:
                   e.transpose(pMq[:, :], a, cm.identf[:, :]), inc=1)
        k.wait("scalar", "tensor", ttm)
        if half:
            k.wait("scalar", "sync", dm0)
        smt = k.op("scalar", lambda e:
                   e.activation(mqtr[:, :], pMq[:, :], AF.Copy), inc=1)
        k.wait("sync", "scalar", smt)
        dm0 = k.dma("sync", mqv[half * 128:(half + 1) * 128, :], mqtr[:, :])
    barrier()
    ctx2a.close()
    barrier()

    ctx2b = ExitStack()
    sb2 = lambda n, s, dt=F32: ctx2b.enter_context(
        nc.sbuf_tensor(tg + n, s, dt, side="right"))
    vstg2 = sb2("vstg2", [128, 2, 8, 65], BF16)
    mqrow = sb2("mqrow", [1, 2, TC], BF16)
    util = sb2("util", [128, 2, TC], BF16)
    rrow = sb2("rrow", [1, TC])
    rrowb = sb2("rrowb", [1, TC], BF16)
    repb = sb2("repb", [64, 2, TC], BF16)
    oT = sb2("oT", [64, 2, TC], BF16)
    pS2 = [ctx2b.enter_context(
        nc.psum_tensor(f"{tg}_pS2{i}", [128, TC], F32)) for i in range(2)]
    pN = [ctx2b.enter_context(
        nc.psum_tensor(f"{tg}_pN{i}", [65, TC], F32)) for i in range(2)]
    pRep2 = ctx2b.enter_context(nc.psum_tensor(tg + "_pRep2", [64, TC], F32))

    vstg_pend2 = [None, None]
    ps2_pend = [None, None]
    pn_pend = [None, None]
    rep_pend = [None]
    o_pend = [None, None]
    for h in range(8):
        rb, gq, gk = hgeo(h)
        for w in range(8):
            idx = h * 8 + w
            vb = idx % 2
            ni = idx % 2
            lo_k = max(0, 4 * w - 4)
            kas = list(range(lo_k, 4 * w + 4))
            k.waitp("gpsimd", vstg_pend2[vb])
            mrm = k.dma_batch("gpsimd", [
                (vstg2[:, vb, j, :],
                 vg_d[ka * 128:(ka + 1) * 128, h * 65:(h + 1) * 65])
                for j, ka in enumerate(kas)
            ] + [(mqrow[0:1, vb, :], mq_d[idx:idx + 1, :])])
            k.wait("tensor", "gpsimd", mrm)
            k.waitp("tensor", pn_pend[ni])
            s_mark = {}

            def emit_s2(j, vb=vb, rb=rb, gq=gq, gk=gk, w=w):
                ci = j % 2
                ka = kas[j]
                k.waitp("tensor", ps2_pend[ci])
                k.op("tensor", lambda e,
                     l=qkT[rb:rb + 64, gk, ka * 128:ka * 128 + 128],
                     r=qkT[rb:rb + 64, gq, w * 512:(w + 1) * 512],
                     pp=pS2[ci][:, :]:
                     e.matmul(pp, l, r, start=True, stop=False), inc=0)
                for qt in range(4):
                    qa = 4 * w + qt
                    blk = pS2[ci][:, qt * 128:(qt + 1) * 128]
                    if ka == qa:
                        k.op("tensor", lambda e, pp=blk:
                             e.matmul(pp, cm.ident[:, :], cm.maskkgt[:, :],
                                      start=False, stop=False), inc=0)
                    elif ka == qa - 4:
                        k.op("tensor", lambda e, pp=blk:
                             e.matmul(pp, cm.ident[:, :], cm.maskcgt[:, :],
                                      start=False, stop=False), inc=0)
                    elif ka < qa - 4 or ka > qa:
                        k.op("tensor", lambda e, pp=blk:
                             e.matmul(pp, cm.ones128[:, :], cm.negrow[:, :],
                                      start=False, stop=False), inc=0)
                k.op("tensor", lambda e, pp=pS2[ci][:, :],
                     r=mqrow[0:1, vb, :]:
                     e.matmul(pp, cm.ones128[:, :], r, start=False,
                              stop=True), inc=1)
                s_mark[j] = k.cnt["tensor"]

            emit_s2(0)
            for j, ka in enumerate(kas):
                ci = j % 2
                if j + 1 < len(kas):
                    emit_s2(j + 1)
                k.wait("scalar", "tensor", s_mark[j])
                um = k.op("scalar", lambda e, pp=pS2[ci][:, :],
                          o=util[:, ci, :]:
                          e.activation(o, pp, AF.Exp), inc=1)
                ps2_pend[ci] = ("scalar", um)
                k.wait("tensor", "scalar", um)
                k.op("tensor", lambda e, l=vstg2[:, vb, j, :],
                     r=util[:, ci, :], pp=pN[ni][:, :], first=(j == 0),
                     last=(j == len(kas) - 1):
                     e.matmul(pp, l, r, start=first, stop=last), inc=1)
            tav = k.cnt["tensor"]
            vstg_pend2[vb] = ("tensor", tav)
            k.wait("vector", "tensor", tav)
            k.waitp("vector", o_pend[ni])
            k.op("vector", lambda e, pp=pN[ni]:
                 e.reciprocal(rrow[:, :], pp[64:65, :]), inc=1)
            vm1 = k.op("vector", lambda e:
                       e.tensor_copy(rrowb[:, :], rrow[:, :]), inc=1)
            k.wait("tensor", "vector", vm1)
            k.waitp("tensor", rep_pend[0])
            tmr = k.op("tensor", lambda e:
                       e.matmul(pRep2[:, :], cm.ones128[0:1, 0:64],
                                rrowb[:, :], start=True, stop=True), inc=1)
            k.wait("scalar", "tensor", tmr)
            smr = k.op("scalar", lambda e, o=repb[:, ni, :]:
                       e.activation(o, pRep2[:, :], AF.Copy), inc=1)
            rep_pend[0] = ("scalar", smr)
            k.wait("vector", "scalar", smr)
            vm2 = k.op("vector", lambda e, pp=pN[ni], o=oT[:, ni, :],
                       rr=repb[:, ni, :]:
                       e.tensor_tensor(o, pp[0:64, :], rr, ALU.mult), inc=1)
            pn_pend[ni] = ("vector", vm2)
            k.wait("sync", "vector", vm2)
            o_pend[ni] = ("sync",
                          k.dma("sync", ot_d[64 * h:64 * h + 64,
                                             w * 512:(w + 1) * 512],
                                oT[:, ni, :]))
    barrier()
    ctx2b.close()
    ctxL.close()
    barrier()

    ctx3 = ExitStack()
    emit_out_proj_dram(k, cm, ctx3, W["l0f_wo"][li], None, ot_d, xi_d, xi_d,
                       tg + "op", nk=4)
    barrier()
    ctx3.close()
    barrier()

    _l0f_ffn(k, cm, W, xi_d, hn_d, li, tg)


def _l0f_ffn(k, cm, W, xi_d, hn_d, li, tg):
    nc = k.nc

    def barrier():
        m = {e: k.cnt[e] for e in ENGINES}
        for e in ENGINES:
            for o in ENGINES:
                if o != e:
                    k.wait(e, o, m[o])

    ctxS = ExitStack()
    sbS = lambda n, s, dt=F32: ctxS.enter_context(
        nc.sbuf_tensor(tg + "f" + n, s, dt, side="right"))
    xib = sbS("xib", [128, NF, T + 1], BF16)
    rho_b = sbS("rho", [1, T], BF16)
    nmurho_b = sbS("nmr", [1, T], BF16)
    lgb = sbS("lg", [1, D], BF16)
    lbb = sbS("lb", [1, D], BF16)
    sq = sbS("sq", [128, 2, TC], BF16)
    murow = sbS("mu", [1, T])
    qrow = sbS("q", [1, T])
    tmp = sbS("tmp", [1, T])
    hnb = sbS("hnb", [128, 2, NF, TC], BF16)
    pS1 = ctxS.enter_context(nc.psum_tensor(tg + "f_pS", [1, TC], F32))
    pQ1 = ctxS.enter_context(nc.psum_tensor(tg + "f_pQ", [1, TC], F32))
    pR = ctxS.enter_context(nc.psum_tensor(tg + "f_pR", [128, 2, TC], F32))
    pM = ctxS.enter_context(nc.psum_tensor(tg + "f_pM", [128, 2, TC], F32))

    wm2 = k.dma_batch("gpsimd", [
        (xib[:, kc, 1:T + 1], xi_d[kc * 128:(kc + 1) * 128, :])
        for kc in range(NF)
    ] + [
        (lgb[:, :], W["l0f_lng"][li, :, :]),
        (lbb[:, :], W["l0f_lnb"][li, :, :]),
    ])
    wm = wm2
    k.wait("vector", "gpsimd", wm)
    zm = k.op("vector", lambda e: e.memset(xib[:, :, 0:1], 0.0), inc=1)
    k.wait("tensor", "vector", zm)
    k.wait("scalar", "vector", zm)
    k.wait("tensor", "gpsimd", wm2)

    def shifted(kc, lo, width):
        off = 1 if kc < 4 else 0
        return xib[:, kc, lo + off:lo + off + width]

    sq_te = {}
    for tc in range(NT):
        for fc in range(NF):
            gi = tc * NF + fc
            k.wait("scalar", "tensor", sq_te.get(gi - 2))
            m = k.op("scalar", lambda e, a=shifted(fc, tc * TC, TC),
                     o=sq[:, fc % 2, :]:
                     e.activation(o, a, AF.Square), inc=1)
            k.op("tensor", lambda e, a=shifted(fc, tc * TC, TC), p=pS1,
                 first=(fc == 0), last=(fc == NF - 1):
                 e.matmul(p[:, :], cm.invD[:], a, start=first, stop=last),
                 inc=(1 if fc == NF - 1 else 0))
            k.wait("tensor", "scalar", m)
            sq_te[gi] = k.op("tensor", lambda e, a=sq[:, fc % 2, :], p=pQ1,
                             first=(fc == 0), last=(fc == NF - 1):
                             e.matmul(p[:, :], cm.invD[:], a, start=first,
                                      stop=last), inc=1)
        te_m = k.cnt["tensor"]
        k.wait("vector", "tensor", te_m)
        k.op("vector", lambda e, p=pS1, o=murow[:, tc * TC:(tc + 1) * TC]:
             e.tensor_copy(o, p[:, :]), inc=1)
        em = k.op("vector", lambda e, p=pQ1,
                  o=qrow[:, tc * TC:(tc + 1) * TC]:
                  e.tensor_copy(o, p[:, :]), inc=1)
        if tc < NT - 1:
            k.wait("tensor", "vector", em)
    k.op("vector", lambda e: e.tensor_mul(tmp[:, :], murow[:, :],
                                          murow[:, :]), inc=1)
    vm = k.op("vector", lambda e: e.tensor_sub(tmp[:, :], qrow[:, :],
                                               tmp[:, :]), inc=1)
    k.wait("scalar", "vector", vm)
    k.op("scalar", lambda e: e.activation(tmp[:, :], tmp[:, :], AF.Ln,
                                          bias=cm.eps[0:1, 0:1]), inc=1)
    sm = k.op("scalar", lambda e: e.activation(tmp[:, :], tmp[:, :], AF.Exp,
                                               scale=-0.5), inc=1)
    k.wait("vector", "scalar", sm)
    k.op("vector", lambda e: e.tensor_copy(rho_b[:, :], tmp[:, :]), inc=1)
    k.op("vector", lambda e: e.tensor_mul(tmp[:, :], murow[:, :], tmp[:, :]),
         inc=1)
    k.op("vector", lambda e: e.tensor_scalar_mul(tmp[:, :], tmp[:, :], -1.0),
         inc=1)
    rm = k.op("vector", lambda e: e.tensor_copy(nmurho_b[:, :], tmp[:, :]),
              inc=1)
    k.wait("tensor", "vector", rm)

    pend_pm = [None, None]
    pend_hb = [None, None]
    for tc in range(NT):
        tl = tc * TC
        hb = tc % 2
        k.waitp("vector", pend_hb[hb])
        hv = None
        for kc in range(NF):
            pb = kc % 2
            k.waitp("tensor", pend_pm[pb])
            k.op("tensor", lambda e, p=pR[:, pb, :],
                 g=lgb[0:1, kc * 128:(kc + 1) * 128],
                 r=rho_b[:, tl:tl + TC]:
                 e.matmul(p, g, r, start=True, stop=True), inc=1)
            k.op("tensor", lambda e, p=pM[:, pb, :],
                 g=lgb[0:1, kc * 128:(kc + 1) * 128],
                 r=nmurho_b[:, tl:tl + TC]:
                 e.matmul(p, g, r, start=True, stop=False), inc=0)
            tm = k.op("tensor", lambda e, p=pM[:, pb, :],
                      b=lbb[0:1, kc * 128:(kc + 1) * 128]:
                      e.matmul(p, b, cm.onesrow[:, :], start=False,
                               stop=True), inc=1)
            k.wait("vector", "tensor", tm)
            k.op("vector", lambda e, a=shifted(kc, tl, TC), p=pR[:, pb, :],
                 o=hnb[:, hb, kc, :]:
                 e.tensor_tensor(o, a, p, ALU.mult), inc=1)
            hv = k.op("vector", lambda e, p=pM[:, pb, :], o=hnb[:, hb, kc, :]:
                      e.tensor_tensor(o, o, p, ALU.add), inc=1)
            pend_pm[pb] = ("vector", hv)
        k.wait("sync", "vector", hv)
        dm_h = k.dma_batch("sync", [
            (hn_d[kc * 128:(kc + 1) * 128, tl:tl + TC], hnb[:, hb, kc, :])
            for kc in range(NF)])
        pend_hb[hb] = ("sync", dm_h)
    barrier()
    ctxS.close()
    barrier()

    chunks = _l0f_chunks()
    ctxF = ExitStack()
    sbF = lambda n, s, dt=F32: ctxF.enter_context(
        nc.sbuf_tensor(tg + "fm" + n, s, dt, side="right"))
    w1b = sbF("w1b", [128, NF, 2 * L0F_HID], BF16)
    b1b = sbF("b1b", [1, 2 * L0F_HID], BF16)
    b2b = sbF("b2b", [1, D], BF16)
    hnt = sbF("hnt", [128, 2, NF, TC], BF16)
    hh = sbF("hh", [128, len(chunks), TC], BF16)
    gg = sbF("gg", [128, 2, TC], BF16)
    w2b = sbF("w2b", [128, len(chunks), D], BF16)
    xres = sbF("xres", [128, 2, TC])
    pA = [ctxF.enter_context(
        nc.psum_tensor(f"{tg}f_pA{i}", [128, TC], F32)) for i in range(2)]
    pG = [ctxF.enter_context(
        nc.psum_tensor(f"{tg}f_pG{i}", [128, TC], F32)) for i in range(2)]
    pZ = [ctxF.enter_context(
        nc.psum_tensor(f"{tg}f_pZ{i}", [128, TC], F32)) for i in range(4)]
    wm = k.dma_batch("gpsimd", [
        (w1b[:, kc, :], W["l0f_w1"][li, kc * 128:(kc + 1) * 128, :])
        for kc in range(NF)
    ] + [
        (w2b[0:wdt, i, :], W["l0f_w2"][li, c0:c0 + wdt, :])
        for i, (c0, wdt) in enumerate(chunks)
    ] + [
        (b1b[:, :], W["l0f_b1"][li, :, :]),
        (b2b[:, :], W["l0f_b2"][li, :, :]),
    ])
    k.wait("tensor", "gpsimd", wm)

    hnt_pend = [None, None]
    ag_pend = [None, None]
    hh_pend = [None]
    z_pend = [None]
    store_pend = [None, None]
    for tch in range(NT):
        tl = tch * TC
        hb = tch % 2
        k.waitp("gpsimd", hnt_pend[hb])
        dm = k.dma_batch("gpsimd", [
            (hnt[:, hb, kc, :], hn_d[kc * 128:(kc + 1) * 128, tl:tl + TC])
            for kc in range(NF)])
        k.wait("tensor", "gpsimd", dm)
        for i, (c0, wdt) in enumerate(chunks):
            bi = i % 2
            k.waitp("tensor", ag_pend[bi])
            for kc in range(NF):
                k.op("tensor", lambda e, l=w1b[:, kc, c0:c0 + wdt],
                     r=hnt[:, hb, kc, :], pp=pA[bi][0:wdt, :],
                     first=(kc == 0):
                     e.matmul(pp, l, r, start=first, stop=False), inc=0)
            tA = k.op("tensor", lambda e, pp=pA[bi][0:wdt, :],
                      b=b1b[0:1, c0:c0 + wdt]:
                      e.matmul(pp, b, cm.onesrow[:, :], start=False,
                               stop=True), inc=1)
            for kc in range(NF):
                k.op("tensor", lambda e,
                     l=w1b[:, kc, L0F_HID + c0:L0F_HID + c0 + wdt],
                     r=hnt[:, hb, kc, :], pp=pG[bi][0:wdt, :],
                     first=(kc == 0):
                     e.matmul(pp, l, r, start=first, stop=False), inc=0)
            tG = k.op("tensor", lambda e, pp=pG[bi][0:wdt, :],
                      b=b1b[0:1, L0F_HID + c0:L0F_HID + c0 + wdt]:
                      e.matmul(pp, b, cm.onesrow[:, :], start=False,
                               stop=True), inc=1)
            k.wait("scalar", "tensor", tG)
            k.waitp("scalar", ag_pend[bi])
            sg = k.op("scalar", lambda e, pp=pG[bi][0:wdt, :],
                      o=gg[0:wdt, bi, :]:
                      e.activation(o, pp, AF.Gelu), inc=1)
            k.wait("vector", "scalar", sg)
            k.wait("vector", "tensor", tA)
            if i < 2:
                k.waitp("vector", hh_pend[0])
            hv = k.op("vector", lambda e, pp=pA[bi][0:wdt, :],
                      o=hh[0:wdt, i, :], a=gg[0:wdt, bi, :]:
                      e.tensor_tensor(o, pp, a, ALU.mult), inc=1)
            ag_pend[bi] = ("vector", hv)
        hh_mark = k.cnt["vector"]
        k.wait("tensor", "vector", hh_mark)
        hnt_pend[hb] = ("tensor", k.cnt["tensor"])
        for och in range(2):
            for i, (c0, wdt) in enumerate(chunks):
                if i == 0:
                    k.waitp("tensor", z_pend[0])
                for o4 in range(4):
                    oc = och * 4 + o4
                    k.op("tensor", lambda e,
                         l=w2b[0:wdt, i, oc * 128:(oc + 1) * 128],
                         r=hh[0:wdt, i, :], pp=pZ[o4][:, :],
                         first=(i == 0):
                         e.matmul(pp, l, r, start=first, stop=False),
                         inc=(1 if o4 == 3 else 0))
            for o4 in range(4):
                oc = och * 4 + o4
                tb = k.op("tensor", lambda e, pp=pZ[o4][:, :],
                          b=b2b[0:1, oc * 128:(oc + 1) * 128]:
                          e.matmul(pp, b, cm.onesrow[:, :], start=False,
                                   stop=True), inc=1)
                sbf = o4 % 2
                k.waitp("sync", store_pend[sbf])
                dmx = k.dma("sync", xres[:, sbf, :],
                            xi_d[oc * 128:(oc + 1) * 128, tl:tl + TC])
                k.wait("vector", "sync", dmx)
                k.wait("vector", "tensor", tb)
                vz = k.op("vector", lambda e, o=xres[:, sbf, :],
                          pp=pZ[o4][:, :]:
                          e.tensor_tensor(o, o, pp, ALU.add), inc=1)
                k.wait("sync", "vector", vz)
                store_pend[sbf] = (
                    "sync",
                    k.dma("sync", xi_d[oc * 128:(oc + 1) * 128, tl:tl + TC],
                          xres[:, sbf, :]))
                z_pend[0] = ("vector", vz)
        hh_pend[0] = ("tensor", k.cnt["tensor"])
    barrier()
    ctxF.close()
    barrier()



def prep_weights_l0f(inp):
    qks = np.concatenate([inp["L0_ffn_q_scale"], inp["L0_ffn_k_scale"]],
                         axis=-1)
    return {
        "l0f_wqkv": np.ascontiguousarray(inp["L0_ffn_wqkv"]),
        "l0f_qks": np.ascontiguousarray(qks).reshape(2, 1, 128)
        .astype(np.float32),
        "l0f_wo": np.ascontiguousarray(inp["L0_ffn_wo"]),
        "l0f_lng": np.ascontiguousarray(inp["L0_ffn_ln_g"]).reshape(2, 1, D),
        "l0f_lnb": np.ascontiguousarray(inp["L0_ffn_ln_b"]).reshape(2, 1, D),
        "l0f_w1": np.ascontiguousarray(inp["L0_ffn_w1"]),
        "l0f_b1": np.ascontiguousarray(inp["L0_ffn_b1"])
        .reshape(2, 1, 2 * L0F_HID),
        "l0f_w2": np.ascontiguousarray(inp["L0_ffn_w2"]),
        "l0f_b2": np.ascontiguousarray(inp["L0_ffn_b2"]).reshape(2, 1, D),
    }


def prep_weights_l2f(inp):
    Fr, Fi = fourier_consts()
    return {
        "l2f_wr": np.ascontiguousarray(inp["L2_ffn_w_freq_real"]),
        "l2f_wi": np.ascontiguousarray(inp["L2_ffn_w_freq_imag"]),
        "l2f_bias": np.ascontiguousarray(inp["L2_ffn_bias"]).reshape(1, D),
        "c_Fr": Fr, "c_Fi": Fi,
    }


def build_full():
    ncr = bass.Bass("TRN2", target_bir_lowering=False, debug=False)
    nc = NCW(ncr)
    k = K(nc)
    ctx = ExitStack()
    cm = Common(nc, k, ctx)
    xt = nc.dram_tensor("xt", [D, T], F32, kind="ExternalInput").ap()
    xo = nc.dram_tensor("xo", [D, T], F32, kind="ExternalOutput").ap()
    hs = [nc.dram_tensor(f"hx{i}", [D, T], F32, kind="Internal").ap()
          for i in range(7)]
    chain = [build_L0a, build_L0f, build_L1a, build_L1f, build_L2a,
             build_L2f, build_L3a, build_L3f]
    xin = xt
    for i, b in enumerate(chain):
        xout = hs[i] if i < 7 else xo
        b(env=(nc, k, ctx, cm, xin, xout))
        xin = xout
    k.finish()
    ctx.close()
    return ncr, None


BLOCK_CHAIN = [
    ("l0a", "build_L0a", "prep_weights_l0a"),
    ("l0f", "build_L0f", "prep_weights_l0f"),
    ("l1a", "build_L1a", "prep_weights_l1a"),
    ("l1f", "build_L1f", "prep_weights_l1f"),
    ("l2a", "build_L2a", "prep_weights_l2a"),
    ("l2f", "build_L2f", "prep_weights_l2f"),
    ("l3a", "build_L3a", "prep_weights_l3a"),
    ("l3f", "build_L3f", "prep_weights_l3f"),
]


def kernel(**inputs):
    x = np.asarray(inputs["L0_attn_x"], np.float32)
    g = globals()
    wmap = dict(get_consts())
    for name, build, prep in BLOCK_CHAIN:
        wmap.update(g[prep](inputs))
    nc, _ = build_full()
    maps = []
    for c in range(8):
        m = {"xt": np.ascontiguousarray(x[c].T)}
        m.update(wmap)
        maps.append(m)
    res = run_bass_kernel_spmd(nc, maps, core_ids=list(range(8)))
    out = np.stack([np.asarray(res.results[c]["xo"]).T for c in range(8)],
                   axis=0)
    return np.ascontiguousarray(out, dtype=np.float32)


# version 12 (final)
# speedup vs baseline: 1.0130x; 1.0130x over previous
import numpy as np
from contextlib import ExitStack

import concourse.bass as bass
import concourse.mybir as mybir
from concourse.bass_utils import run_bass_kernel_spmd

F32 = mybir.dt.float32
BF16 = mybir.dt.bfloat16
I32 = mybir.dt.int32
U32 = mybir.dt.uint32
AF = mybir.ActivationFunctionType
ALU = mybir.AluOpType
AX = mybir.AxisListType

T = 4096
D = 1024
NF = 8
NT = 8
TC = 512
NTT = 32

ENGINES = ("tensor", "vector", "scalar", "gpsimd", "sync")


class K:

    AUTO_CHAIN = ("vector", "scalar", "gpsimd")

    def __init__(self, nc):
        self.nc = nc
        self.ops = {e: [] for e in ENGINES}
        self.cnt = {e: 0 for e in ENGINES}
        self.sems = {e: nc.alloc_semaphore(f"sem_{e}") for e in ENGINES}

    def op(self, eng, fn, inc=0, chain=None):
        if chain is None:
            chain = eng in self.AUTO_CHAIN
        if chain:
            if self.cnt[eng]:
                self.wait(eng, eng, self.cnt[eng])
            inc = max(inc, 1)
        tgt = None
        if inc:
            self.cnt[eng] += inc
            tgt = self.cnt[eng]
        sem = self.sems[eng]

        def run(e, f=fn, n=inc, s=sem):
            inst = f(e)
            if n:
                inst.then_inc(s, n)
        self.ops[eng].append(run)
        return tgt

    def dma(self, q, out, in_):
        m = self.op(q, lambda e, o=out, i=in_: e.dma_start(out=o, in_=i),
                    inc=16)
        self.wait(q, q, m)
        return m

    def dma_batch(self, q, pairs):
        m = None
        for out, in_ in pairs:
            m = self.op(q, lambda e, o=out, i=in_: e.dma_start(out=o, in_=i),
                        inc=16, chain=False)
        self.wait(q, q, m)
        return m

    def wait(self, eng, on, tgt):
        if tgt is None:
            return
        assert tgt <= self.cnt[on], (eng, on, tgt, self.cnt[on])
        self.ops[eng].append(lambda e, s=self.sems[on], v=tgt: e.wait_ge(s, v))

    def waitp(self, eng, pend):
        if pend is not None:
            self.wait(eng, pend[0], pend[1])

    def finish(self):
        with self.nc.Block() as block:
            for eng in ENGINES:
                getattr(block, eng)(
                    lambda e, ops=self.ops[eng]: [f(e) for f in ops])



def make_consts():
    c = {}
    c["c_eps"] = np.array([[1e-5, 1e-6, 1e-8, 1e-12, 1.0, 0.0]], dtype=np.float32)
    c["c_ident"] = np.eye(128, dtype=np.float32)
    c["c_ones128"] = np.ones((1, 128), dtype=np.float32)
    c["c_invD"] = np.full((128, 1), 1.0 / D, dtype=np.float32)
    c["c_one_col"] = np.ones((128, 1), dtype=np.float32)
    c["c_onesrow"] = np.ones((1, TC), dtype=np.float32)
    c["c_epscol"] = np.tile(np.array([[1e-5, 1e-6, 1e-8, 1e-12, 1.0, 0.0]],
                                     dtype=np.float32), (128, 1))
    p = np.arange(128)
    c["c_maskkgt"] = np.where(p[:, None] > p[None, :], -1e30, 0.0).astype(np.float32)
    c["c_negrow"] = np.full((1, 128), -1e30, dtype=np.float32)
    c["c_ones128f"] = np.ones((1, 128), dtype=np.float32)
    c["c_identf"] = np.eye(128, dtype=np.float32)
    c["c_phicol"] = np.tile(np.array(
        [[np.pi / 2, -np.log(16.0), 0.0, 12582912.0]],
        dtype=np.float32), (128, 1))
    c["c_iotaneg"] = np.tile((np.arange(1024, dtype=np.float64) - 1e9
                              ).astype(np.float32)[None, :], (128, 1))
    freqs = 1.0 / 10000.0 ** (np.arange(0, 64, 2, dtype=np.float64) / 64.0)
    ang = np.arange(T, dtype=np.float64)[:, None] * freqs[None, :]
    c["c_l0cos"] = np.cos(ang).astype(np.float32)
    c["c_l0sin"] = np.sin(ang).astype(np.float32)
    p = np.arange(128)
    c["c_maskcgt"] = np.where(p[None, :] > p[:, None], -1e30, 0.0).astype(np.float32)
    half_idx = np.arange(0, 64, 2, dtype=np.float64)
    invf = 1.0 / 10000.0 ** (half_idx / 64.0)
    t = np.arange(T, dtype=np.float64)
    fr = t[:, None] * invf[None, :]
    frc = np.concatenate([fr, fr], axis=1)
    base = (half_idx + 0.4 * 64.0) / (1.4 * 64.0)
    power = (t - T // 2) / 256.0
    scale = base[None, :] ** power[:, None]
    scale = np.concatenate([scale, scale], axis=1)
    cosv, sinv = np.cos(frc), np.sin(frc)
    A = np.zeros((T, 128)); B = np.zeros((T, 128))
    A[:, 0:64] = cosv * scale * 8.0
    A[:, 64:128] = cosv / scale
    sq = sinv * scale * 8.0
    si = sinv / scale
    B[:, 0:32] = -sq[:, 32:64]
    B[:, 0:32] = -sq[:, 0:32]
    B[:, 32:64] = sq[:, 32:64]
    B[:, 64:96] = -si[:, 0:32]
    B[:, 96:128] = si[:, 64 - 64 + 32:64]
    B[:, 96:128] = si[:, 32:64]
    c["c_xpA"] = A.astype(np.float32)
    c["c_xpB"] = B.astype(np.float32)
    return c


_CONSTS = None


def get_consts():
    global _CONSTS
    if _CONSTS is None:
        _CONSTS = make_consts()
    return _CONSTS


class Common:

    def __init__(self, nc, k, ctx):
        consts = get_consts()
        self.in_names = list(consts.keys())
        self.aps = {}
        for name in self.in_names:
            arr = consts[name]
            self.aps[name] = nc.dram_tensor(
                name, list(arr.shape), F32, kind="ExternalInput").ap()
        sb = lambda n, s, dt=F32: ctx.enter_context(nc.sbuf_tensor(n, s, dt))
        self.eps = sb("sb_eps", [1, 6])
        self.ident = sb("sb_ident", [128, 128], BF16)
        self.ones128 = sb("sb_ones128", [1, 128], BF16)
        self.invD = sb("sb_invD", [128, 1], BF16)
        self.onecol = sb("sb_onecol", [128, 1], BF16)
        self.onesrow = sb("sb_onesrow", [1, TC], BF16)
        self.epscol = sb("sb_epscol", [128, 6])
        self.maskkgt = sb("sb_maskkgt", [128, 128], BF16)
        self.maskcgt = sb("sb_maskcgt", [128, 128], BF16)
        self.negrow = sb("sb_negrow", [1, 128], BF16)
        self.ones128f = sb("sb_ones128f", [1, 128])
        self.identf = sb("sb_identf", [128, 128])
        self.phicol = sb("sb_phicol", [128, 4])
        k.dma("sync", self.phicol[:], self.aps["c_phicol"][:, :])
        k.dma("sync", self.identf[:], self.aps["c_identf"][:, :])
        k.dma("gpsimd", self.maskkgt[:], self.aps["c_maskkgt"][:, :])
        k.dma("gpsimd", self.maskcgt[:], self.aps["c_maskcgt"][:, :])
        k.dma("gpsimd", self.negrow[:], self.aps["c_negrow"][:, :])
        k.dma("sync", self.ones128f[:], self.aps["c_ones128f"][:, :])
        k.dma("sync", self.epscol[:], self.aps["c_epscol"][:, :])
        k.dma("sync", self.eps[:], self.aps["c_eps"][:, :])
        k.dma("gpsimd", self.ident[:], self.aps["c_ident"][:, :])
        k.dma("gpsimd", self.ones128[:], self.aps["c_ones128"][:, :])
        k.dma("gpsimd", self.invD[:], self.aps["c_invD"][:, :])
        k.dma("gpsimd", self.onecol[:], self.aps["c_one_col"][:, :])
        k.dma("gpsimd", self.onesrow[:], self.aps["c_onesrow"][:, :])
        sy, gp = k.cnt["sync"], k.cnt["gpsimd"]
        for e in ENGINES:
            k.wait(e, "sync", sy)
            k.wait(e, "gpsimd", gp)


def declare_w(nc, names_shapes):
    return {n: nc.dram_tensor(n, list(s), F32, kind="ExternalInput").ap()
            for n, s in names_shapes.items()}



def emit_ln(k, cm, ctx_unused, xdram, xnb, tag, eps_idx=0):
    nc = k.nc
    ctx = ExitStack()
    sb = lambda n, s, dt=F32: ctx.enter_context(
        nc.sbuf_tensor(tag + n, s, dt, side="right"))
    xa = sb("_xa", [128, 2, NF, TC], BF16)
    sq = sb("_sq", [128, 2, TC], BF16)
    murow = sb("_mu", [1, T])
    qrow = sb("_q", [1, T])
    rho_b = sb("_rhob", [1, T], BF16)
    nmurho_b = sb("_nmrb", [1, T], BF16)
    tmp = sb("_tmp", [1, T])
    tmpf = sb("_tmpf", [128, 2, TC])
    pS = ctx.enter_context(nc.psum_tensor(tag + "_pS", [1, TC], F32))
    pQ = ctx.enter_context(nc.psum_tensor(tag + "_pQ", [1, TC], F32))
    pR = ctx.enter_context(nc.psum_tensor(tag + "_pR", [128, TC], F32))
    pM = ctx.enter_context(nc.psum_tensor(tag + "_pM", [128, TC], F32))

    te_buf = [None, None]
    sc_buf = [None, None]
    sq_te = {}
    for tc in range(NT):
        buf = tc % 2
        k.wait("gpsimd", "tensor", te_buf[buf])
        k.wait("gpsimd", "scalar", sc_buf[buf])
        dm = k.dma_batch("gpsimd", [
            (xa[:, buf, fc, :],
             xdram[fc * 128:(fc + 1) * 128, tc * TC:(tc + 1) * TC])
            for fc in range(NF)])
        k.wait("scalar", "gpsimd", dm)
        k.wait("tensor", "gpsimd", dm)
        last_sq = None
        for fc in range(NF):
            gi = tc * NF + fc
            k.wait("scalar", "tensor", sq_te.get(gi - 2))
            m = k.op("scalar", lambda e, a=xa[:, buf, fc, :], o=sq[:, fc % 2, :]:
                     e.activation(o, a, AF.Square), inc=1)
            last_sq = m
            k.op("tensor", lambda e, a=xa[:, buf, fc, :], p=pS,
                 first=(fc == 0), last=(fc == NF - 1):
                 e.matmul(p[:, :], cm.invD[:], a, start=first, stop=last), inc=0)
            k.wait("tensor", "scalar", m)
            sq_te[gi] = k.op("tensor", lambda e, a=sq[:, fc % 2, :], p=pQ,
                             first=(fc == 0), last=(fc == NF - 1):
                             e.matmul(p[:, :], cm.invD[:], a, start=first,
                                      stop=last), inc=1)
        te_buf[buf] = k.cnt["tensor"]
        sc_buf[buf] = last_sq
        k.wait("vector", "tensor", te_buf[buf])
        k.op("vector", lambda e, p=pS, o=murow[:, tc * TC:(tc + 1) * TC]:
             e.tensor_copy(o, p[:, :]), inc=0)
        em = k.op("vector", lambda e, p=pQ, o=qrow[:, tc * TC:(tc + 1) * TC]:
                  e.tensor_copy(o, p[:, :]), inc=1)
        if tc < NT - 1:
            k.wait("tensor", "vector", em)

    k.op("vector", lambda e: e.tensor_mul(tmp[:, :], murow[:, :], murow[:, :]),
         inc=0)
    vm = k.op("vector", lambda e: e.tensor_sub(tmp[:, :], qrow[:, :], tmp[:, :]),
              inc=1)
    k.wait("scalar", "vector", vm)
    k.op("scalar", lambda e: e.activation(
        tmp[:, :], tmp[:, :], AF.Ln, bias=cm.eps[0:1, eps_idx:eps_idx + 1]), inc=0)
    sm = k.op("scalar", lambda e: e.activation(tmp[:, :], tmp[:, :], AF.Exp,
                                               scale=-0.5), inc=1)
    k.wait("vector", "scalar", sm)
    k.op("vector", lambda e: e.tensor_copy(rho_b[:, :], tmp[:, :]), inc=0)
    k.op("vector", lambda e: e.tensor_mul(tmp[:, :], murow[:, :], tmp[:, :]), inc=0)
    k.op("vector", lambda e: e.tensor_scalar_mul(tmp[:, :], tmp[:, :], -1.0), inc=0)
    rm = k.op("vector", lambda e: e.tensor_copy(nmurho_b[:, :], tmp[:, :]), inc=1)

    k.wait("tensor", "vector", rm)
    ve_buf = [None, None]
    prev_ve = None
    passa_sc = k.cnt["scalar"]
    passa_te = k.cnt["tensor"]
    for tc in range(NT):
        buf = tc % 2
        if tc < 2:
            k.wait("gpsimd", "scalar", passa_sc)
            k.wait("gpsimd", "tensor", passa_te)
        tm = None
        k.wait("tensor", "vector", prev_ve)
        k.op("tensor", lambda e, p=pR, r=rho_b[:, tc * TC:(tc + 1) * TC]:
             e.matmul(p[:, :], cm.ones128[:, :], r, start=True, stop=True), inc=0)
        tm = k.op("tensor", lambda e, p=pM, r=nmurho_b[:, tc * TC:(tc + 1) * TC]:
                  e.matmul(p[:, :], cm.ones128[:, :], r, start=True, stop=True),
                  inc=1)
        k.wait("gpsimd", "vector", ve_buf[buf])
        dm = k.dma_batch("gpsimd", [
            (xa[:, buf, fc, :],
             xdram[fc * 128:(fc + 1) * 128, tc * TC:(tc + 1) * TC])
            for fc in range(NF)])
        k.wait("vector", "gpsimd", dm)
        k.wait("vector", "tensor", tm)
        for fc in range(NF):
            k.op("vector", lambda e, a=xa[:, buf, fc, :], p=pR,
                 o=tmpf[:, fc % 2, :]:
                 e.tensor_tensor(o, a, p[:, :], ALU.mult), inc=0)
            k.op("vector", lambda e, a=tmpf[:, fc % 2, :], p=pM,
                 o=xnb[:, fc, tc * TC:(tc + 1) * TC]:
                 e.tensor_tensor(o, a, p[:, :], ALU.add), inc=(1 if fc == NF - 1 else 0))
        ve_buf[buf] = k.cnt["vector"]
        prev_ve = k.cnt["vector"]
    marks = {e: k.cnt[e] for e in ENGINES}
    ctx.close()
    for e in ENGINES:
        for o in ("tensor", "vector", "scalar", "gpsimd"):
            if o != e:
                k.wait(e, o, marks[o])
    return k.cnt["vector"]



def emit_ln_sbuf_affine(k, cm, ctx_unused, xnb, gamma_dram, beta_dram, tag,
                        eps_idx=0):
    nc = k.nc
    ctx = ExitStack()
    sb = lambda n, s, dt=F32: ctx.enter_context(
        nc.sbuf_tensor(tag + n, s, dt, side="right"))
    sq = sb("_sq", [128, 2, TC], BF16)
    murow = sb("_mu", [1, T])
    qrow = sb("_q", [1, T])
    rho_b = sb("_rhob", [1, T], BF16)
    nmurho_b = sb("_nmrb", [1, T], BF16)
    tmp = sb("_tmp", [1, T])
    tmpf = sb("_tmpf", [128, 2, TC])
    gb = sb("_gb", [1, D], BF16)
    bb = sb("_bb", [1, D], BF16)
    pS = ctx.enter_context(nc.psum_tensor(tag + "_pS", [1, TC], F32))
    pQ = ctx.enter_context(nc.psum_tensor(tag + "_pQ", [1, TC], F32))
    pR = ctx.enter_context(nc.psum_tensor(tag + "_pR", [128, TC], F32))
    pM = ctx.enter_context(nc.psum_tensor(tag + "_pM", [128, TC], F32))
    gm = k.dma("gpsimd", gb[:, :], gamma_dram[:, :])
    gm = k.dma("gpsimd", bb[:, :], beta_dram[:, :])
    k.wait("tensor", "gpsimd", gm)

    sq_te = {}
    for tc in range(NT):
        last_sq = None
        for fc in range(NF):
            gi = tc * NF + fc
            k.wait("scalar", "tensor", sq_te.get(gi - 2))
            m = k.op("scalar", lambda e, a=xnb[:, fc, tc * TC:(tc + 1) * TC],
                     o=sq[:, fc % 2, :]:
                     e.activation(o, a, AF.Square), inc=1)
            last_sq = m
            k.op("tensor", lambda e, a=xnb[:, fc, tc * TC:(tc + 1) * TC], p=pS,
                 first=(fc == 0), last=(fc == NF - 1):
                 e.matmul(p[:, :], cm.invD[:], a, start=first, stop=last), inc=0)
            k.wait("tensor", "scalar", m)
            sq_te[gi] = k.op("tensor", lambda e, a=sq[:, fc % 2, :], p=pQ,
                             first=(fc == 0), last=(fc == NF - 1):
                             e.matmul(p[:, :], cm.invD[:], a, start=first,
                                      stop=last), inc=1)
        te_m = k.cnt["tensor"]
        k.wait("vector", "tensor", te_m)
        k.op("vector", lambda e, p=pS, o=murow[:, tc * TC:(tc + 1) * TC]:
             e.tensor_copy(o, p[:, :]), inc=0)
        em = k.op("vector", lambda e, p=pQ, o=qrow[:, tc * TC:(tc + 1) * TC]:
                  e.tensor_copy(o, p[:, :]), inc=1)
        if tc < NT - 1:
            k.wait("tensor", "vector", em)

    k.op("vector", lambda e: e.tensor_mul(tmp[:, :], murow[:, :], murow[:, :]),
         inc=0)
    vm = k.op("vector", lambda e: e.tensor_sub(tmp[:, :], qrow[:, :], tmp[:, :]),
              inc=1)
    k.wait("scalar", "vector", vm)
    k.op("scalar", lambda e: e.activation(
        tmp[:, :], tmp[:, :], AF.Ln, bias=cm.eps[0:1, eps_idx:eps_idx + 1]), inc=0)
    sm = k.op("scalar", lambda e: e.activation(tmp[:, :], tmp[:, :], AF.Exp,
                                               scale=-0.5), inc=1)
    k.wait("vector", "scalar", sm)
    k.op("vector", lambda e: e.tensor_copy(rho_b[:, :], tmp[:, :]), inc=0)
    k.op("vector", lambda e: e.tensor_mul(tmp[:, :], murow[:, :], tmp[:, :]), inc=0)
    k.op("vector", lambda e: e.tensor_scalar_mul(tmp[:, :], tmp[:, :], -1.0), inc=0)
    rm = k.op("vector", lambda e: e.tensor_copy(nmurho_b[:, :], tmp[:, :]), inc=1)
    k.wait("tensor", "vector", rm)

    prev_ve = None
    for tc in range(NT):
        for fc in range(NF):
            k.wait("tensor", "vector", prev_ve)
            k.op("tensor", lambda e, p=pR, g=gb[0:1, fc * 128:(fc + 1) * 128],
                 r=rho_b[:, tc * TC:(tc + 1) * TC]:
                 e.matmul(p[:, :], g, r, start=True, stop=True), inc=0)
            k.op("tensor", lambda e, p=pM, g=gb[0:1, fc * 128:(fc + 1) * 128],
                 r=nmurho_b[:, tc * TC:(tc + 1) * TC]:
                 e.matmul(p[:, :], g, r, start=True, stop=False), inc=0)
            tm = k.op("tensor", lambda e, p=pM, b=bb[0:1, fc * 128:(fc + 1) * 128]:
                      e.matmul(p[:, :], b, cm.onesrow[:, :], start=False,
                               stop=True), inc=1)
            k.wait("vector", "tensor", tm)
            k.op("vector", lambda e, a=xnb[:, fc, tc * TC:(tc + 1) * TC], p=pR,
                 o=tmpf[:, fc % 2, :]:
                 e.tensor_tensor(o, a, p[:, :], ALU.mult), inc=1)
            k.op("vector", lambda e, a=tmpf[:, fc % 2, :], p=pM,
                 o=xnb[:, fc, tc * TC:(tc + 1) * TC]:
                 e.tensor_tensor(o, a, p[:, :], ALU.add), inc=1)
            prev_ve = k.cnt["vector"]
    marks = {e: k.cnt[e] for e in ENGINES}
    ctx.close()
    for e in ENGINES:
        for o in ("tensor", "vector", "scalar", "gpsimd"):
            if o != e:
                k.wait(e, o, marks[o])
    return k.cnt["vector"]



class PsumRot:

    def __init__(self, k, ctx, tag, n=4, shape=(128, TC)):
        self.k = k
        self.banks = [ctx.enter_context(
            k.nc.psum_tensor(f"{tag}_rot{i}", list(shape), F32)) for i in range(n)]
        self.pend = [None] * n
        self.i = 0

    def next(self):
        i = self.i
        self.i = (self.i + 1) % len(self.banks)
        p = self.banks[i]
        if self.pend[i] is not None:
            self.k.wait("tensor", self.pend[i][0], self.pend[i][1])

        def release(em, idx=i):
            self.pend[idx] = em
        return p, release


def emit_mm_acc(k, lhs_slices, rhs_slices, psum, final_inc=1):
    n = len(lhs_slices)
    for i, (w, r) in enumerate(zip(lhs_slices, rhs_slices)):
        k.op("tensor", lambda e, ww=w, rr=r, p=psum, first=(i == 0),
             last=(i == n - 1): e.matmul(p[:, :], ww, rr, start=first, stop=last),
             inc=(final_inc if i == n - 1 else 0))
    return k.cnt["tensor"]



def fourier_consts():
    j = np.arange(D)
    ang = (-2.0 * np.pi / D) * np.outer(j, j % D).astype(np.float64)
    return (np.cos(ang).astype(np.float32), np.sin(ang).astype(np.float32))


class NCW:

    def __init__(self, nc):
        self._nc = nc
        self._names = {}

    def _uniq(self, name):
        i = self._names.get(name, 0)
        self._names[name] = i + 1
        return name if i == 0 else f"{name}_u{i}"

    def sbuf_tensor(self, name, *a, **kw):
        return self._nc.sbuf_tensor(self._uniq(name), *a, **kw)

    def psum_tensor(self, name, *a, **kw):
        return self._nc.psum_tensor(self._uniq(name), *a, **kw)

    def __getattr__(self, attr):
        return getattr(self._nc, attr)


def _block_env(env):
    if env is not None:
        nc, k, ctx_outer, cm, xt, xo = env
        return nc, k, ExitStack(), cm, xt, xo, False
    nc = bass.Bass("TRN2", target_bir_lowering=False, debug=False)
    k = K(nc)
    ctx = ExitStack()
    cm = Common(nc, k, ctx)
    xt = nc.dram_tensor("xt", [D, T], F32, kind="ExternalInput").ap()
    xo = nc.dram_tensor("xo", [D, T], F32, kind="ExternalOutput").ap()
    return nc, k, ctx, cm, xt, xo, True


def _block_close(k, ctx):
    m = {e: k.cnt[e] for e in ENGINES}
    ctx.close()
    for e in ENGINES:
        for o in ENGINES:
            if o != e:
                k.wait(e, o, m[o])


def build_L2f(env=None):
    nc, k, ctx, cm, xt, xo, standalone = _block_env(env)
    W = declare_w(nc, {
        "l2f_wr": (D, D), "l2f_wi": (D, D), "l2f_bias": (1, D),
        "c_Fr": (D, D), "c_Fi": (D, D),
    })
    sb = lambda n, s, dt=F32: ctx.enter_context(nc.sbuf_tensor(n, s, dt))
    xnb = sb("xnb", [128, NF, T], BF16)
    Mb = sb("Mb", [128, NF, D], BF16)
    biasb = sb("biasb", [1, D], BF16)
    biasf = sb("biasf", [1, D])
    xres = sb("xres", [128, 2, TC])
    rot = PsumRot(k, ctx, "l2f")
    ctx2 = ExitStack()
    sb2 = lambda n, s, dt=F32: ctx2.enter_context(nc.sbuf_tensor(n, s, dt))
    Frb = sb2("Frb", [128, NF, D], BF16)
    Fib = sb2("Fib", [128, NF, D], BF16)
    wrb = sb2("wrb", [128, NF, D], BF16)
    wip = sb2("wip", [128, NF, D], BF16)
    ArT = sb2("ArT", [128, NF, D], BF16)
    AiT = sb2("AiT", [128, NF, D], BF16)

    k.dma_batch("gpsimd", [
        (b_[:, fc, :], W[wn][fc * 128:(fc + 1) * 128, :])
        for fc in range(NF)
        for b_, wn in ((Frb, "c_Fr"), (Fib, "c_Fi"),
                       (wrb, "l2f_wr"), (wip, "l2f_wi"))])
    mload = k.cnt["gpsimd"]
    mb = k.dma("sync", biasf[:, :], W["l2f_bias"][:, :])
    k.wait("vector", "gpsimd", mload)
    k.wait("vector", "sync", mb)
    nw = k.op("vector", lambda e: e.tensor_copy(biasb[:, :], biasf[:, :]), inc=1)

    def mm_pair(dst, l1, r1, l2, r2, scale):
        last = None
        for oc in range(NF):
            for ic in range(2):
                p, release = rot.next()
                lhs, rhs = [], []
                for kc in range(NF):
                    lhs += [l1[:, kc, oc * 128:(oc + 1) * 128],
                            l2[:, kc, oc * 128:(oc + 1) * 128]]
                    rhs += [r1[:, kc, ic * 512:(ic + 1) * 512],
                            r2[:, kc, ic * 512:(ic + 1) * 512]]
                tem = emit_mm_acc(k, lhs, rhs, p)
                k.wait("scalar", "tensor", tem)
                em = k.op("scalar", lambda e, pp=p,
                          o=dst[:, oc, ic * 512:(ic + 1) * 512], s=scale:
                          e.activation(o, pp[:, :], AF.Copy, scale=s), inc=1)
                release(("scalar", em))
                last = em
        return last

    k.wait("tensor", "gpsimd", mload)
    m2 = mm_pair(AiT, wrb, Fib, wip, Frb, 1.0)
    k.wait("vector", "tensor", k.cnt["tensor"])
    ng = None
    for fc in range(NF):
        ng = k.op("vector", lambda e, a=wip[:, fc, :]:
                  e.tensor_scalar_mul(a, a, -1.0), inc=(1 if fc == NF - 1 else 0))
    k.wait("tensor", "vector", ng)
    m1 = mm_pair(ArT, wrb, Frb, wip, Fib, 1.0)
    m3 = mm_pair(Mb, ArT, Frb, AiT, Fib, 1.0 / D)
    te_done = k.cnt["tensor"]
    sc_done = k.cnt["scalar"]
    ctx2.close()
    for e in ("gpsimd", "vector", "sync"):
        k.wait(e, "tensor", te_done)
        k.wait(e, "scalar", sc_done)

    ln_mark = emit_ln(k, cm, ctx, xt, xnb, "ln2f")
    k.wait("tensor", "vector", nw)
    k.wait("tensor", "vector", ln_mark)

    pend_store = [None, None]
    call = [0]

    def out_evac(oc, tc, p, release):
        tm = k.op("tensor", lambda e, pp=p, b=biasb[0:1, oc * 128:(oc + 1) * 128]:
                  e.matmul(pp[:, :], b, cm.onesrow[:, :], start=False, stop=True),
                  inc=1)
        buf = call[0] % 2
        call[0] += 1
        k.wait("sync", "sync", pend_store[buf])
        dm = k.dma("sync", xres[:, buf, :],
                   xt[oc * 128:(oc + 1) * 128, tc * TC:(tc + 1) * TC])
        k.wait("vector", "sync", dm)
        k.wait("vector", "tensor", tm)
        vm = k.op("vector", lambda e, pp=p, xr=xres[:, buf, :]:
                  e.tensor_tensor(xr, xr, pp[:, :], ALU.add), inc=1)
        k.wait("sync", "vector", vm)
        sm = k.dma("sync", xo[oc * 128:(oc + 1) * 128, tc * TC:(tc + 1) * TC],
                   xres[:, buf, :])
        pend_store[buf] = sm
        release(("vector", vm))

    for oc in range(NF):
        for tc in range(NT):
            p, release = rot.next()
            lhs = [Mb[:, kc, oc * 128:(oc + 1) * 128] for kc in range(NF)]
            rhs = [xnb[:, kc, tc * TC:(tc + 1) * TC] for kc in range(NF)]
            emit_mm_acc(k, lhs, rhs, p, final_inc=0)
            out_evac(oc, tc, p, release)

    k.wait("sync", "sync", k.cnt["sync"])
    if not standalone:
        return _block_close(k, ctx)
    k.finish()
    ctx.close()
    in_names = (["xt", "l2f_wr", "l2f_wi", "l2f_bias", "c_Fr", "c_Fi"]
                + cm.in_names)
    return nc, in_names



def emit_proj_evac(k, cm, rot, wb, xnb, bias_b, evac, scale_done=None):
    for oc in range(NF):
        for tc in range(NT):
            p, release = rot.next()
            for kc in range(NF):
                last = (kc == NF - 1 and bias_b is None)
                tem = k.op("tensor",
                           lambda e, w=wb[:, kc, oc * 128:(oc + 1) * 128],
                           r=xnb[:, kc, tc * TC:(tc + 1) * TC], pp=p,
                           first=(kc == 0), la=last:
                           e.matmul(pp[:, :], w, r, start=first, stop=la),
                           inc=(1 if last else 0))
            if bias_b is not None:
                tem = k.op("tensor",
                           lambda e, pp=p, b=bias_b[0:1, oc * 128:(oc + 1) * 128]:
                           e.matmul(pp[:, :], b, cm.onesrow[:, :], start=False,
                                    stop=True), inc=1)
            evac(oc, tc, p, tem, release)


def emit_out_proj_dram(k, cm, ctx, wdram, bias_dram, srcdram, xdram, xodram,
                       tag, nk=NF):
    nc = k.nc
    sb = lambda n, s, dt=F32: ctx.enter_context(nc.sbuf_tensor(tag + n, s, dt))
    wb = sb("_wb", [128, nk, D], BF16)
    biasb = sb("_biasb", [1, D], BF16)
    srcb = sb("_srcb", [128, 2, nk, TC], BF16)
    xres = sb("_xres", [128, 4, TC])
    rot = PsumRot(k, ctx, tag + "r")
    wm = k.dma_batch("gpsimd", [
        (wb[:, kc, :], wdram[kc * 128:(kc + 1) * 128, :])
        for kc in range(nk)
    ] + ([(biasb[:, :], bias_dram[:, :])] if bias_dram is not None else []))
    k.wait("tensor", "gpsimd", wm)
    pend_store = {}
    load_mark = {}
    srcb_pend = [None, None]
    NITER = NT * NF

    def issue_loads(fi0):
        pairs = []
        fis = [fi for fi in (fi0, fi0 + 1) if fi < NITER]
        for fi in fis:
            tc_, oc_ = divmod(fi, NF)
            sm2 = pend_store.get(fi - 4)
            if sm2 is not None:
                k.wait("gpsimd", "sync", sm2)
            pairs.append((xres[:, fi % 4, :],
                          xdram[oc_ * 128:(oc_ + 1) * 128,
                                tc_ * TC:(tc_ + 1) * TC]))
        if pairs:
            m = k.dma_batch("gpsimd", pairs)
            for fi in fis:
                load_mark[fi] = m

    issue_loads(0)
    issue_loads(2)
    for tc in range(NT):
        sbuf_i = tc % 2
        k.wait("gpsimd", "tensor", srcb_pend[sbuf_i])
        sm = k.dma_batch("gpsimd", [
            (srcb[:, sbuf_i, kc, :],
             srcdram[kc * 128:(kc + 1) * 128, tc * TC:(tc + 1) * TC])
            for kc in range(nk)])
        k.wait("tensor", "gpsimd", sm)
        for oc in range(NF):
            fi = tc * NF + oc
            p, release = rot.next()
            for kc in range(nk):
                k.op("tensor", lambda e, w=wb[:, kc, oc * 128:(oc + 1) * 128],
                     r=srcb[:, sbuf_i, kc, :], pp=p, first=(kc == 0),
                     last=(kc == nk - 1 and bias_dram is None):
                     e.matmul(pp[:, :], w, r, start=first, stop=last),
                     inc=(1 if (kc == nk - 1 and bias_dram is None) else 0))
            if bias_dram is not None:
                tm = k.op("tensor",
                          lambda e, pp=p, b=biasb[0:1, oc * 128:(oc + 1) * 128]:
                          e.matmul(pp[:, :], b, cm.onesrow[:, :], start=False,
                                   stop=True), inc=1)
            else:
                tm = k.cnt["tensor"]
            buf = fi % 4
            k.wait("vector", "gpsimd", load_mark[fi])
            k.wait("vector", "tensor", tm)
            vm = k.op("vector", lambda e, pp=p, xr=xres[:, buf, :]:
                      e.tensor_tensor(xr, xr, pp[:, :], ALU.add), inc=1)
            k.wait("sync", "vector", vm)
            pend_store[fi] = k.dma(
                "sync", xodram[oc * 128:(oc + 1) * 128, tc * TC:(tc + 1) * TC],
                xres[:, buf, :])
            release(("vector", vm))
            if fi % 2 == 1:
                issue_loads(fi + 3)
        srcb_pend[sbuf_i] = k.cnt["tensor"]



def build_L1a(env=None):
    nc, k, ctx, cm, xt, xo, standalone = _block_env(env)
    W = declare_w(nc, {
        "l1a_wqT": (D, D), "l1a_wkT": (D, D), "l1a_wvT": (D, D),
        "l1a_woT": (D, D), "l1a_bq": (1, D), "l1a_bk": (1, D),
        "l1a_bv": (1, D), "l1a_bo": (1, D),
    })
    qd = nc.dram_tensor("l1a_qd", [D, T], BF16, kind="Internal").ap()
    kd = nc.dram_tensor("l1a_kd", [D, T], BF16, kind="Internal").ap()
    otd = nc.dram_tensor("l1a_otd", [D, T], BF16, kind="Internal").ap()

    sb = lambda n, s, dt=F32: ctx.enter_context(nc.sbuf_tensor(n, s, dt))

    ctxA = ExitStack()
    sbA = lambda n, s, dt=F32: ctxA.enter_context(
        nc.sbuf_tensor(n, s, dt, side="right"))
    xnb = sbA("xnb", [128, NF, T], BF16)

    ln_mark = emit_ln(k, cm, ctx, xt, xnb, "ln1a")

    vtok = sb("vtok", [128, NTT, 16, 65], BF16)
    k2cols = sb("k2cols", [128, NTT, 16])
    mcols = sb("mcols", [128, NTT, 16])
    wqb = sbA("wqb", [128, NF, D], BF16)
    wkb = sbA("wkb", [128, NF, D], BF16)
    wvb = sbA("wvb", [128, NF, D], BF16)
    browsb = sbA("browsb", [1, 3 * D], BF16)
    spill = sbA("spill", [128, 4, TC], BF16)
    sqt = sbA("sqt", [128, D], BF16)

    bm = k.dma_batch("gpsimd", [
        (wb_[:, kc, :], W[wn][kc * 128:(kc + 1) * 128, :])
        for kc in range(NF)
        for wb_, wn in ((wqb, "l1a_wqT"), (wkb, "l1a_wkT"), (wvb, "l1a_wvT"))
    ] + [
        (browsb[0:1, 0:D], W["l1a_bq"][:, :]),
        (browsb[0:1, D:2 * D], W["l1a_bk"][:, :]),
        (browsb[0:1, 2 * D:3 * D], W["l1a_bv"][:, :]),
    ])
    bqr = browsb[0:1, 0:D]
    bkr = browsb[0:1, D:2 * D]
    bvr = browsb[0:1, 2 * D:3 * D]

    rot = PsumRot(k, ctxA, "l1a", 2)
    k.wait("tensor", "gpsimd", bm)
    k.wait("tensor", "vector", ln_mark)

    spill_pend = [None] * 4
    spill_call = [0]

    def spill_evac(dst_dram, scale):
        def evac(oc, tc, p, tem, release):
            i = spill_call[0] % 4
            spill_call[0] += 1
            k.wait("scalar", "sync", spill_pend[i])
            k.wait("scalar", "tensor", tem)
            em = k.op("scalar", lambda e, pp=p, o=spill[:, i, :], s=scale:
                      e.activation(o, pp[:, :], AF.Copy, scale=s), inc=1)
            k.wait("sync", "scalar", em)
            dmm = k.dma("sync", dst_dram[oc * 128:(oc + 1) * 128,
                                         tc * TC:(tc + 1) * TC], spill[:, i, :])
            spill_pend[i] = dmm
            release(("scalar", em))
        return evac

    emit_proj_evac(k, cm, rot, wqb, xnb, bqr, spill_evac(qd, 1.0))
    emit_proj_evac(k, cm, rot, wkb, xnb, bkr, spill_evac(kd, -2.0))

    pV = ctxA.enter_context(nc.psum_tensor("l1a_pV", [128, D], F32))
    pKt = ctxA.enter_context(nc.psum_tensor("l1a_pKt", [128, D], F32))
    pv_pend = [None]
    pk_pend = [None]
    for tt in range(NTT):
        t0 = tt * 128
        k.wait("tensor", "vector", pv_pend[0])
        for kc in range(NF):
            for half in range(2):
                k.op("tensor", lambda e, l=xnb[:, kc, t0:t0 + 128],
                     r=wvb[:, kc, half * 512:(half + 1) * 512],
                     pp=pV[:, bass.ts(half, 512)], first=(kc == 0):
                     e.matmul(pp, l, r, start=first, stop=False), inc=0)
        for half in range(2):
            k.op("tensor", lambda e, pp=pV[:, bass.ts(half, 512)],
                 b=bvr[0:1, half * 512:(half + 1) * 512]:
                 e.matmul(pp, cm.ones128[:, :], b, start=False, stop=True), inc=0)
        k.wait("tensor", "scalar", pk_pend[0])
        for kc in range(NF):
            for half in range(2):
                k.op("tensor", lambda e, l=xnb[:, kc, t0:t0 + 128],
                     r=wkb[:, kc, half * 512:(half + 1) * 512],
                     pp=pKt[:, bass.ts(half, 512)], first=(kc == 0):
                     e.matmul(pp, l, r, start=first, stop=False), inc=0)
        for half in range(2):
            k.op("tensor", lambda e, pp=pKt[:, bass.ts(half, 512)],
                 b=bkr[0:1, half * 512:(half + 1) * 512]:
                 e.matmul(pp, cm.ones128[:, :], b, start=False, stop=True),
                 inc=(1 if half == 1 else 0))
        tmk = k.cnt["tensor"]
        k.wait("scalar", "tensor", tmk)
        sm = k.op("scalar", lambda e: e.activation(sqt[:, :], pKt[:, :], AF.Square),
                  inc=1)
        pk_pend[0] = sm
        k.wait("vector", "scalar", sm)
        k.op("vector", lambda e, o=k2cols[:, tt, :]:
             e.reduce_sum(o, sqt[:].rearrange("p (h d) -> p h d", h=16), axis=AX.X),
             inc=1)
        vm0 = k.cnt["vector"]
        k.wait("scalar", "vector", vm0)
        sm2 = k.op("scalar", lambda e, o=mcols[:, tt, :], a=k2cols[:, tt, :]:
                   e.activation(o, a, AF.Sqrt), inc=1)
        k.wait("vector", "scalar", sm2)
        for h in range(16):
            k.op("vector", lambda e, o=vtok[:, tt, h, 0:64],
                 a=pV[:, 64 * h:64 * h + 64], s=mcols[:, tt, h:h + 1]:
                 e.tensor_scalar_mul(o, a, s), inc=0)
            k.op("vector", lambda e, o=vtok[:, tt, h, 64:65],
                 a=mcols[:, tt, h:h + 1]:
                 e.tensor_copy(o, a), inc=(1 if h == 15 else 0))
        pv_pend[0] = k.cnt["vector"]

    teA, scA, veA, gpA, syA = (k.cnt["tensor"], k.cnt["scalar"],
                               k.cnt["vector"], k.cnt["gpsimd"], k.cnt["sync"])
    ctxA.close()
    for e in ENGINES:
        k.wait(e, "tensor", teA)
        k.wait(e, "scalar", scA)
        k.wait(e, "vector", veA)
        k.wait(e, "sync", syA)

    ctxB = ExitStack()
    sbB = lambda n, s, dt=F32: ctxB.enter_context(
        nc.sbuf_tensor(n, s, dt, side="right"))
    qh = sbB("qh", [65, 2, T], BF16)
    kh = sbB("kh", [65, 2, T], BF16)
    sqh = sbB("sqh", [64, T], BF16)
    q2row = sbB("q2row", [1, T], BF16)
    stile = sbB("stile", [128, 2, 2 * TC])
    ltile = sbB("ltile", [128, 2, 2 * TC])
    util = sbB("util", [128, 2, 2 * TC], BF16)
    rrow = sbB("rrow", [1, TC])
    rrowb = sbB("rrowb", [1, TC], BF16)
    repb = sbB("repb", [64, 2, TC], BF16)
    oT = sbB("oT", [64, 2, TC], BF16)
    pD = [ctxB.enter_context(nc.psum_tensor(f"l1a_pD{i}", [128, 2 * TC], F32))
          for i in range(2)]
    pN = [ctxB.enter_context(nc.psum_tensor(f"l1a_pN{i}", [65, TC], F32))
          for i in range(2)]
    pRep = ctxB.enter_context(nc.psum_tensor("l1a_pRep", [64, TC], F32))
    pQ2 = ctxB.enter_context(nc.psum_tensor("l1a_pQ2", [1, TC], F32))

    head_pend = [None, None]
    pd_pend = [None, None]
    u_pend = {}
    pn_pend = [None, None]
    q2_pend = [None]
    rep_pend = [None]
    o_pend = [None, None]
    sq_pend = [None]

    vkm = k.op("vector", lambda e: e.memset(kh[64:65, :, :], 1.0), inc=1)
    k.wait("tensor", "vector", vkm)

    for h in range(16):
        hb = h % 2
        fcq = h // 2
        rowbase = 64 * (h % 2)
        k.wait("gpsimd", "tensor", head_pend[hb])
        dmq = k.dma("gpsimd", qh[0:64, hb, :],
                    qd[64 * h:64 * h + 64, :])
        dmk = k.dma("gpsimd", kh[0:64, hb, :],
                    kd[64 * h:64 * h + 64, :])
        k.wait("scalar", "gpsimd", dmq)
        sm = k.op("scalar", lambda e, a=qh[0:64, hb, :]:
                  e.activation(sqh[:, :], a, AF.Square), inc=1)
        k.wait("tensor", "scalar", sm)
        k.wait("tensor", "vector", q2_pend[0])
        k.wait("vector", "tensor", head_pend[hb]) if head_pend[hb] else None
        for tc in range(NT):
            k.op("tensor", lambda e, a=sqh[:, tc * TC:(tc + 1) * TC]:
                 e.matmul(pQ2[:, :], cm.onecol[0:64, :], a, start=True, stop=True),
                 inc=1)
            tmq = k.cnt["tensor"]
            k.wait("vector", "tensor", tmq)
            vq = k.op("vector", lambda e, o=qh[64:65, hb,
                                               tc * TC:(tc + 1) * TC]:
                      e.tensor_copy(o, pQ2[:, :]), inc=1)
            if tc < NT - 1:
                k.wait("tensor", "vector", vq)
        q2m = k.cnt["vector"]
        q2_pend[0] = q2m
        k.wait("tensor", "vector", q2m)
        k.wait("tensor", "gpsimd", dmk)
        for qc in range(NT):
            ni = qc % 2
            k.wait("tensor", pn_pend[ni][0], pn_pend[ni][1]) if pn_pend[ni] else None
            d2_mark = {}

            def emit_d2(kp, qc=qc):
                di = kp % 2
                k.wait("tensor", "scalar", pd_pend[di])
                for sub in range(2):
                    kt = kp * 2 + sub
                    k.op("tensor", lambda e, l=kh[:, hb, kt * 128:(kt + 1) * 128],
                         r=qh[:, hb, qc * TC:(qc + 1) * TC],
                         pp=pD[di][:, bass.ts(sub, TC)]:
                         e.matmul(pp, l, r, start=True, stop=True), inc=1)
                d2_mark[kp] = k.cnt["tensor"]

            emit_d2(0)
            for kp in range(16):
                di = kp % 2
                if kp + 1 < 16:
                    emit_d2(kp + 1)
                k.wait("scalar", "tensor", d2_mark[kp])
                for sub in range(2):
                    kt = kp * 2 + sub
                    k.op("scalar", lambda e, pp=pD[di][:, bass.ts(sub, TC)],
                         o=stile[:, di, bass.ts(sub, TC)],
                         b=k2cols[:, kt, h:h + 1]:
                         e.activation(o, pp, AF.Ln, bias=b),
                         inc=(1 if sub == 1 else 0))
                pd_pend[di] = k.cnt["scalar"]
                k.wait("scalar", "tensor", u_pend.get((h, qc, kp - 2), None))                     if (h, qc, kp - 2) in u_pend else None
                um = k.op("scalar", lambda e, a=stile[:, di, :], o=util[:, di, :]:
                          e.activation(o, a, AF.Exp, scale=-1.0), inc=1)
                k.wait("tensor", "scalar", um)
                for sub in range(2):
                    kt = kp * 2 + sub
                    u_pend[(h, qc, kp)] = k.op(
                        "tensor", lambda e, l=vtok[:, kt, h, :],
                        r=util[:, di, bass.ts(sub, TC)], pp=pN[ni],
                        first=(kp == 0 and sub == 0),
                        last=(kp == 15 and sub == 1):
                        e.matmul(pp[:, :], l, r, start=first, stop=last),
                        inc=(1 if sub == 1 else 0))
            tmn = k.cnt["tensor"]
            k.wait("vector", "tensor", tmn)
            k.wait("vector", "sync", o_pend[ni])
            k.op("vector", lambda e, pp=pN[ni]:
                 e.reciprocal(rrow[:, :], pp[64:65, :]), inc=0)
            vm1 = k.op("vector", lambda e:
                       e.tensor_copy(rrowb[:, :], rrow[:, :]), inc=1)
            k.wait("tensor", "vector", vm1)
            k.wait("tensor", "scalar", rep_pend[0])
            tmr = k.op("tensor", lambda e:
                       e.matmul(pRep[:, :], cm.ones128[0:1, 0:64], rrowb[:, :],
                                start=True, stop=True), inc=1)
            k.wait("scalar", "tensor", tmr)
            smr = k.op("scalar", lambda e, o=repb[:, ni, :]:
                       e.activation(o, pRep[:, :], AF.Copy), inc=1)
            rep_pend[0] = smr
            k.wait("vector", "scalar", smr)
            vm2 = k.op("vector", lambda e, pp=pN[ni], o=oT[:, ni, :],
                       rr=repb[:, ni, :]:
                       e.tensor_tensor(o, pp[0:64, :], rr, ALU.mult), inc=1)
            pn_pend[ni] = ("vector", vm2)
            k.wait("sync", "vector", vm2)
            o_pend[ni] = k.dma("sync", otd[64 * h:64 * h + 64,
                                           qc * TC:(qc + 1) * TC], oT[:, ni, :])
        head_pend[hb] = k.cnt["tensor"]
        u_pend.clear()

    te2, sc2, ve2, sy2 = (k.cnt["tensor"], k.cnt["scalar"], k.cnt["vector"],
                          k.cnt["sync"])
    ctxB.close()
    for e in ENGINES:
        k.wait(e, "tensor", te2)
        k.wait(e, "scalar", sc2)
        k.wait(e, "vector", ve2)
        k.wait(e, "sync", sy2)

    emit_out_proj_dram(k, cm, ctx, W["l1a_woT"], W["l1a_bo"], otd, xt, xo, "l1aop")

    k.wait("sync", "sync", k.cnt["sync"])
    if not standalone:
        return _block_close(k, ctx)
    k.finish()
    ctx.close()
    in_names = ["xt"] + list(W.keys()) + cm.in_names
    in_names.remove("l1a_qd") if "l1a_qd" in in_names else None
    return nc, in_names


def prep_weights_l1a(inp):
    return {
        "l1a_wqT": np.ascontiguousarray(inp["L1_attn_Wq"].T),
        "l1a_wkT": np.ascontiguousarray(inp["L1_attn_Wk"].T),
        "l1a_wvT": np.ascontiguousarray(inp["L1_attn_Wv"].T),
        "l1a_woT": np.ascontiguousarray(inp["L1_attn_Wo"].T),
        "l1a_bq": inp["L1_attn_bq"].reshape(1, D).astype(np.float32),
        "l1a_bk": inp["L1_attn_bk"].reshape(1, D).astype(np.float32),
        "l1a_bv": inp["L1_attn_bv"].reshape(1, D).astype(np.float32),
        "l1a_bo": inp["L1_attn_bo"].reshape(1, D).astype(np.float32),
    }



def build_L3a(env=None):
    nc, k, ctx, cm, xt, xo, standalone = _block_env(env)
    W = declare_w(nc, {
        "l3a_wq": (D, D), "l3a_wk": (D, D), "l3a_wv": (D, D), "l3a_wo": (D, D),
        "l3a_nullkpad": (D, 128), "l3a_nullvrow": (1, 16 * 65),
        "l3a_ls": (1, 16),
    })
    qd = nc.dram_tensor("l3a_qd", [D, T], BF16, kind="Internal").ap()
    kd = nc.dram_tensor("l3a_kd", [D, T + 128], BF16, kind="Internal").ap()
    otd = nc.dram_tensor("l3a_otd", [D, T], BF16, kind="Internal").ap()
    mqd = nc.dram_tensor("l3a_mqd", [16, T], BF16, kind="Internal").ap()

    sb = lambda n, s, dt=F32: ctx.enter_context(nc.sbuf_tensor(n, s, dt))

    lsr = sb("lsr", [1, 16])
    crow = sb("crow", [1, 16])
    c_all = sb("c_all", [128, 16])
    ctx0 = ExitStack()
    pC = ctx0.enter_context(nc.psum_tensor("l3a_pC", [128, 16], F32))
    lm = k.dma("sync", lsr[:, :], W["l3a_ls"][:, :])
    k.wait("scalar", "sync", lm)
    sm = k.op("scalar", lambda e: e.activation(crow[:, :], lsr[:, :], AF.Exp),
              inc=1)
    k.wait("vector", "scalar", sm)
    vm = k.op("vector", lambda e: e.tensor_scalar_mul(crow[:, :], crow[:, :],
                                                      0.125), inc=1)
    k.wait("tensor", "vector", vm)
    tm0 = k.op("tensor", lambda e: e.matmul(pC[:, :], cm.ones128f[:, :],
                                            crow[:, :], start=True, stop=True),
               inc=1)
    k.wait("vector", "tensor", tm0)
    cam = k.op("vector", lambda e: e.tensor_copy(c_all[:, :], pC[:, :]), inc=1)
    ctx0.close()

    ctxA = ExitStack()
    sbA = lambda n, s, dt=F32: ctxA.enter_context(
        nc.sbuf_tensor(n, s, dt, side="right"))
    xnb = sbA("xnb", [128, NF, T], BF16)
    k.wait("tensor", "vector", cam)
    ln_mark = emit_ln(k, cm, ctx, xt, xnb, "ln3a")

    vtok = sb("vtok", [128, NTT + 1, 16, 65], BF16)
    wqb = sbA("wqb", [128, NF, D], BF16)
    wkb = sbA("wkb", [128, NF, D], BF16)
    wvb = sbA("wvb", [128, NF, D], BF16)
    spill = sbA("spill", [128, 4, TC], BF16)
    k.dma_batch("gpsimd", [
        (wb_[:, kc, :], W[wn][kc * 128:(kc + 1) * 128, :])
        for kc in range(NF)
        for wb_, wn in ((wqb, "l3a_wq"), (wkb, "l3a_wk"), (wvb, "l3a_wv"))
    ] + [
        (kd[:, T:T + 128], W["l3a_nullkpad"][:, :]),
    ])
    nvm = k.op("vector", lambda e: e.memset(vtok[:, NTT, :, :], 0.0), inc=1)
    k.wait("gpsimd", "vector", nvm)
    wm = k.dma("gpsimd", vtok[0:1, NTT, :, :],
               W["l3a_nullvrow"][:, :].rearrange("o (h e) -> o h e", h=16))
    rot = PsumRot(k, ctxA, "l3a", 2)
    k.wait("tensor", "gpsimd", wm)
    k.wait("tensor", "vector", ln_mark)
    k.wait("tensor", "vector", cam)

    spill_pend = [None] * 4
    spill_call = [0]

    def spill_evac(dst_dram, scaled):
        def evac(oc, tc, p, tem, release):
            i = spill_call[0] % 4
            spill_call[0] += 1
            k.wait("scalar", "sync", spill_pend[i])
            k.wait("scalar", "tensor", tem)
            if scaled:
                k.op("scalar", lambda e, pp=p, o=spill[0:64, i, :],
                     s=c_all[0:64, 2 * oc:2 * oc + 1]:
                     e.activation(o, pp[0:64, :], AF.Copy, scale=s), inc=1)
                em = k.op("scalar", lambda e, pp=p, o=spill[64:128, i, :],
                          s=c_all[64:128, 2 * oc + 1:2 * oc + 2]:
                          e.activation(o, pp[64:128, :], AF.Copy, scale=s), inc=1)
            else:
                em = k.op("scalar", lambda e, pp=p, o=spill[:, i, :]:
                          e.activation(o, pp[:, :], AF.Copy), inc=1)
            k.wait("sync", "scalar", em)
            dmm = k.dma("sync", dst_dram[oc * 128:(oc + 1) * 128,
                                         tc * TC:(tc + 1) * TC], spill[:, i, :])
            spill_pend[i] = dmm
            release(("scalar", em))
        return evac

    emit_proj_evac(k, cm, rot, wqb, xnb, None, spill_evac(qd, True))
    emit_proj_evac(k, cm, rot, wkb, xnb, None, spill_evac(kd, False))

    pV = ctxA.enter_context(nc.psum_tensor("l3a_pV", [128, D], F32))
    pv_pend = [None]
    for tt in range(NTT):
        t0 = tt * 128
        k.wait("tensor", "vector", pv_pend[0])
        for kc in range(NF):
            for half in range(2):
                k.op("tensor", lambda e, l=xnb[:, kc, t0:t0 + 128],
                     r=wvb[:, kc, half * 512:(half + 1) * 512],
                     pp=pV[:, bass.ts(half, 512)], first=(kc == 0),
                     last=(kc == NF - 1):
                     e.matmul(pp, l, r, start=first, stop=last),
                     inc=(1 if (kc == NF - 1 and half == 1) else 0))
        tmk = k.cnt["tensor"]
        k.wait("vector", "tensor", tmk)
        k.op("vector", lambda e, o=vtok[:, tt, :, 0:64],
             a=pV[:].rearrange("p (h e) -> p h e", h=16):
             e.tensor_copy(o, a), inc=1)
        k.op("vector", lambda e, o=vtok[:, tt, :, 64:65]:
             e.memset(o, 1.0), inc=1)
        pv_pend[0] = k.cnt["vector"]

    teA, scA, veA, syA, gpA = (k.cnt["tensor"], k.cnt["scalar"],
                               k.cnt["vector"], k.cnt["sync"], k.cnt["gpsimd"])
    ctxA.close()
    for e in ENGINES:
        k.wait(e, "tensor", teA)
        k.wait(e, "scalar", scA)
        k.wait(e, "vector", veA)
        k.wait(e, "sync", syA)

    ctxB = ExitStack()
    sbB = lambda n, s, dt=F32: ctxB.enter_context(
        nc.sbuf_tensor(n, s, dt, side="right"))
    qh = sbB("qh", [65, 2, T], BF16)
    kh = sbB("kh", [65, 2, T + 128], BF16)
    mqc = sbB("mqc", [128, NTT])
    mqt = sbB("mqt", [128, 12])
    mqtr = sbB("mqtr", [32, 128], BF16)
    util = sbB("util", [128, 2, 2 * TC], BF16)
    rrow = sbB("rrow", [1, TC])
    rrowb = sbB("rrowb", [1, TC], BF16)
    repb = sbB("repb", [64, 2, TC], BF16)
    oT = sbB("oT", [64, 2, TC], BF16)
    pD = [ctxB.enter_context(nc.psum_tensor(f"l3a_pD{i}", [128, 2 * TC], F32))
          for i in range(2)]
    pN = [ctxB.enter_context(nc.psum_tensor(f"l3a_pN{i}", [65, TC], F32))
          for i in range(2)]
    pRep = ctxB.enter_context(nc.psum_tensor("l3a_pRep", [64, TC], F32))
    pMq = ctxB.enter_context(nc.psum_tensor("l3a_pMq", [32, 128], F32))

    head_pend = [None, None]
    pd_pend = [None, None]
    st_pend = [None] * 4
    pn_pend = [None, None]
    rep_pend = [None]
    o_pend = [None, None]
    mq_pend = [None]

    vkm = k.op("vector", lambda e: e.memset(kh[64:65, :, :], 1.0), inc=1)
    k.wait("tensor", "vector", vkm)
    k.wait("gpsimd", "vector", vkm)
    for h in range(16):
        hb = h % 2
        k.wait("gpsimd", "tensor", head_pend[hb])
        dmq = k.dma("gpsimd", qh[0:64, hb, :], qd[64 * h:64 * h + 64, :])
        dmk = k.dma("gpsimd", kh[0:64, hb, :], kd[64 * h:64 * h + 64, :])
        k.wait("tensor", "gpsimd", dmk)
        k.wait("tensor", "vector", mq_pend[0])
        k.waitp("tensor", pd_pend[0])
        k.waitp("tensor", pd_pend[1])
        for qt in range(NTT):
            nkc = qt // 4 + 1
            for kci in range(nkc + 1):
                sl = kci % 4
                pp = pD[sl // 2][:, (sl % 2) * TC:(sl % 2) * TC + TC]
                k.waitp("tensor", st_pend[sl])
                if kci < nkc:
                    src_lo = kci * 512
                    wdt = 512
                else:
                    src_lo = T
                    wdt = 128
                k.op("tensor", lambda e, l=qh[0:64, hb, qt * 128:(qt + 1) * 128],
                     r=kh[0:64, hb, src_lo:src_lo + wdt], ppp=pp[:, 0:wdt]:
                     e.matmul(ppp, l, r, start=True, stop=True), inc=1)
                tms = k.cnt["tensor"]
                k.wait("vector", "tensor", tms)
                vlast = k.op("vector", lambda e, ppp=pp[:, 0:wdt],
                             o=mqt[:, kci:kci + 1]:
                             e.reduce_max(o, ppp, axis=AX.X), inc=1)
                st_pend[sl] = ("vector", vlast)
            k.op("vector", lambda e, o=mqc[:, qt:qt + 1],
                 a=mqt[:, 0:nkc + 1]:
                 e.reduce_max(o, a, axis=AX.X, negate=True), inc=1)
        k.wait("tensor", "vector", k.cnt["vector"])
        tmt = k.op("tensor", lambda e: e.transpose(pMq[:, :], mqc[:, :],
                                                   cm.identf[:, :]), inc=1)
        k.wait("scalar", "tensor", tmt)
        smt = k.op("scalar", lambda e: e.activation(mqtr[:, :], pMq[:, :],
                                                    AF.Copy), inc=1)
        k.wait("sync", "scalar", smt)
        k.dma("sync",
              mqd[h:h + 1, :].rearrange("o (a b) -> (o a) b", b=128),
              mqtr[:, :])
        mrm = k.dma("sync", qh[64:65, hb, :], mqd[h:h + 1, :])
        k.wait("tensor", "sync", mrm)

        for qc in range(NT):
            ni = qc % 2
            if pn_pend[ni] is not None:
                k.wait("tensor", pn_pend[ni][0], pn_pend[ni][1])
            kts = list(range((qc + 1) * 4)) + [NTT]
            nkp = (len(kts) + 1) // 2
            s_mark = {}

            def emit_s(kp, qc=qc, hb=hb):
                di = kp % 2
                k.waitp("tensor", pd_pend[di])
                subs = kts[2 * kp:2 * kp + 2]
                for si, kt in enumerate(subs):
                    klo = kt * 128 if kt < NTT else T
                    masks = []
                    for qsub in range(4):
                        qt = qc * 4 + qsub
                        if kt != NTT and kt > qt:
                            masks.append((qsub, cm.ones128, cm.negrow))
                        elif kt != NTT and kt == qt:
                            masks.append((qsub, cm.ident, cm.maskkgt))
                    k.op("tensor", lambda e, l=kh[:, hb, klo:klo + 128],
                         r=qh[:, hb, qc * TC:(qc + 1) * TC],
                         ppp=pD[di][:, bass.ts(si, TC)], last=(not masks):
                         e.matmul(ppp, l, r, start=True, stop=last), inc=1)
                    for mi, (qsub, ml, mr) in enumerate(masks):
                        k.op("tensor", lambda e, ml=ml, mr=mr,
                             ppp=pD[di][:, si * TC + qsub * 128:
                                        si * TC + qsub * 128 + 128],
                             last=(mi == len(masks) - 1):
                             e.matmul(ppp, ml[:, :], mr[:, :], start=False,
                                      stop=last),
                             inc=(1 if mi == len(masks) - 1 else 0))
                s_mark[kp] = k.cnt["tensor"]

            first_u = True
            emit_s(0)
            for kp in range(nkp):
                di = kp % 2
                if kp + 1 < nkp:
                    emit_s(kp + 1)
                subs = kts[2 * kp:2 * kp + 2]
                k.wait("scalar", "tensor", s_mark[kp])
                wdt2 = len(subs) * TC
                um = k.op("scalar", lambda e, a=pD[di][:, 0:wdt2],
                          o=util[:, di, 0:wdt2]:
                          e.activation(o, a, AF.Exp), inc=1)
                pd_pend[di] = ("scalar", um)
                k.wait("tensor", "scalar", um)
                for si, kt in enumerate(subs):
                    k.op("tensor", lambda e, l=vtok[:, kt, h, :],
                         r=util[:, di, bass.ts(si, TC)], pp=pN[ni],
                         first=first_u,
                         last=(kp == nkp - 1 and si == len(subs) - 1):
                         e.matmul(pp[:, :], l, r, start=first, stop=last),
                         inc=(1 if si == len(subs) - 1 else 0))
                    first_u = False
            tmn = k.cnt["tensor"]
            k.wait("vector", "tensor", tmn)
            k.wait("vector", "sync", o_pend[ni])
            k.op("vector", lambda e, pp=pN[ni]:
                 e.reciprocal(rrow[:, :], pp[64:65, :]), inc=1)
            vm1 = k.op("vector", lambda e:
                       e.tensor_copy(rrowb[:, :], rrow[:, :]), inc=1)
            k.wait("tensor", "vector", vm1)
            k.wait("tensor", "scalar", rep_pend[0])
            tmr = k.op("tensor", lambda e:
                       e.matmul(pRep[:, :], cm.ones128[0:1, 0:64], rrowb[:, :],
                                start=True, stop=True), inc=1)
            k.wait("scalar", "tensor", tmr)
            smr = k.op("scalar", lambda e, o=repb[:, ni, :]:
                       e.activation(o, pRep[:, :], AF.Copy), inc=1)
            rep_pend[0] = smr
            k.wait("vector", "scalar", smr)
            vm2 = k.op("vector", lambda e, pp=pN[ni], o=oT[:, ni, :],
                       rr=repb[:, ni, :]:
                       e.tensor_tensor(o, pp[0:64, :], rr, ALU.mult), inc=1)
            pn_pend[ni] = ("vector", vm2)
            k.wait("sync", "vector", vm2)
            o_pend[ni] = k.dma("sync", otd[64 * h:64 * h + 64,
                                           qc * TC:(qc + 1) * TC], oT[:, ni, :])
        head_pend[hb] = k.cnt["tensor"]
        mq_pend[0] = k.cnt["vector"]

    te2, sc2, ve2, sy2 = (k.cnt["tensor"], k.cnt["scalar"], k.cnt["vector"],
                          k.cnt["sync"])
    ctxB.close()
    for e in ENGINES:
        k.wait(e, "tensor", te2)
        k.wait(e, "scalar", sc2)
        k.wait(e, "vector", ve2)
        k.wait(e, "sync", sy2)

    emit_out_proj_dram(k, cm, ctx, W["l3a_wo"], None, otd, xt, xo, "l3aop")
    k.wait("sync", "sync", k.cnt["sync"])
    if not standalone:
        return _block_close(k, ctx)
    k.finish()
    ctx.close()
    in_names = ["xt"] + list(W.keys()) + cm.in_names
    return nc, in_names


def prep_weights_l3a(inp):
    nullk = np.asarray(inp["L3_attn_null_k"]).reshape(16, 64)
    nullv = np.asarray(inp["L3_attn_null_v"]).reshape(16, 64)
    nkpad = np.zeros((D, 128), dtype=np.float32)
    nkpad[:, 0] = nullk.reshape(D)
    nvrow = np.zeros((1, 16 * 65), dtype=np.float32)
    for h in range(16):
        nvrow[0, h * 65:h * 65 + 64] = nullv[h]
        nvrow[0, h * 65 + 64] = 1.0
    return {
        "l3a_wq": np.ascontiguousarray(inp["L3_attn_Wq"]),
        "l3a_wk": np.ascontiguousarray(inp["L3_attn_Wk"]),
        "l3a_wv": np.ascontiguousarray(inp["L3_attn_Wv"]),
        "l3a_wo": np.ascontiguousarray(inp["L3_attn_Wo"]),
        "l3a_nullkpad": nkpad,
        "l3a_nullvrow": nvrow,
        "l3a_ls": np.asarray(inp["L3_attn_logit_scale"]).reshape(1, 16).astype(np.float32),
    }



TWOPI = 6.283185307179586
INV2PI = 0.15915494309189535
RMAGIC = 12582912.0


def build_L2a(env=None):
    nc, k, ctx, cm, xt, xo, standalone = _block_env(env)
    W = declare_w(nc, {
        "l2a_wqT": (D, D), "l2a_wkT": (D, D), "l2a_wvT": (D, D),
        "l2a_woT": (D, D), "l2a_rfT": (64, 256),
    })
    qd = nc.dram_tensor("l2a_qd", [D, T], BF16, kind="Internal").ap()
    kd = nc.dram_tensor("l2a_kd", [D, T], BF16, kind="Internal").ap()
    otd = nc.dram_tensor("l2a_otd", [D, T], BF16, kind="Internal").ap()

    sb = lambda n, s, dt=F32: ctx.enter_context(nc.sbuf_tensor(n, s, dt))

    ctxA = ExitStack()
    sbA = lambda n, s, dt=F32: ctxA.enter_context(
        nc.sbuf_tensor(n, s, dt, side="right"))
    xnb = sbA("xnb", [128, NF, T], BF16)
    ln_mark = emit_ln(k, cm, ctx, xt, xnb, "ln2a")

    vtok = sb("vtok", [128, NTT, 16, 65], BF16)
    rfgq = sb("rfgq", [65, 4, 128], BF16)
    rfgk = sb("rfgk", [65, 512], BF16)
    wqb = sbA("wqb", [128, NF, D], BF16)
    wkb = sbA("wkb", [128, NF, D], BF16)
    wvb = sbA("wvb", [128, NF, D], BF16)
    spill = sbA("spill", [128, 4, TC], BF16)
    sqt = sbA("sqt", [128, D], BF16)
    k2c = sbA("k2c", [128, 16])
    kscol = sbA("kscol", [128, 16])
    wm = k.dma_batch("gpsimd", [
        (wb_[:, kc, :], W[wn][kc * 128:(kc + 1) * 128, :])
        for kc in range(NF)
        for wb_, wn in ((wqb, "l2a_wqT"), (wkb, "l2a_wkT"), (wvb, "l2a_wvT"))])
    rff = ctxA.enter_context(nc.sbuf_tensor("l2a_rff", [64, 256], F32,
                                             side="right"))
    wms = k.dma("sync", rff[:, :], W["l2a_rfT"][:, :])
    k.wait("vector", "sync", wms)
    for v in range(4):
        mcv = v % 2
        k.op("vector", lambda e, o=rfgq[0:64, v, :],
             a=rff[:, mcv * 128:(mcv + 1) * 128]:
             e.tensor_scalar_mul(o, a, -INV2PI), inc=1)
        k.op("vector", lambda e, o=rfgq[64:65, v, :], s=(0.25 if v < 2
                                                         else 0.5):
             e.memset(o, s), inc=1)
    k.op("vector", lambda e, o=rfgk[0:64, 0:256]:
         e.tensor_scalar_mul(o, rff[:, :], -INV2PI), inc=1)
    k.op("vector", lambda e, o=rfgk[0:64, 256:512]:
         e.tensor_scalar_mul(o, rff[:, :], -INV2PI), inc=1)
    k.op("vector", lambda e, o=rfgk[64:65, 0:256]: e.memset(o, 0.25), inc=1)
    wmv = k.op("vector", lambda e, o=rfgk[64:65, 256:512]:
               e.memset(o, 0.5), inc=1)
    k.wait("tensor", "vector", wmv)
    rot = PsumRot(k, ctxA, "l2a", 2)
    k.wait("tensor", "gpsimd", wm)
    k.wait("tensor", "vector", ln_mark)

    spill_pend = [None] * 4
    spill_call = [0]

    def spill_evac(dst_dram):
        def evac(oc, tc, p, tem, release):
            i = spill_call[0] % 4
            spill_call[0] += 1
            k.wait("scalar", "sync", spill_pend[i])
            k.wait("scalar", "tensor", tem)
            em = k.op("scalar", lambda e, pp=p, o=spill[:, i, :]:
                      e.activation(o, pp[:, :], AF.Copy), inc=1)
            k.wait("sync", "scalar", em)
            dmm = k.dma("sync", dst_dram[oc * 128:(oc + 1) * 128,
                                         tc * TC:(tc + 1) * TC], spill[:, i, :])
            spill_pend[i] = dmm
            release(("scalar", em))
        return evac

    emit_proj_evac(k, cm, rot, wqb, xnb, None, spill_evac(qd))
    emit_proj_evac(k, cm, rot, wkb, xnb, None, spill_evac(kd))

    pV = ctxA.enter_context(nc.psum_tensor("l2a_pV", [128, D], F32))
    pKt = ctxA.enter_context(nc.psum_tensor("l2a_pKt", [128, D], F32))
    pv_pend = [None]
    pk_pend = [None]
    for tt in range(NTT):
        t0 = tt * 128
        k.wait("tensor", "vector", pv_pend[0])
        for kc in range(NF):
            for half in range(2):
                k.op("tensor", lambda e, l=xnb[:, kc, t0:t0 + 128],
                     r=wvb[:, kc, half * 512:(half + 1) * 512],
                     pp=pV[:, bass.ts(half, 512)], first=(kc == 0),
                     last=(kc == NF - 1):
                     e.matmul(pp, l, r, start=first, stop=last), inc=0)
        k.waitp("tensor", pk_pend[0])
        for kc in range(NF):
            for half in range(2):
                k.op("tensor", lambda e, l=xnb[:, kc, t0:t0 + 128],
                     r=wkb[:, kc, half * 512:(half + 1) * 512],
                     pp=pKt[:, bass.ts(half, 512)], first=(kc == 0),
                     last=(kc == NF - 1):
                     e.matmul(pp, l, r, start=first, stop=last),
                     inc=(1 if (kc == NF - 1 and half == 1) else 0))
        tmk = k.cnt["tensor"]
        k.wait("scalar", "tensor", tmk)
        sm = k.op("scalar", lambda e: e.activation(sqt[:, :], pKt[:, :],
                                                   AF.Square), inc=1)
        pk_pend[0] = ("scalar", sm)
        k.wait("vector", "scalar", sm)
        k.op("vector", lambda e:
             e.reduce_sum(k2c[:, :], sqt[:].rearrange("p (h d) -> p h d", h=16),
                          axis=AX.X), inc=1)
        vm0 = k.cnt["vector"]
        k.wait("scalar", "vector", vm0)
        sm2 = k.op("scalar", lambda e:
                   e.activation(kscol[:, :], k2c[:, :], AF.Exp, scale=-0.5,
                                bias=cm.phicol[:, 1:2]), inc=1)
        k.wait("vector", "scalar", sm2)
        for h in range(16):
            k.op("vector", lambda e, o=vtok[:, tt, h, 0:64],
                 a=pV[:, 64 * h:64 * h + 64], s=kscol[:, h:h + 1]:
                 e.tensor_scalar_mul(o, a, s), inc=0)
            k.op("vector", lambda e, o=vtok[:, tt, h, 64:65],
                 a=kscol[:, h:h + 1]:
                 e.tensor_copy(o, a), inc=(1 if h == 15 else 0))
        pv_pend[0] = k.cnt["vector"]

    teA, scA, veA, syA = (k.cnt["tensor"], k.cnt["scalar"], k.cnt["vector"],
                          k.cnt["sync"])
    ctxA.close()
    for e in ENGINES:
        k.wait(e, "tensor", teA)
        k.wait(e, "scalar", scA)
        k.wait(e, "vector", veA)
        k.wait(e, "sync", syA)

    ctxB = ExitStack()
    sbB = lambda n, s, dt=F32: ctxB.enter_context(
        nc.sbuf_tensor(n, s, dt, side="right"))
    qh = sbB("qh", [65, 2, T], BF16)
    kh = sbB("kh", [65, 2, T], BF16)
    sqh = sbB("sqh", [64, T], BF16)
    sintb = sbB("sintb", [128, 2, TC], BF16)
    q2row = sbB("q2row", [1, T])
    qscrow = sbB("qscrow", [1, T], BF16)
    phiq = sbB("phiq", [128, 4, T], BF16)
    ftile = sbB("ftile", [128, 4, TC])
    fint = sbB("fint", [128, TC], I32)
    KVb = sbB("KVb", [128, 4, 65], BF16)
    phik = sbB("phik", [128, 2, 512], BF16)
    rrow = sbB("rrow", [1, TC])
    rrowb = sbB("rrowb", [1, TC], BF16)
    repb = sbB("repb", [64, 2, TC], BF16)
    oT = sbB("oT", [64, 2, TC], BF16)
    pP = [ctxB.enter_context(nc.psum_tensor(f"l2a_pP{i}", [128, TC], F32))
          for i in range(2)]
    pQ2 = ctxB.enter_context(nc.psum_tensor("l2a_pQ2", [1, TC], F32))
    pRp = ctxB.enter_context(nc.psum_tensor("l2a_pRp", [128, TC], F32))
    pKV = ctxB.enter_context(nc.psum_tensor("l2a_pKV", [65, TC], F32))
    pKVt = ctxB.enter_context(nc.psum_tensor("l2a_pKVt", [128, 65], BF16))
    pN = [ctxB.enter_context(nc.psum_tensor(f"l2a_pN{i}", [65, TC], F32))
          for i in range(2)]
    pRep = pRp

    head_pend = [None, None]
    q2_pend = [None]
    pp_pend = [None, None]
    prp_pend = [None]
    phik_pend = [None, None]
    pn_pend = [None, None]
    rep_pend = [None]
    o_pend = [None, None]
    kvb_pend = [None]
    rr_pend = [None]

    ks_last = [None, None]
    ds_last = [None, None]
    fs_call = [0]

    def fs_head(pp, src_slice, tem, w=TC):
        si = fs_call[0] % 2
        fs_call[0] += 1
        k.wait("scalar", "tensor", tem)
        k.waitp("scalar", ks_last[si])
        sm1 = k.op("scalar", lambda e, a=src_slice, o=ftile[:, si, 0:w]:
                   e.activation(o, a, AF.Copy, bias=RMAGIC), inc=1)
        k.wait("vector", "scalar", sm1)
        k.op("vector", lambda e, o=ftile[:, si, 0:w], a=ftile[:, si, 0:w]:
             e.tensor_scalar_add(o, a, -RMAGIC), inc=1)
        k.waitp("vector", ds_last[si])
        subm = k.op("vector", lambda e, o=ftile[:, 2 + si, 0:w], a=src_slice,
                    b=ftile[:, si, 0:w]:
                    e.tensor_tensor(o, a, b, ALU.subtract), inc=1)
        ks_last[si] = ("vector", subm)
        vm = k.op("vector", lambda e, o=ftile[:, 2 + si, 0:w]:
                  e.tensor_scalar(o, o, 0.49999997, -0.49999997, ALU.min,
                                  ALU.max), inc=1)
        return si, ("vector", subm), vm

    def fs_tail(si, vm, dst, w=TC, dst_pend=None):
        k.wait("scalar", "vector", vm)
        k.waitp("scalar", dst_pend)
        sm = k.op("scalar", lambda e, o=dst, a=ftile[:, 2 + si, 0:w]:
                  e.activation(o, a, AF.Sin, scale=TWOPI), inc=1)
        ds_last[si] = ("scalar", sm)
        return sm

    vam = k.op("vector", lambda e: e.memset(qh[64:65, :, :], 1.0), inc=1)
    vam = k.op("vector", lambda e: e.memset(kh[64:65, :, :], 1.0), inc=1)
    k.wait("tensor", "vector", vam)
    k.wait("gpsimd", "vector", vam)
    for h in range(16):
        hb = h % 2
        k.wait("gpsimd", "tensor", head_pend[hb])
        dmq = k.dma("gpsimd", qh[0:64, hb, :], qd[64 * h:64 * h + 64, :])
        dmk = k.dma("gpsimd", kh[0:64, hb, :], kd[64 * h:64 * h + 64, :])
        k.wait("scalar", "gpsimd", dmq)
        sm = k.op("scalar", lambda e, a=qh[0:64, hb, :]:
                  e.activation(sqh[:, :], a, AF.Square), inc=1)
        k.wait("tensor", "scalar", sm)
        k.wait("tensor", "vector", q2_pend[0])
        for tc in range(NT):
            k.op("tensor", lambda e, a=sqh[:, tc * TC:(tc + 1) * TC]:
                 e.matmul(pQ2[:, :], cm.onecol[0:64, :], a, start=True,
                          stop=True), inc=1)
            tmq = k.cnt["tensor"]
            k.wait("vector", "tensor", tmq)
            vq = k.op("vector", lambda e, o=q2row[0:1, tc * TC:(tc + 1) * TC]:
                      e.tensor_copy(o, pQ2[:, :]), inc=1)
            if tc < NT - 1:
                k.wait("tensor", "vector", vq)
        k.wait("scalar", "vector", k.cnt["vector"])
        sm = k.op("scalar", lambda e:
                  e.activation(qscrow[:, :], q2row[:, :], AF.Exp, scale=-0.5,
                               bias=cm.phicol[0:1, 1:2]), inc=1)
        q2_pend[0] = k.cnt["vector"]

        st_last = [None, None]
        tail = [None]

        def flush_tail():
            if tail[0] is None:
                return
            si, vm, tcp, vp, trp = tail[0]
            tail[0] = None
            smx = fs_tail(si, vm, sintb[:, si, :], dst_pend=st_last[si])
            k.wait("vector", "scalar", smx)
            k.wait("vector", "tensor", trp)
            vmx = k.op("vector", lambda e, a=sintb[:, si, :],
                       o=phiq[:, vp, tcp * TC:(tcp + 1) * TC]:
                       e.tensor_tensor(o, a, pRp[:, :], ALU.mult), inc=1)
            st_last[si] = ("vector", vmx)

        trp = None
        for tc in range(NT):
            flush_tail()
            prp_pend[0] = ("vector", k.cnt["vector"])
            k.wait("tensor", "scalar", sm)
            k.waitp("tensor", prp_pend[0])
            k.waitp("tensor", rep_pend[0])
            trp = k.op("tensor", lambda e, r=qscrow[0:1, tc * TC:(tc + 1) * TC]:
                       e.matmul(pRp[:, :], cm.ones128[:, :], r, start=True,
                                stop=True), inc=1)
            for v in range(4):
                pi2 = v % 2
                k.waitp("tensor", pp_pend[pi2])
                tem = k.op("tensor", lambda e, l=rfgq[:, v, :],
                           r=qh[:, hb, tc * TC:(tc + 1) * TC], pp=pP[pi2]:
                           e.matmul(pp[:, :], l, r, start=True, stop=True),
                           inc=1)
                si, rel, vm = fs_head(pP[pi2], pP[pi2][:, :], tem)
                pp_pend[pi2] = rel
                flush_tail()
                tail[0] = (si, vm, tc, v, trp)
        flush_tail()
        prp_pend[0] = ("vector", k.cnt["vector"])

        k.wait("tensor", "gpsimd", dmk)
        k.waitp("tensor", kvb_pend[0])
        ktail = [None]

        def kflush():
            if ktail[0] is None:
                return
            si, vm, ttp = ktail[0]
            ktail[0] = None
            smx = fs_tail(si, vm, phik[:, si, :], dst_pend=phik_pend[si])
            k.wait("tensor", "scalar", smx)
            um = k.op("tensor", lambda e, l=vtok[:, ttp, h, :],
                      r=phik[:, si, :], pp=pKV, first=(ttp == 0),
                      last=(ttp == NTT - 1):
                      e.matmul(pp[:, 0:512], l, r, start=first, stop=last),
                      inc=1)
            phik_pend[si] = ("tensor", um)

        for tt in range(NTT):
            pi2 = tt % 2
            k.waitp("tensor", pp_pend[pi2])
            tem = k.op("tensor", lambda e, l=kh[:, hb, tt * 128:(tt + 1) * 128],
                       r=rfgk[:, :], pp=pP[pi2]:
                       e.matmul(pp[:, :], l, r, start=True, stop=True),
                       inc=1)
            si, rel, vm = fs_head(pP[pi2], pP[pi2][:, :], tem)
            pp_pend[pi2] = rel
            kflush()
            ktail[0] = (si, vm, tt)
        kflush()
        tkv = k.cnt["tensor"]
        k.wait("scalar", "tensor", tkv)
        smkv = k.op("scalar", lambda e: e.activation(phik[0:65, 0, :], pKV[:, 0:512],
                                                     AF.Copy), inc=1)
        k.wait("tensor", "scalar", smkv)
        for fc4 in range(4):
            k.op("tensor", lambda e, a=phik[0:65, 0, fc4 * 128:(fc4 + 1) * 128]:
                 e.transpose(pKVt[:, 0:65], a, cm.ident[0:65, 0:65]), inc=1)
            tmt = k.cnt["tensor"]
            k.wait("vector", "tensor", tmt)
            vkv = k.op("vector", lambda e, o=KVb[:, fc4, :]:
                       e.tensor_copy(o, pKVt[0:128, 0:65]), inc=1)
            if fc4 < 3:
                k.wait("tensor", "vector", vkv)
        kvb_pend[0] = ("vector", k.cnt["vector"])
        k.wait("tensor", "vector", k.cnt["vector"])

        for qc in range(NT):
            ni = qc % 2
            k.waitp("tensor", pn_pend[ni])
            for fc4 in range(4):
                k.op("tensor", lambda e, l=KVb[:, fc4, :],
                     r=phiq[:, fc4, qc * TC:(qc + 1) * TC], pp=pN[ni],
                     first=(fc4 == 0), last=(fc4 == 3):
                     e.matmul(pp[:, :], l, r, start=first, stop=last),
                     inc=(1 if fc4 == 3 else 0))
            tmn = k.cnt["tensor"]
            k.wait("vector", "tensor", tmn)
            k.wait("vector", "sync", o_pend[ni])
            k.waitp("vector", rr_pend[0])
            vmx = k.op("vector", lambda e, pp=pN[ni]:
                       e.tensor_scalar_max(rrow[:, :], pp[64:65, :], 1e-6),
                       inc=1)
            k.wait("scalar", "vector", vmx)
            k.wait("scalar", "tensor", tmn)
            k.op("scalar", lambda e:
                 e.activation(rrow[:, :], rrow[:, :], AF.Ln), inc=1)
            vm1 = k.op("scalar", lambda e:
                       e.activation(rrowb[:, :], rrow[:, :], AF.Exp,
                                    scale=-1.0), inc=1)
            rr_pend[0] = ("scalar", vm1)
            k.wait("tensor", "scalar", vm1)
            k.waitp("tensor", rep_pend[0])
            k.waitp("tensor", prp_pend[0])
            tmr = k.op("tensor", lambda e:
                       e.matmul(pRep[0:64, :], cm.ones128[0:1, 0:64], rrowb[:, :],
                                start=True, stop=True), inc=1)
            k.wait("scalar", "tensor", tmr)
            smr = k.op("scalar", lambda e, o=repb[:, ni, :]:
                       e.activation(o, pRep[0:64, :], AF.Copy), inc=1)
            rep_pend[0] = ("scalar", smr)
            prp_pend[0] = ("scalar", smr)
            k.wait("vector", "scalar", smr)
            vm2 = k.op("vector", lambda e, pp=pN[ni], o=oT[:, ni, :],
                       rr=repb[:, ni, :]:
                       e.tensor_tensor(o, pp[0:64, :], rr, ALU.mult), inc=1)
            pn_pend[ni] = ("vector", vm2)
            k.wait("sync", "vector", vm2)
            o_pend[ni] = k.dma("sync", otd[64 * h:64 * h + 64,
                                           qc * TC:(qc + 1) * TC], oT[:, ni, :])
        head_pend[hb] = k.cnt["tensor"]

    te2, sc2, ve2, sy2 = (k.cnt["tensor"], k.cnt["scalar"], k.cnt["vector"],
                          k.cnt["sync"])
    ctxB.close()
    for e in ENGINES:
        k.wait(e, "tensor", te2)
        k.wait(e, "scalar", sc2)
        k.wait(e, "vector", ve2)
        k.wait(e, "sync", sy2)

    emit_out_proj_dram(k, cm, ctx, W["l2a_woT"], None, otd, xt, xo, "l2aop")
    k.wait("sync", "sync", k.cnt["sync"])
    if not standalone:
        return _block_close(k, ctx)
    k.finish()
    ctx.close()
    in_names = ["xt"] + list(W.keys()) + cm.in_names
    return nc, in_names


def prep_weights_l2a(inp):
    return {
        "l2a_wqT": np.ascontiguousarray(inp["L2_attn_wq"].T),
        "l2a_wkT": np.ascontiguousarray(inp["L2_attn_wk"].T),
        "l2a_wvT": np.ascontiguousarray(inp["L2_attn_wv"].T),
        "l2a_woT": np.ascontiguousarray(inp["L2_attn_wo"].T),
        "l2a_rfT": np.ascontiguousarray(inp["L2_attn_rf"].T),
    }



def build_L0a(env=None):
    nc, k, ctx, cm, xt, xo, standalone = _block_env(env)
    W = declare_w(nc, {
        "l0a_wq": (D, D), "l0a_wk": (D, D), "l0a_wv": (D, D), "l0a_wo": (D, D),
    })
    otd = nc.dram_tensor("l0a_otd", [D, T], BF16, kind="Internal").ap()
    sb = lambda n, s, dt=F32: ctx.enter_context(nc.sbuf_tensor(n, s, dt))

    ctxA = ExitStack()
    sbA = lambda n, s, dt=F32: ctxA.enter_context(
        nc.sbuf_tensor(n, s, dt, side="right"))
    xnb = sbA("xnb", [128, NF, T], BF16)
    ln_mark = emit_ln(k, cm, ctx, xt, xnb, "ln0a")

    qT = sb("l0a_qT", [128, NF, T], BF16)
    vkTb = sb("l0a_vkTb", [128, 8, 65], BF16)
    vkTacc = sb("l0a_vkTacc", [128, 8, 65])

    cosk = sbA("l0a_cosk", [128, NTT, 32], BF16)
    sink = sbA("l0a_sink", [128, NTT, 32], BF16)
    wqb = sbA("wqb", [128, NF, D], BF16)
    wkb = sbA("wkb", [128, NF, D], BF16)
    wvb = sbA("wvb", [128, NF, D], BF16)
    rope_f = sbA("rope_f", [128, 2, D])
    ktile = sbA("ktile", [128, 2, D], BF16)
    qtile = sbA("qtile", [128, 2, D], BF16)
    vaug = sbA("vaug", [128, 2, 16, 65], BF16)

    cm2 = k.dma_batch("gpsimd", [
        (wb_[:, kc, :], W[wn][kc * 128:(kc + 1) * 128, :])
        for kc in range(NF)
        for wb_, wn in ((wqb, "l0a_wq"), (wkb, "l0a_wk"), (wvb, "l0a_wv"))
    ] + [
        (cosk[:, :, :],
         cm.aps["c_l0cos"][:, :].rearrange("(a p) f -> p a f", p=128)),
        (sink[:, :, :],
         cm.aps["c_l0sin"][:, :].rearrange("(a p) f -> p a f", p=128)),
    ])
    k.wait("tensor", "gpsimd", cm2)
    k.wait("tensor", "vector", ln_mark)
    k.wait("vector", "gpsimd", cm2)

    pA = ctxA.enter_context(nc.psum_tensor("l0a_pA", [128, D], F32))
    pB = pA
    pVK = ctxA.enter_context(nc.psum_tensor("l0a_pVK", [128, 1024], F32))
    pT = ctxA.enter_context(nc.psum_tensor("l0a_pT", [128, 128], BF16))

    pa_pend = [None]
    pb_pend = pa_pend
    vz = k.op("vector", lambda e: e.memset(vkTacc[:, :, :], 0.0), inc=1)
    pt_pend = [None]
    ve_tiles = [None, None]

    def emit_rope(psrc, dst_tile, tt, relu_dst):
        cosb = bass.AP(cosk.ap().tensor, cosk[:, tt, :].offset,
                       [[NTT * 32, 128], [0, 16], [1, 32]])
        sinb = bass.AP(sink.ap().tensor, sink[:, tt, :].offset,
                       [[NTT * 32, 128], [0, 16], [1, 32]])
        x1 = bass.AP(psrc.ap().tensor, psrc.ap().offset,
                     [[psrc.shape[1], 128], [64, 16], [2, 32]])
        x2 = bass.AP(psrc.ap().tensor, psrc.ap().offset + 1,
                     [[psrc.shape[1], 128], [64, 16], [2, 32]])
        r3 = lambda i: rope_f[:, 0 if i == 2 else i, 0:512].rearrange(
            "p (h f) -> p h f", h=16)
        k.op("vector", lambda e, a=x1, b=cosb, o=r3(0):
             e.tensor_tensor(o, a, b, ALU.mult), inc=1)
        k.op("vector", lambda e, a=x2, b=sinb, o=r3(1):
             e.tensor_tensor(o, a, b, ALU.mult), inc=1)
        k.op("vector", lambda e, o=r3(2):
             e.tensor_sub(o, r3(0), r3(1)), inc=1)
        k.op("vector", lambda e, o=dst_tile[:].rearrange(
            "p (h q f) -> p (h q) f", h=16, q=2)[:, 0::2, :], a=r3(2):
             e.tensor_relu(o, a), inc=1)
        k.op("vector", lambda e, a=x1, b=sinb, o=r3(0):
             e.tensor_tensor(o, a, b, ALU.mult), inc=1)
        k.op("vector", lambda e, a=x2, b=cosb, o=r3(1):
             e.tensor_tensor(o, a, b, ALU.mult), inc=1)
        k.op("vector", lambda e, o=r3(2):
             e.tensor_add(o, r3(0), r3(1)), inc=1)
        m = k.op("vector", lambda e, o=dst_tile[:].rearrange(
            "p (h q f) -> p (h q) f", h=16, q=2)[:, 1::2, :], a=r3(2):
             e.tensor_relu(o, a), inc=1)
        return m

    for tt in range(NTT):
        t0 = tt * 128
        buf = tt % 2
        k.waitp("tensor", pa_pend[0])
        k.waitp("tensor", ve_tiles[buf])
        for kc in range(NF):
            for half in range(2):
                k.op("tensor", lambda e, l=xnb[:, kc, t0:t0 + 128],
                     r=wqb[:, kc, half * 512:(half + 1) * 512],
                     pp=pA[:, bass.ts(half, 512)], first=(kc == 0),
                     last=(kc == NF - 1):
                     e.matmul(pp, l, r, start=first, stop=last),
                     inc=(1 if (kc == NF - 1 and half == 1) else 0))
        tmq = k.cnt["tensor"]
        k.wait("vector", "tensor", tmq)
        k.waitp("vector", ve_tiles[buf])
        qm = emit_rope(pA, qtile[:, buf, :], tt, None)
        pa_pend[0] = ("vector", qm)
        k.wait("tensor", "vector", qm)
        for fc in range(NF):
            k.waitp("tensor", pt_pend[0])
            k.op("tensor", lambda e, a=qtile[:, buf, fc * 128:(fc + 1) * 128]:
                 e.transpose(pT[:, :], a, cm.ident[:, :]), inc=1)
            tmt = k.cnt["tensor"]
            k.wait("vector", "tensor", tmt)
            vmt = k.op("vector", lambda e, o=qT[:, fc, t0:t0 + 128]:
                       e.tensor_copy(o, pT[:, :]), inc=1)
            pt_pend[0] = ("vector", vmt)
        k.waitp("tensor", pb_pend[0])
        for kc in range(NF):
            for half in range(2):
                k.op("tensor", lambda e, l=xnb[:, kc, t0:t0 + 128],
                     r=wkb[:, kc, half * 512:(half + 1) * 512],
                     pp=pB[:, bass.ts(half, 512)], first=(kc == 0),
                     last=(kc == NF - 1):
                     e.matmul(pp, l, r, start=first, stop=last),
                     inc=(1 if (kc == NF - 1 and half == 1) else 0))
        tmk = k.cnt["tensor"]
        k.wait("vector", "tensor", tmk)
        km = emit_rope(pB, ktile[:, buf, :], tt, None)
        pb_pend[0] = ("vector", km)
        k.wait("tensor", "vector", km)
        for kc in range(NF):
            for half in range(2):
                k.op("tensor", lambda e, l=xnb[:, kc, t0:t0 + 128],
                     r=wvb[:, kc, half * 512:(half + 1) * 512],
                     pp=pB[:, bass.ts(half, 512)], first=(kc == 0),
                     last=(kc == NF - 1):
                     e.matmul(pp, l, r, start=first, stop=last),
                     inc=(1 if (kc == NF - 1 and half == 1) else 0))
        tmv = k.cnt["tensor"]
        k.wait("vector", "tensor", tmv)
        k.op("vector", lambda e, o=vaug[:, buf, :, 0:64],
             a=pB[:].rearrange("p (h v) -> p h v", h=16):
             e.tensor_copy(o, a), inc=1)
        vm = k.op("vector", lambda e, o=vaug[:, buf, :, 64:65]:
                  e.memset(o, 1.0), inc=1)
        pb_pend[0] = ("vector", vm)
        k.wait("tensor", "vector", vm)
        if tt == 0:
            k.wait("tensor", "vector", vz)
        for h in range(16):
            rb = 64 * (h % 2)
            k.op("tensor", lambda e, l=ktile[:, buf, 64 * h:64 * h + 64],
                 r=vaug[:, buf, h, :],
                 pp=pVK[rb:rb + 64, 128 * (h // 2):128 * (h // 2) + 65]:
                 e.matmul(pp, l, r, start=True, stop=True),
                 inc=(1 if h == 15 else 0))
        tvk = k.cnt["tensor"]
        k.wait("vector", "tensor", tvk)
        for h in range(16):
            rb = 64 * (h % 2)
            k.op("vector", lambda e, o=vkTacc[rb:rb + 64, h // 2, :],
                 a=pVK[rb:rb + 64, 128 * (h // 2):128 * (h // 2) + 65]:
                 e.tensor_tensor(o, o, a, ALU.add), inc=1)
        ve_tiles[buf] = ("vector", k.cnt["vector"])

    vkm = k.op("vector", lambda e:
               e.tensor_copy(vkTb[:, :, :], vkTacc[:, :, :]), inc=1)

    teA, scA, veA, syA = (k.cnt["tensor"], k.cnt["scalar"], k.cnt["vector"],
                          k.cnt["sync"])
    ctxA.close()
    for e in ENGINES:
        k.wait(e, "tensor", teA)
        k.wait(e, "scalar", scA)
        k.wait(e, "vector", veA)
        k.wait(e, "sync", syA)

    ctxB = ExitStack()
    sbB = lambda n, s, dt=F32: ctxB.enter_context(
        nc.sbuf_tensor(n, s, dt, side="right"))
    rrow = sbB("rrow", [1, TC])
    rrowb = sbB("rrowb", [1, TC], BF16)
    repb = sbB("repb", [64, 2, TC], BF16)
    oT = sbB("oT", [64, 2, TC], BF16)
    pN = [ctxB.enter_context(nc.psum_tensor(f"l0a_pN{i}", [65, TC], F32))
          for i in range(2)]
    pRep = ctxB.enter_context(nc.psum_tensor("l0a_pRep", [64, TC], F32))
    pn_pend = [None, None]
    rep_pend = [None]
    o_pend = [None, None]
    for h in range(16):
        fcq = h // 2
        rbase = 64 * (h % 2)
        for tc in range(NT):
            ni = (h * NT + tc) % 2
            k.waitp("tensor", pn_pend[ni])
            k.op("tensor", lambda e, l=vkTb[rbase:rbase + 64, h // 2, :],
                 r=qT[rbase:rbase + 64, fcq, tc * TC:(tc + 1) * TC], pp=pN[ni]:
                 e.matmul(pp[:, :], l, r, start=True, stop=True), inc=1)
            tmn = k.cnt["tensor"]
            k.wait("vector", "tensor", tmn)
            k.wait("vector", "sync", o_pend[ni])
            k.wait("scalar", "tensor", tmn)
            k.op("scalar", lambda e, pp=pN[ni]:
                 e.activation(rrow[:, :], pp[64:65, :], AF.Ln,
                              bias=cm.eps[0:1, 1:2]), inc=1)
            vm1 = k.op("scalar", lambda e:
                       e.activation(rrowb[:, :], rrow[:, :], AF.Exp,
                                    scale=-1.0), inc=1)
            k.wait("tensor", "scalar", vm1)
            k.waitp("tensor", rep_pend[0])
            tmr = k.op("tensor", lambda e:
                       e.matmul(pRep[:, :], cm.ones128[0:1, 0:64], rrowb[:, :],
                                start=True, stop=True), inc=1)
            k.wait("scalar", "tensor", tmr)
            smr = k.op("scalar", lambda e, o=repb[:, ni, :]:
                       e.activation(o, pRep[:, :], AF.Copy), inc=1)
            rep_pend[0] = ("scalar", smr)
            k.wait("vector", "scalar", smr)
            vm2 = k.op("vector", lambda e, pp=pN[ni], o=oT[:, ni, :],
                       rr=repb[:, ni, :]:
                       e.tensor_tensor(o, pp[0:64, :], rr, ALU.mult), inc=1)
            pn_pend[ni] = ("vector", vm2)
            k.wait("sync", "vector", vm2)
            o_pend[ni] = k.dma("sync", otd[64 * h:64 * h + 64,
                                           tc * TC:(tc + 1) * TC], oT[:, ni, :])

    te2, sc2, ve2, sy2 = (k.cnt["tensor"], k.cnt["scalar"], k.cnt["vector"],
                          k.cnt["sync"])
    ctxB.close()
    for e in ENGINES:
        k.wait(e, "tensor", te2)
        k.wait(e, "scalar", sc2)
        k.wait(e, "vector", ve2)
        k.wait(e, "sync", sy2)

    emit_out_proj_dram(k, cm, ctx, W["l0a_wo"], None, otd, xt, xo, "l0aop")
    k.wait("sync", "sync", k.cnt["sync"])
    if not standalone:
        return _block_close(k, ctx)
    k.finish()
    ctx.close()
    in_names = ["xt"] + list(W.keys()) + cm.in_names
    return nc, in_names


def prep_weights_l0a(inp):
    return {
        "l0a_wq": np.ascontiguousarray(inp["L0_attn_Wq"]),
        "l0a_wk": np.ascontiguousarray(inp["L0_attn_Wk"]),
        "l0a_wv": np.ascontiguousarray(inp["L0_attn_Wv"]),
        "l0a_wo": np.ascontiguousarray(inp["L0_attn_Wo"]),
    }



def build_L3f(env=None):
    nc, k, ctx, cm, xt, xo, standalone = _block_env(env)
    W = declare_w(nc, {
        "l3f_wg": (D, D), "l3f_bg": (1, D), "l3f_we": (8, D, D),
        "l3f_be": (8, 1, D),
    })
    sb = lambda n, s, dt=F32: ctx.enter_context(nc.sbuf_tensor(n, s, dt))

    ctxA = ExitStack()
    sbA = lambda n, s, dt=F32: ctxA.enter_context(
        nc.sbuf_tensor(n, s, dt, side="right"))
    xnb = sbA("xnb", [128, NF, T], BF16)
    ln_mark = emit_ln(k, cm, ctx, xt, xnb, "ln3f")

    acc = sb("l3f_acc", [128, NTT, D], BF16)
    ecol = sb("l3f_ecol", [128, NTT])
    pcol = sb("l3f_pcol", [128, NTT])

    ctxG = ExitStack()
    sbG = lambda n, s, dt=F32: ctxG.enter_context(
        nc.sbuf_tensor(n, s, dt, side="right"))
    wgb = sbG("wgb", [128, NF, D], BF16)
    bgb = sbG("bgb", [1, D], BF16)
    iotan = sbG("iotan", [128, D])
    gex = sbG("gex", [128, D], BF16)
    mxc = sbG("mxc", [128, 1])
    nmxc = sbG("nmxc", [128, 1])
    sec = sbG("sec", [128, 1])
    eqv = sbG("eqv", [128, D])
    idxc = sbG("idxc", [128, 4])
    wm = k.dma_batch("gpsimd", [
        (wgb[:, kc, :], W["l3f_wg"][kc * 128:(kc + 1) * 128, :])
        for kc in range(NF)] + [(bgb[:, :], W["l3f_bg"][:, :])])
    im = k.dma("sync", iotan[:, :], cm.aps["c_iotaneg"][:, :])
    k.wait("tensor", "gpsimd", wm)
    k.wait("tensor", "vector", ln_mark)
    k.wait("vector", "sync", im)

    pG = ctxG.enter_context(nc.psum_tensor("l3f_pG", [128, D], F32))
    pg_pend = [None]
    for tt in range(NTT):
        t0 = tt * 128
        k.waitp("tensor", pg_pend[0])
        for kc in range(NF):
            for half in range(2):
                k.op("tensor", lambda e, l=xnb[:, kc, t0:t0 + 128],
                     r=wgb[:, kc, half * 512:(half + 1) * 512],
                     pp=pG[:, bass.ts(half, 512)], first=(kc == 0):
                     e.matmul(pp, l, r, start=first, stop=False), inc=0)
        for half in range(2):
            k.op("tensor", lambda e, pp=pG[:, bass.ts(half, 512)],
                 b=bgb[0:1, half * 512:(half + 1) * 512]:
                 e.matmul(pp, cm.ones128[:, :], b, start=False, stop=True),
                 inc=(1 if half == 1 else 0))
        tmg = k.cnt["tensor"]
        k.wait("vector", "tensor", tmg)
        k.op("vector", lambda e: e.reduce_max(mxc[:, :], pG[:, :], axis=AX.X),
             inc=1)
        k.op("vector", lambda e: e.tensor_scalar_mul(nmxc[:, :], mxc[:, :],
                                                     -1.0), inc=1)
        vmx = k.cnt["vector"]
        k.wait("scalar", "vector", vmx)
        sme = k.op("scalar", lambda e:
                   e.activation(gex[:, :], pG[:, :], AF.Exp, bias=nmxc[:, :],
                                accum_out=sec[:, :]), inc=1)
        k.wait("vector", "scalar", sme)
        k.op("vector", lambda e, o=pcol[:, tt:tt + 1]:
             e.reciprocal(o, sec[:, :]), inc=1)
        k.op("vector", lambda e:
             e.tensor_single_scalar(eqv[:, :], pG[:, :], mxc[:, :], ALU.is_ge),
             inc=1)
        pg_pend[0] = ("vector", k.cnt["vector"])
        k.op("vector", lambda e:
             e.tensor_mul(eqv[:, :], eqv[:, :], iotan[:, :]), inc=1)
        k.op("vector", lambda e, o=idxc[:, 0:1]:
             e.tensor_reduce(o, eqv[:, :], AX.X, ALU.min), inc=1)
        k.op("vector", lambda e, o=idxc[:, 0:1]:
             e.tensor_scalar_add(o, o, 1e9), inc=1)
        k.op("vector", lambda e, o=idxc[:, 1:2], a=idxc[:, 0:1]:
             e.tensor_scalar(o, a, 0.125, -0.4375, ALU.mult, ALU.add), inc=1)
        k.op("vector", lambda e, o=idxc[:, 1:2], a=idxc[:, 1:2]:
             e.tensor_scalar_add(o, a, RMAGIC), inc=1)
        k.op("vector", lambda e, o=idxc[:, 1:2], a=idxc[:, 1:2]:
             e.tensor_scalar_add(o, a, -RMAGIC), inc=1)
        k.op("vector", lambda e, o=ecol[:, tt:tt + 1], a=idxc[:, 1:2],
             b=idxc[:, 0:1]:
             e.scalar_tensor_tensor(o, a, -8.0, b, ALU.mult, ALU.add), inc=1)

    marksG = {e: k.cnt[e] for e in ENGINES}
    ctxG.close()
    for e in ENGINES:
        for o in ("tensor", "vector", "scalar", "sync"):
            if o != e:
                k.wait(e, o, marksG[o])

    ctxB = ExitStack()
    sbB = lambda n, s, dt=F32: ctxB.enter_context(
        nc.sbuf_tensor(n, s, dt, side="right"))
    web = sbB("web", [128, 2, NF, D], BF16)
    beb = sbB("beb", [1, 2, D], BF16)
    zb = sbB("zb", [128, D], BF16)
    mcol = sbB("mcol", [128, 2])
    pZ = ctxB.enter_context(nc.psum_tensor("l3f_pZ", [128, D], F32))
    az = k.op("vector", lambda e: e.memset(acc[:, :, :], 0.0), inc=1)
    k.wait("tensor", "vector", az)
    we_pend = [None, None]
    pz_pend = [None]
    zb_pend = [None]
    for ei in range(8):
        eb = ei % 2
        k.waitp("gpsimd", we_pend[eb])
        wm = k.dma_batch("gpsimd", [
            (web[:, eb, kc, :], W["l3f_we"][ei, kc * 128:(kc + 1) * 128, :])
            for kc in range(NF)] + [(beb[0:1, eb, :], W["l3f_be"][ei, :, :])])
        k.wait("tensor", "gpsimd", wm)
        for tt in range(NTT):
            t0 = tt * 128
            k.waitp("tensor", pz_pend[0])
            for kc in range(NF):
                for half in range(2):
                    k.op("tensor", lambda e, l=xnb[:, kc, t0:t0 + 128],
                         r=web[:, eb, kc, half * 512:(half + 1) * 512],
                         pp=pZ[:, bass.ts(half, 512)], first=(kc == 0):
                         e.matmul(pp, l, r, start=first, stop=False), inc=0)
            for half in range(2):
                k.op("tensor", lambda e, pp=pZ[:, bass.ts(half, 512)],
                     b=beb[0:1, eb, half * 512:(half + 1) * 512]:
                     e.matmul(pp, cm.ones128[:, :], b, start=False, stop=True),
                     inc=(1 if half == 1 else 0))
            tmz = k.cnt["tensor"]
            k.wait("scalar", "tensor", tmz)
            k.waitp("scalar", zb_pend[0])
            smz = k.op("scalar", lambda e:
                       e.activation(zb[:, :], pZ[:, :], AF.Relu), inc=1)
            pz_pend[0] = ("scalar", smz)
            k.wait("vector", "scalar", smz)
            k.op("vector", lambda e, o=mcol[:, 0:1], a=ecol[:, tt:tt + 1],
                 ef=float(ei):
                 e.tensor_single_scalar(o, a, ef, ALU.is_equal), inc=1)
            k.op("vector", lambda e, o=mcol[:, 1:2], a=mcol[:, 0:1],
                 b=pcol[:, tt:tt + 1]:
                 e.tensor_mul(o, a, b), inc=1)
            k.op("vector", lambda e, o=acc[:, tt, :], s=mcol[:, 1:2]:
                 e.scalar_tensor_tensor(o, zb[:, :], s, o, ALU.mult, ALU.add),
                 inc=1)
            zb_pend[0] = ("vector", k.cnt["vector"])
        we_pend[eb] = ("tensor", k.cnt["tensor"])

    marksB = {e: k.cnt[e] for e in ENGINES}
    ctxB.close()
    for e in ENGINES:
        for o in ("tensor", "vector", "scalar", "sync"):
            if o != e:
                k.wait(e, o, marksB[o])
    ctxA.close()

    ctxC = ExitStack()
    sbC = lambda n, s, dt=F32: ctxC.enter_context(
        nc.sbuf_tensor(n, s, dt, side="right"))
    xres = sbC("xres", [128, 2, TC])
    pT = ctxC.enter_context(nc.psum_tensor("l3f_pT", [128, 128], BF16))
    pend_store = [None, None]
    pt_pend = [None]
    call = [0]
    for fc in range(NF):
        for tc in range(NT):
            buf = call[0] % 2
            call[0] += 1
            k.wait("sync", "sync", pend_store[buf])
            dmx = k.dma("sync", xres[:, buf, :],
                        xt[fc * 128:(fc + 1) * 128, tc * TC:(tc + 1) * TC])
            for q4 in range(4):
                tt = tc * 4 + q4
                k.waitp("tensor", pt_pend[0])
                k.op("tensor", lambda e, a=acc[:, tt, fc * 128:(fc + 1) * 128]:
                     e.transpose(pT[:, :], a, cm.ident[:, :]), inc=1)
                tmt = k.cnt["tensor"]
                k.wait("vector", "tensor", tmt)
                if q4 == 0:
                    k.wait("vector", "sync", dmx)
                vmt = k.op("vector", lambda e, o=xres[:, buf,
                                                      q4 * 128:q4 * 128 + 128]:
                           e.tensor_tensor(o, o, pT[:, :], ALU.add), inc=1)
                pt_pend[0] = ("vector", vmt)
            k.wait("sync", "vector", k.cnt["vector"])
            pend_store[buf] = k.dma(
                "sync", xo[fc * 128:(fc + 1) * 128, tc * TC:(tc + 1) * TC],
                xres[:, buf, :])

    k.wait("sync", "sync", k.cnt["sync"])
    ctxC.close()
    if not standalone:
        return _block_close(k, ctx)
    k.finish()
    ctx.close()
    in_names = ["xt"] + list(W.keys()) + cm.in_names
    return nc, in_names


def prep_weights_l3f(inp):
    return {
        "l3f_wg": np.ascontiguousarray(inp["L3_ffn_Wg"]),
        "l3f_bg": inp["L3_ffn_bg"].reshape(1, D).astype(np.float32),
        "l3f_we": np.ascontiguousarray(inp["L3_ffn_We"]),
        "l3f_be": np.ascontiguousarray(inp["L3_ffn_be"]).reshape(8, 1, D),
    }



HM = 4096


def build_L1f(env=None):
    nc, k, ctx, cm, xt, xo, standalone = _block_env(env)
    W = declare_w(nc, {
        "l1f_gw": (D, 8), "l1f_w1": (8, D, HM), "l1f_b1": (8, 128, 32),
        "l1f_w2": (8, HM, D), "l1f_b2": (8, D), "l1f_gamma": (1, D),
        "l1f_beta": (1, D),
    })
    cw_d = nc.dram_tensor("l1f_cwd", [8, T], F32, kind="Internal").ap()
    h_d = nc.dram_tensor("l1f_hd", [8, 32, 128, T], BF16, kind="Internal").ap()
    sb = lambda n, s, dt=F32: ctx.enter_context(nc.sbuf_tensor(n, s, dt))

    ctxA = ExitStack()
    sbA = lambda n, s, dt=F32: ctxA.enter_context(
        nc.sbuf_tensor(n, s, dt, side="right"))
    xnb = sbA("xnb", [128, NF, T], BF16)
    ln_mark = emit_ln(k, cm, ctx, xt, xnb, "ln1f")
    xn_d = nc.dram_tensor("l1f_xnd", [D, T], BF16, kind="Internal").ap()
    spm = None
    for fc in range(NF):
        spm = k.dma("sync", xn_d[fc * 128:(fc + 1) * 128, :], xnb[:, fc, :])
    k.wait("vector", "sync", spm)
    emit_ln_sbuf_affine(k, cm, ctx, xnb, W["l1f_gamma"], W["l1f_beta"], "ln1fi")

    ctxG = ExitStack()
    sbG = lambda n, s, dt=F32: ctxG.enter_context(
        nc.sbuf_tensor(n, s, dt, side="right"))
    gwb = sbG("gwb", [128, NF, 8], BF16)
    ex = sbG("ex", [128, 8])
    c1 = sbG("c1", [128, 6])
    cwc = sbG("cwc", [128, 8])
    cwtr = sbG("cwtr", [8, 128])
    mskd = sbG("mskd", [128, 8])
    wm = k.dma_batch("gpsimd", [
        (gwb[:, kc, :], W["l1f_gw"][kc * 128:(kc + 1) * 128, :])
        for kc in range(NF)])
    k.wait("tensor", "gpsimd", wm)
    k.wait("tensor", "vector", ln_mark)
    pG = ctxG.enter_context(nc.psum_tensor("l1f_pG", [128, 8], F32))
    pCt = ctxG.enter_context(nc.psum_tensor("l1f_pCt", [8, 128], F32))
    pg_pend = [None]
    ct_pend = [None]
    cw_store = [None]
    for tt in range(NTT):
        t0 = tt * 128
        k.waitp("tensor", pg_pend[0])
        for kc in range(NF):
            k.op("tensor", lambda e, l=xnb[:, kc, t0:t0 + 128],
                 r=gwb[:, kc, :], pp=pG, first=(kc == 0), last=(kc == NF - 1):
                 e.matmul(pp[:, :], l, r, start=first, stop=last),
                 inc=(1 if kc == NF - 1 else 0))
        tmg = k.cnt["tensor"]
        k.wait("vector", "tensor", tmg)
        k.op("vector", lambda e: e.reduce_max(c1[:, 0:1], pG[:, :], axis=AX.X,
                                              negate=True), inc=1)
        vmx = k.cnt["vector"]
        k.wait("scalar", "vector", vmx)
        sme = k.op("scalar", lambda e:
                   e.activation(ex[:, :], pG[:, :], AF.Exp, bias=c1[:, 0:1]),
                   inc=1)
        pg_pend[0] = ("scalar", sme)
        k.wait("vector", "scalar", sme)
        k.op("vector", lambda e: e.reduce_max(c1[:, 1:2], ex[:, :], axis=AX.X),
             inc=1)
        k.op("vector", lambda e:
             e.tensor_single_scalar(mskd[:, :], ex[:, :], c1[:, 1:2],
                                    ALU.is_ge), inc=1)
        k.op("vector", lambda e:
             e.scalar_tensor_tensor(mskd[:, :], mskd[:, :], -1e30, ex[:, :],
                                    ALU.mult, ALU.add), inc=1)
        k.op("vector", lambda e: e.reduce_max(c1[:, 3:4], mskd[:, :],
                                              axis=AX.X), inc=1)
        k.op("vector", lambda e:
             e.tensor_tensor(c1[:, 4:5], c1[:, 1:2], c1[:, 3:4], ALU.add),
             inc=1)
        k.op("vector", lambda e:
             e.reciprocal(c1[:, 4:5], c1[:, 4:5]), inc=1)
        k.op("vector", lambda e:
             e.tensor_single_scalar(cwc[:, :], ex[:, :], c1[:, 3:4], ALU.is_ge),
             inc=1)
        k.op("vector", lambda e:
             e.tensor_mul(cwc[:, :], cwc[:, :], ex[:, :]), inc=1)
        vcw = k.op("vector", lambda e:
                   e.tensor_scalar_mul(cwc[:, :], cwc[:, :], c1[:, 4:5]), inc=1)
        k.wait("tensor", "vector", vcw)
        k.waitp("tensor", ct_pend[0])
        tmt = k.op("tensor", lambda e:
                   e.transpose(pCt[:, :], cwc[:, :], cm.identf[:, :]), inc=1)
        k.wait("vector", "tensor", tmt)
        k.waitp("vector", cw_store[0])
        vct = k.op("vector", lambda e:
                   e.tensor_copy(cwtr[:, :], pCt[:, :]), inc=1)
        ct_pend[0] = ("vector", vct)
        k.wait("sync", "vector", vct)
        cw_store[0] = ("sync", k.dma("sync", cw_d[:, t0:t0 + 128], cwtr[:, :]))

    marksG = {e: k.cnt[e] for e in ENGINES}
    ctxG.close()
    for e in ENGINES:
        for o in ("tensor", "vector", "scalar", "sync", "gpsimd"):
            if o != e:
                k.wait(e, o, marksG[o])

    ctxB = ExitStack()
    sbB = lambda n, s, dt=F32: ctxB.enter_context(
        nc.sbuf_tensor(n, s, dt, side="right"))
    w1t = sbB("w1t", [128, 2, NF, 128], BF16)
    b1c = sbB("b1c", [128, 32])
    cwrow = sbB("cwrow", [1, T], BF16)
    cwrepb = sbB("cwrepb", [128, T], BF16)
    hb = sbB("hb", [128, 6, TC], BF16)
    hcw = sbB("hcw", [128, 6, TC], BF16)
    pH = [ctxB.enter_context(nc.psum_tensor(f"l1f_pH{i}", [128, TC], F32))
          for i in range(6)]
    pRc = ctxB.enter_context(nc.psum_tensor("l1f_pRc", [128, TC], F32))
    w1_pend = [None, None]
    phA_call = [0]
    ph_pend = [None] * 6
    hcw_pend = [None] * 6
    rc_pend = [None]
    prev_e_te = [None]
    prev_e_sc = [None]
    for ei in range(8):
        k.wait("gpsimd", "tensor", prev_e_te[0]) if prev_e_te[0] else None
        cwm = k.dma("gpsimd", cwrow[:, :], cw_d[ei:ei + 1, :])
        k.wait("tensor", "gpsimd", cwm)
        k.wait("gpsimd", "scalar", prev_e_sc[0]) if prev_e_sc[0] else None
        bm = k.dma("gpsimd", b1c[:, :], W["l1f_b1"][ei, :, :])
        for tc in range(NT):
            k.waitp("tensor", rc_pend[0])
            trc = k.op("tensor", lambda e, r=cwrow[0:1, tc * TC:(tc + 1) * TC]:
                       e.matmul(pRc[:, :], cm.ones128[:, :], r, start=True,
                                stop=True), inc=1)
            k.wait("vector", "tensor", trc)
            vrc = k.op("vector", lambda e, o=cwrepb[:, tc * TC:(tc + 1) * TC]:
                       e.tensor_copy(o, pRc[:, :]), inc=1)
            rc_pend[0] = ("vector", vrc)
        cwrep_mark = k.cnt["vector"]
        k.wait("scalar", "gpsimd", bm)
        for hc in range(32):
            wb_i = hc % 2
            k.waitp("gpsimd", w1_pend[wb_i])
            wm = k.dma_batch("gpsimd", [
                (w1t[:, wb_i, kc, :],
                 W["l1f_w1"][ei, kc * 128:(kc + 1) * 128,
                             hc * 128:(hc + 1) * 128])
                for kc in range(NF)])
            k.wait("tensor", "gpsimd", wm)
            for tch in range(NT):
                pi = phA_call[0] % 6
                phA_call[0] += 1
                k.waitp("tensor", ph_pend[pi])
                for kc in range(NF):
                    k.op("tensor", lambda e, l=w1t[:, wb_i, kc, :],
                         r=xnb[:, kc, tch * TC:(tch + 1) * TC], pp=pH[pi],
                         first=(kc == 0), last=(kc == NF - 1):
                         e.matmul(pp[:, :], l, r, start=first, stop=last),
                         inc=(1 if kc == NF - 1 else 0))
                tmh = k.cnt["tensor"]
                k.wait("scalar", "tensor", tmh)
                k.waitp("scalar", hcw_pend[pi])
                smg = k.op("scalar", lambda e, pp=pH[pi], o=hb[:, pi, :],
                           b=b1c[:, hc:hc + 1]:
                           e.activation(o, pp[:, :], AF.Gelu, bias=b), inc=1)
                ph_pend[pi] = ("scalar", smg)
                k.wait("vector", "scalar", smg)
                if ei == 0 and hc == 0 and tch < 2:
                    k.wait("vector", "vector", cwrep_mark)
                vmc = k.op("vector", lambda e, o=hcw[:, pi, :], a=hb[:, pi, :],
                           b=cwrepb[:, tch * TC:(tch + 1) * TC]:
                           e.tensor_tensor(o, a, b, ALU.mult), inc=1)
                k.wait("sync", "vector", vmc)
                sm2 = k.dma("sync", h_d[ei, hc, :, tch * TC:(tch + 1) * TC],
                            hcw[:, pi, :])
                hcw_pend[pi] = ("sync", sm2)
            w1_pend[wb_i] = ("tensor", k.cnt["tensor"])
        prev_e_te[0] = k.cnt["tensor"]
        prev_e_sc[0] = k.cnt["scalar"]

    marksA = {e: k.cnt[e] for e in ENGINES}
    ctxB.close()
    ctxA.close()
    for e in ENGINES:
        for o in ("tensor", "vector", "scalar", "sync", "gpsimd"):
            if o != e:
                k.wait(e, o, marksA[o])

    ctxC = ExitStack()
    sbC = lambda n, s, dt=F32: ctxC.enter_context(
        nc.sbuf_tensor(n, s, dt, side="right"))
    w2e = sbC("w2e", [128, 32, D], BF16)
    b2b = sbC("b2b", [8, D], BF16)
    cwt8 = sbC("cwt8", [8, T], BF16)
    ht = sbC("ht", [128, 8, TC], BF16)
    xres = sbC("xres", [128, 2, TC])
    zacc = sbC("zacc", [128, NF, T], BF16)
    pZ = [ctxC.enter_context(nc.psum_tensor(f"l1f_pZ{i}", [128, TC], F32))
          for i in range(8)]
    bm = k.dma("gpsimd", b2b[:, :], W["l1f_b2"][:, :])
    bm = k.dma("gpsimd", cwt8[:, :], cw_d[:, :])
    zm = k.op("vector", lambda e: e.memset(zacc[:, :, :], 0.0), inc=1)
    ev_pend = [None] * NF
    ht_cons = [None, None]
    w2_used = [None] * 4
    w2_mark = [None] * 4
    for ei in range(8):
        if ei == 0:
            k.wait("vector", "vector", zm)

        def issue_w2(wb, ei=ei):
            k.waitp("gpsimd", w2_used[wb])
            w2_mark[wb] = k.dma_batch("gpsimd", [
                (w2e[:, wb * 8 + j, :],
                 W["l1f_w2"][ei, (wb * 8 + j) * 128:(wb * 8 + j + 1) * 128, :])
                for j in range(8)])

        for tc in range(NT):
            tl = tc * TC
            gm = {}

            def issue_group(hg, tl=tl, ei=ei, gm=gm):
                half = hg % 2
                k.waitp("gpsimd", ht_cons[half])
                gm[hg] = k.dma_batch("gpsimd", [
                    (ht[:, half * 4 + j, :],
                     h_d[ei, hg * 4 + j, :, tl:tl + TC])
                    for j in range(4)])

            if tc == 0:
                issue_w2(0)
            issue_group(0)
            issue_group(1)
            if tc == 0:
                issue_w2(1)
            for hg in range(8):
                half = hg % 2
                k.wait("tensor", "gpsimd", gm[hg])
                if tc == 0:
                    k.wait("tensor", "gpsimd", w2_mark[hg // 2])
                for j in range(4):
                    hc = hg * 4 + j
                    hbuf = half * 4 + j
                    for oc in range(NF):
                        if hc == 0:
                            k.waitp("tensor", ev_pend[oc])
                        k.op("tensor", lambda e,
                             l=w2e[:, hc, oc * 128:(oc + 1) * 128],
                             r=ht[:, hbuf, :], pp=pZ[oc], first=(hc == 0),
                             last=(hc == 31):
                             e.matmul(pp[:, :], l, r, start=first, stop=last),
                             inc=(1 if oc == NF - 1 else 0))
                ht_cons[half] = ("tensor", k.cnt["tensor"])
                if hg + 2 < 8:
                    issue_group(hg + 2)
                if tc == 0 and hg % 2 == 0 and hg // 2 + 2 < 4:
                    issue_w2(hg // 2 + 2)
            if tc == NT - 1:
                for wb in range(4):
                    w2_used[wb] = ("tensor", k.cnt["tensor"])
            tmz = k.cnt["tensor"]
            k.wait("vector", "tensor", tmz)
            for oc in range(NF):
                k.op("vector", lambda e, o=zacc[:, oc, tl:tl + TC], pp=pZ[oc]:
                     e.tensor_tensor(o, o, pp[:, :], ALU.add), inc=1)
                ev_pend[oc] = ("vector", k.cnt["vector"])

    marksB2 = {e: k.cnt[e] for e in ENGINES}
    for e in ENGINES:
        for o in ("tensor", "vector", "scalar", "sync", "gpsimd"):
            if o != e:
                k.wait(e, o, marksB2[o])

    pend_store = [None, None]
    call = [0]
    pB2 = pZ[0]
    pb2_pend = [None]
    xnt = sbC("xnt", [128, 2, TC], BF16)
    xnt_pend = [None, None]
    for oc in range(NF):
        for tc in range(NT):
            tl = tc * TC
            buf = call[0] % 2
            call[0] += 1
            k.waitp("tensor", pb2_pend[0])
            tb2 = k.op("tensor", lambda e, l=b2b[:, oc * 128:(oc + 1) * 128],
                       r=cwt8[:, tl:tl + TC], pp=pB2:
                       e.matmul(pp[:, :], l, r, start=True, stop=True), inc=1)
            k.wait("sync", "sync", pend_store[buf])
            dmx = k.dma("sync", xres[:, buf, :],
                        xt[oc * 128:(oc + 1) * 128, tl:tl + TC])
            k.waitp("gpsimd", xnt_pend[buf])
            dmn = k.dma("gpsimd", xnt[:, buf, :],
                        xn_d[oc * 128:(oc + 1) * 128, tl:tl + TC])
            k.wait("vector", "sync", dmx)
            k.wait("vector", "gpsimd", dmn)
            k.wait("vector", "tensor", tb2)
            k.op("vector", lambda e, o=xres[:, buf, :], pp=pB2:
                 e.tensor_tensor(o, o, pp[:, :], ALU.add), inc=1)
            k.op("vector", lambda e, o=xres[:, buf, :],
                 a=zacc[:, oc, tl:tl + TC]:
                 e.tensor_tensor(o, o, a, ALU.add), inc=1)
            vmz = k.op("vector", lambda e, o=xres[:, buf, :],
                       a=xnt[:, buf, :]:
                       e.tensor_tensor(o, o, a, ALU.add), inc=1)
            pb2_pend[0] = ("vector", vmz)
            xnt_pend[buf] = ("vector", vmz)
            k.wait("sync", "vector", vmz)
            pend_store[buf] = k.dma("sync", xo[oc * 128:(oc + 1) * 128,
                                               tl:tl + TC], xres[:, buf, :])

    k.wait("sync", "sync", k.cnt["sync"])
    ctxC.close()
    if not standalone:
        return _block_close(k, ctx)
    k.finish()
    ctx.close()
    in_names = ["xt"] + list(W.keys()) + cm.in_names
    return nc, in_names


def prep_weights_l1f(inp):
    return {
        "l1f_gamma": inp["L1_ffn_gamma"].reshape(1, D).astype(np.float32),
        "l1f_beta": inp["L1_ffn_beta"].reshape(1, D).astype(np.float32),
        "l1f_gw": np.ascontiguousarray(inp["L1_ffn_gate_w"]),
        "l1f_w1": np.ascontiguousarray(inp["L1_ffn_w1"]),
        "l1f_b1": np.ascontiguousarray(
            inp["L1_ffn_b1"].reshape(8, 32, 128).transpose(0, 2, 1)),
        "l1f_w2": np.ascontiguousarray(inp["L1_ffn_w2"]),
        "l1f_b2": np.ascontiguousarray(inp["L1_ffn_b2"]),
    }



L0F_HID = 2730


def _l0f_chunks():
    out = []
    c = 0
    while c < L0F_HID:
        w = min(128, L0F_HID - c)
        out.append((c, w))
        c += w
    return out


def build_L0f(env=None):
    nc, k, ctx, cm, xt, xo, standalone = _block_env(env)
    W = declare_w(nc, {
        "l0f_wqkv": (2, D, 1536), "l0f_qks": (2, 1, 128),
        "l0f_wo": (2, 512, D), "l0f_lng": (2, 1, D), "l0f_lnb": (2, 1, D),
        "l0f_w1": (2, D, 2 * L0F_HID), "l0f_b1": (2, 1, 2 * L0F_HID),
        "l0f_w2": (2, L0F_HID, D), "l0f_b2": (2, 1, D),
    })
    xi_d = nc.dram_tensor("l0f_xid", [D, T], F32, kind="Internal").ap()
    ot_d = nc.dram_tensor("l0f_otd", [512, T], BF16, kind="Internal").ap()
    mq_d = nc.dram_tensor("l0f_mqd", [64, 512], BF16, kind="Internal").ap()
    vg_d = nc.dram_tensor("l0f_vgd", [NTT * 128, 8 * 65], BF16,
                          kind="Internal").ap()
    hn_d = nc.dram_tensor("l0f_hnd", [D, T], BF16, kind="Internal").ap()
    sb = lambda n, s, dt=F32: ctx.enter_context(nc.sbuf_tensor(n, s, dt))

    xpA = sb("l0f_xpA", [128, NTT, 128], BF16)
    xpB = sb("l0f_xpB", [128, NTT, 128], BF16)
    k.dma("gpsimd", xpA[:, :, :],
          cm.aps["c_xpA"][:, :].rearrange("(a p) f -> p a f", p=128))
    tbm = k.dma("gpsimd", xpB[:, :, :],
                cm.aps["c_xpB"][:, :].rearrange("(a p) f -> p a f", p=128))

    ctx0 = ExitStack()
    sb0 = lambda n, s, dt=F32: ctx0.enter_context(
        nc.sbuf_tensor(n, s, dt, side="right"))
    xnb = sb0("xnb", [128, NF, T], BF16)
    ln_mark = emit_ln(k, cm, ctx, xt, xnb, "ln0f")
    spm = None
    for fc in range(NF):
        spm = k.dma("gpsimd", xi_d[fc * 128:(fc + 1) * 128, :], xnb[:, fc, :])
    m0 = {e: k.cnt[e] for e in ENGINES}
    ctx0.close()
    for e in ENGINES:
        for o in ENGINES:
            if o != e:
                k.wait(e, o, m0[o])

    for li in range(2):
        _l0f_layer(k, cm, ctx, W, xi_d, ot_d, mq_d, vg_d, hn_d, xpA, xpB, li)

    ctxF = ExitStack()
    sbF = lambda n, s, dt=F32: ctxF.enter_context(
        nc.sbuf_tensor(n, s, dt, side="right"))
    xres = sbF("f_xres", [128, 2, TC])
    xadd = sbF("f_xadd", [128, 2, TC])
    pend_store = [None, None]
    buf_pend = [None, None]
    call = [0]
    for fc in range(NF):
        for tc in range(NT):
            buf = call[0] % 2
            call[0] += 1
            k.wait("sync", "sync", pend_store[buf])
            k.waitp("gpsimd", buf_pend[buf])
            dm1 = k.dma("sync", xres[:, buf, :],
                        xt[fc * 128:(fc + 1) * 128, tc * TC:(tc + 1) * TC])
            dm2 = k.dma("gpsimd", xadd[:, buf, :],
                        xi_d[fc * 128:(fc + 1) * 128, tc * TC:(tc + 1) * TC])
            k.wait("vector", "sync", dm1)
            k.wait("vector", "gpsimd", dm2)
            vm = k.op("vector", lambda e, o=xres[:, buf, :], a=xadd[:, buf, :]:
                      e.tensor_tensor(o, o, a, ALU.add), inc=1)
            buf_pend[buf] = ("vector", vm)
            k.wait("sync", "vector", vm)
            pend_store[buf] = k.dma(
                "sync", xo[fc * 128:(fc + 1) * 128, tc * TC:(tc + 1) * TC],
                xres[:, buf, :])
    k.wait("sync", "sync", k.cnt["sync"])
    if not standalone:
        ctxF.close()
        return _block_close(k, ctx)
    k.finish()
    ctxF.close()
    ctx.close()
    in_names = ["xt"] + list(W.keys()) + cm.in_names
    return nc, in_names


def _l0f_layer(k, cm, ctx, W, xi_d, ot_d, mq_d, vg_d, hn_d, xpA, xpB, li):
    nc = k.nc
    tg = f"l0fL{li}"

    def barrier():
        m = {e: k.cnt[e] for e in ENGINES}
        for e in ENGINES:
            for o in ENGINES:
                if o != e:
                    k.wait(e, o, m[o])

    ctxL = ExitStack()
    qkT = ctxL.enter_context(
        nc.sbuf_tensor(tg + "qkT", [128, 8, T], BF16, side="right"))

    ctx1 = ExitStack()
    sb1 = lambda n, s, dt=F32: ctx1.enter_context(
        nc.sbuf_tensor(tg + n, s, dt, side="right"))
    xib = sb1("xib", [128, NF, T + 1], BF16)
    wqkvb = sb1("wqkvb", [128, NF, 1536], BF16)
    qksr = sb1("qksr", [1, 128], BF16)
    screp = sb1("screp", [128, 128], BF16)
    rno = sb1("rno", [128, 3, 16])
    ropef = sb1("ropef", [128, 2, 1024])
    qkb = sb1("qkb", [128, 1024], BF16)
    vstg = sb1("vstg", [128, 2, 8, 65], BF16)
    pQKV = [ctx1.enter_context(
        nc.psum_tensor(f"{tg}_pqkv{i}", [128, 1536], F32)) for i in range(2)]
    pT = [ctx1.enter_context(
        nc.psum_tensor(f"{tg}_pT{i}", [128, 128], BF16)) for i in range(2)]
    pRp = pQKV[0][:, 0:128]

    wm2 = k.dma_batch("gpsimd", [
        p for kc in range(NF) for p in (
            (xib[:, kc, 1:T + 1], xi_d[kc * 128:(kc + 1) * 128, :]),
            (wqkvb[:, kc, :], W["l0f_wqkv"][li, kc * 128:(kc + 1) * 128, :]))
    ] + [(qksr[:, :], W["l0f_qks"][li, :, :])])
    k.wait("vector", "gpsimd", wm2)
    k.op("vector", lambda e: e.memset(xib[:, :, 0:1], 0.0), inc=1)
    zm = k.op("vector", lambda e: e.memset(vstg[:, :, :, 64:65], 1.0), inc=1)
    k.wait("tensor", "gpsimd", wm2)
    trp = k.op("tensor", lambda e:
               e.matmul(pRp, cm.ones128[:, :], qksr[:, :], start=True,
                        stop=True), inc=1)
    k.wait("vector", "tensor", trp)
    vsc = k.op("vector", lambda e: e.tensor_copy(screp[:, :], pRp),
               inc=1)
    k.wait("tensor", "vector", zm)
    k.wait("tensor", "vector", vsc)

    def shifted(kc, lo, width):
        off = 1 if kc < 4 else 0
        return xib[:, kc, lo + off:lo + off + width]

    pq_pend = [None, None]
    pt_pend = [None, None]
    vst_pend = [None, None]
    qm_prev = None
    for tt in range(NTT):
        t0 = tt * 128
        pi = tt % 2
        vb = tt % 2
        k.waitp("tensor", pq_pend[pi])
        for third in range(3):
            for kc in range(NF):
                k.op("tensor", lambda e, l=shifted(kc, t0, 128),
                     r=wqkvb[:, kc, third * 512:(third + 1) * 512],
                     pp=pQKV[pi][:, third * 512:(third + 1) * 512],
                     first=(kc == 0), last=(kc == NF - 1):
                     e.matmul(pp, l, r, start=first, stop=last),
                     inc=(1 if kc == NF - 1 else 0))
        tmq = k.cnt["tensor"]
        k.wait("scalar", "tensor", tmq)
        k.waitp("scalar", qm_prev)
        sm1 = k.op("scalar", lambda e, pp=pQKV[pi]:
                   e.activation(ropef[:, 0, :], pp[:, 0:1024], AF.Square),
                   inc=1)
        k.wait("vector", "scalar", sm1)
        k.op("vector", lambda e:
             e.reduce_sum(rno[:, 0, :], ropef[:, 0, :].rearrange(
                 "p (g d) -> p g d", g=16), axis=AX.X), inc=1)
        vr = k.cnt["vector"]
        k.wait("scalar", "vector", vr)
        sm2 = k.op("scalar", lambda e:
                   e.activation(rno[:, 1, :], rno[:, 0, :], AF.Sqrt), inc=1)
        k.wait("vector", "scalar", sm2)
        k.op("vector", lambda e:
             e.tensor_scalar_max(rno[:, 1, :], rno[:, 1, :], 1e-12), inc=1)
        k.op("vector", lambda e:
             e.reciprocal(rno[:, 2, :], rno[:, 1, :]), inc=1)
        rno_b = bass.AP(rno.ap().tensor, rno[:, 2, :].offset,
                        [[48, 128], [1, 16], [0, 64]])
        k.op("vector", lambda e, pp=pQKV[pi], b=rno_b:
             e.tensor_tensor(ropef[:, 0, :].rearrange(
                 "p (g d) -> p g d", g=16), pp[:, 0:1024], b, ALU.mult),
             inc=1)
        k.waitp("vector", vst_pend[vb])
        vmv = k.op("vector", lambda e, pp=pQKV[pi], o=vstg[:, vb, :, 0:64]:
                   e.tensor_copy(o, pp[:, 1024:1536].rearrange(
                       "p (h v) -> p h v", h=8)), inc=1)
        pq_pend[pi] = ("vector", vmv)
        k.wait("sync", "vector", vmv)
        vst_pend[vb] = ("sync",
                        k.dma("sync", vg_d[t0:t0 + 128, :],
                              vstg[:, vb, :, :].rearrange("p h v -> p (h v)")))
        for qk in range(2):
            scr_b = bass.AP(screp.ap().tensor,
                            screp.ap().offset + qk * 64,
                            [[128, 128], [0, 8], [1, 64]])
            k.op("vector", lambda e, b=scr_b,
                 o=ropef[:, 0, qk * 512:(qk + 1) * 512].rearrange(
                     "p (h c) -> p h c", h=8):
                 e.tensor_tensor(o, o, b, ALU.mult), inc=1)
        for qk in range(2):
            B_b = bass.AP(xpB.ap().tensor,
                          xpB[:, tt, :].offset + qk * 64,
                          [[NTT * 128, 128], [0, 8], [32, 2], [1, 32]])
            vsw = bass.AP(ropef.ap().tensor,
                          ropef[:, 0, :].offset + qk * 512 + 32,
                          [[2048, 128], [64, 8], [-32, 2], [1, 32]])
            k.op("vector", lambda e, b=B_b, a=vsw,
                 o=ropef[:, 1, qk * 512:(qk + 1) * 512].rearrange(
                     "p (h a c) -> p h a c", h=8, a=2):
                 e.tensor_tensor(o, a, b, ALU.mult), inc=1)
        for qk in range(2):
            A_b = bass.AP(xpA.ap().tensor,
                          xpA[:, tt, :].offset + qk * 64,
                          [[NTT * 128, 128], [0, 8], [1, 64]])
            k.op("vector", lambda e, b=A_b,
                 o=ropef[:, 0, qk * 512:(qk + 1) * 512].rearrange(
                     "p (h c) -> p h c", h=8):
                 e.tensor_tensor(o, o, b, ALU.mult), inc=1)
        qm = k.op("vector", lambda e:
                  e.tensor_tensor(qkb[:, :], ropef[:, 0, :], ropef[:, 1, :],
                                  ALU.add), inc=1)
        qm_prev = ("vector", qm)
        k.wait("tensor", "vector", qm)
        for g in range(8):
            gi = g % 2
            k.waitp("tensor", pt_pend[gi])
            ttm = k.op("tensor", lambda e, a=qkb[:, g * 128:(g + 1) * 128],
                       pp=pT[gi][:, :]:
                       e.transpose(pp, a, cm.ident[:, :]), inc=1)
            k.wait("scalar", "tensor", ttm)
            vmt = k.op("scalar", lambda e, o=qkT[:, g, t0:t0 + 128],
                       pp=pT[gi][:, :]:
                       e.activation(o, pp, AF.Copy), inc=1)
            pt_pend[gi] = ("scalar", vmt)
    barrier()
    ctx1.close()
    barrier()

    ctx2a = ExitStack()
    mqc = ctx2a.enter_context(
        nc.sbuf_tensor(tg + "mqc", [128, 256], F32, side="right"))
    mqtr = ctx2a.enter_context(
        nc.sbuf_tensor(tg + "mqtr", [128, 128], BF16, side="right"))
    pSa = [ctx2a.enter_context(
        nc.psum_tensor(f"{tg}_pSa{i}", [128, 640], F32)) for i in range(2)]
    pMq = ctx2a.enter_context(nc.psum_tensor(tg + "_pMq", [128, 128], F32))

    def hgeo(h):
        return 64 * (h % 2), h // 2, 4 + h // 2

    ps_pend = [None, None]
    cnt1 = [0]
    for h in range(8):
        rb, gq, gk = hgeo(h)
        for w in range(8):
            for qt in range(4):
                qa = 4 * w + qt
                lo = max(0, qa - 4)
                nkt = qa - lo + 1
                ci = cnt1[0] % 2
                cnt1[0] += 1
                k.waitp("tensor", ps_pend[ci])
                if nkt == 5:
                    k.op("tensor", lambda e,
                         l=qkT[rb:rb + 64, gq, qa * 128:qa * 128 + 128],
                         r=qkT[rb:rb + 64, gk, (lo + 4) * 128:(lo + 5) * 128],
                         pp=pSa[ci][:, 512:640]:
                         e.matmul(pp, l, r, start=True, stop=False), inc=1)
                    k.op("tensor", lambda e, pp=pSa[ci][:, 512:640]:
                         e.matmul(pp, cm.ident[:, :], cm.maskcgt[:, :],
                                  start=False, stop=True), inc=1)
                    k.op("tensor", lambda e,
                         l=qkT[rb:rb + 64, gq, qa * 128:qa * 128 + 128],
                         r=qkT[rb:rb + 64, gk, lo * 128:(lo + 4) * 128],
                         pp=pSa[ci][:, 0:512]:
                         e.matmul(pp, l, r, start=True, stop=False), inc=1)
                    k.op("tensor", lambda e, pp=pSa[ci][:, 0:128]:
                         e.matmul(pp, cm.ident[:, :], cm.maskkgt[:, :],
                                  start=False, stop=True), inc=1)
                else:
                    k.op("tensor", lambda e,
                         l=qkT[rb:rb + 64, gq, qa * 128:qa * 128 + 128],
                         r=qkT[rb:rb + 64, gk, lo * 128:(lo + nkt) * 128],
                         pp=pSa[ci][:, 0:nkt * 128]:
                         e.matmul(pp, l, r, start=True, stop=False), inc=1)
                    k.op("tensor", lambda e,
                         pp=pSa[ci][:, (nkt - 1) * 128:nkt * 128]:
                         e.matmul(pp, cm.ident[:, :], cm.maskcgt[:, :],
                                  start=False, stop=True), inc=1)
                tmm = k.cnt["tensor"]
                k.wait("vector", "tensor", tmm)
                c = h * 32 + w * 4 + qt
                vv = k.op("vector", lambda e, pp=pSa[ci][:, 0:nkt * 128],
                          o=mqc[:, c:c + 1]:
                          e.reduce_max(o, pp, axis=AX.X, negate=True), inc=1)
                ps_pend[ci] = ("vector", vv)
    k.wait("tensor", "vector", k.cnt["vector"])
    mqv = mq_d[:, :].rearrange("a (b c) -> (a b) c", c=128)
    smt = None
    dm0 = None
    for half in range(2):
        if half:
            k.wait("tensor", "scalar", smt)
        ttm = k.op("tensor", lambda e, a=mqc[:, half * 128:(half + 1) * 128]:
                   e.transpose(pMq[:, :], a, cm.identf[:, :]), inc=1)
        k.wait("scalar", "tensor", ttm)
        if half:
            k.wait("scalar", "sync", dm0)
        smt = k.op("scalar", lambda e:
                   e.activation(mqtr[:, :], pMq[:, :], AF.Copy), inc=1)
        k.wait("sync", "scalar", smt)
        dm0 = k.dma("sync", mqv[half * 128:(half + 1) * 128, :], mqtr[:, :])
    barrier()
    ctx2a.close()
    barrier()

    ctx2b = ExitStack()
    sb2 = lambda n, s, dt=F32: ctx2b.enter_context(
        nc.sbuf_tensor(tg + n, s, dt, side="right"))
    vstg2 = sb2("vstg2", [128, 2, 8, 65], BF16)
    mqrow = sb2("mqrow", [1, 2, TC], BF16)
    util = sb2("util", [128, 2, TC], BF16)
    rrow = sb2("rrow", [1, TC])
    rrowb = sb2("rrowb", [1, TC], BF16)
    repb = sb2("repb", [64, 2, TC], BF16)
    oT = sb2("oT", [64, 2, TC], BF16)
    pS2 = [ctx2b.enter_context(
        nc.psum_tensor(f"{tg}_pS2{i}", [128, TC], F32)) for i in range(2)]
    pN = [ctx2b.enter_context(
        nc.psum_tensor(f"{tg}_pN{i}", [65, TC], F32)) for i in range(2)]
    pRep2 = ctx2b.enter_context(nc.psum_tensor(tg + "_pRep2", [64, TC], F32))

    vstg_pend2 = [None, None]
    ps2_pend = [None, None]
    pn_pend = [None, None]
    rep_pend = [None]
    o_pend = [None, None]
    for h in range(8):
        rb, gq, gk = hgeo(h)
        for w in range(8):
            idx = h * 8 + w
            vb = idx % 2
            ni = idx % 2
            lo_k = max(0, 4 * w - 4)
            kas = list(range(lo_k, 4 * w + 4))
            k.waitp("gpsimd", vstg_pend2[vb])
            mrm = k.dma_batch("gpsimd", [
                (vstg2[:, vb, j, :],
                 vg_d[ka * 128:(ka + 1) * 128, h * 65:(h + 1) * 65])
                for j, ka in enumerate(kas)
            ] + [(mqrow[0:1, vb, :], mq_d[idx:idx + 1, :])])
            k.wait("tensor", "gpsimd", mrm)
            k.waitp("tensor", pn_pend[ni])
            s_mark = {}

            def emit_s2(j, vb=vb, rb=rb, gq=gq, gk=gk, w=w):
                ci = j % 2
                ka = kas[j]
                k.waitp("tensor", ps2_pend[ci])
                k.op("tensor", lambda e,
                     l=qkT[rb:rb + 64, gk, ka * 128:ka * 128 + 128],
                     r=qkT[rb:rb + 64, gq, w * 512:(w + 1) * 512],
                     pp=pS2[ci][:, :]:
                     e.matmul(pp, l, r, start=True, stop=False), inc=0)
                for qt in range(4):
                    qa = 4 * w + qt
                    blk = pS2[ci][:, qt * 128:(qt + 1) * 128]
                    if ka == qa:
                        k.op("tensor", lambda e, pp=blk:
                             e.matmul(pp, cm.ident[:, :], cm.maskkgt[:, :],
                                      start=False, stop=False), inc=0)
                    elif ka == qa - 4:
                        k.op("tensor", lambda e, pp=blk:
                             e.matmul(pp, cm.ident[:, :], cm.maskcgt[:, :],
                                      start=False, stop=False), inc=0)
                    elif ka < qa - 4 or ka > qa:
                        k.op("tensor", lambda e, pp=blk:
                             e.matmul(pp, cm.ones128[:, :], cm.negrow[:, :],
                                      start=False, stop=False), inc=0)
                k.op("tensor", lambda e, pp=pS2[ci][:, :],
                     r=mqrow[0:1, vb, :]:
                     e.matmul(pp, cm.ones128[:, :], r, start=False,
                              stop=True), inc=1)
                s_mark[j] = k.cnt["tensor"]

            emit_s2(0)
            for j, ka in enumerate(kas):
                ci = j % 2
                if j + 1 < len(kas):
                    emit_s2(j + 1)
                k.wait("scalar", "tensor", s_mark[j])
                um = k.op("scalar", lambda e, pp=pS2[ci][:, :],
                          o=util[:, ci, :]:
                          e.activation(o, pp, AF.Exp), inc=1)
                ps2_pend[ci] = ("scalar", um)
                k.wait("tensor", "scalar", um)
                k.op("tensor", lambda e, l=vstg2[:, vb, j, :],
                     r=util[:, ci, :], pp=pN[ni][:, :], first=(j == 0),
                     last=(j == len(kas) - 1):
                     e.matmul(pp, l, r, start=first, stop=last), inc=1)
            tav = k.cnt["tensor"]
            vstg_pend2[vb] = ("tensor", tav)
            k.wait("vector", "tensor", tav)
            k.waitp("vector", o_pend[ni])
            k.op("vector", lambda e, pp=pN[ni]:
                 e.reciprocal(rrow[:, :], pp[64:65, :]), inc=1)
            vm1 = k.op("vector", lambda e:
                       e.tensor_copy(rrowb[:, :], rrow[:, :]), inc=1)
            k.wait("tensor", "vector", vm1)
            k.waitp("tensor", rep_pend[0])
            tmr = k.op("tensor", lambda e:
                       e.matmul(pRep2[:, :], cm.ones128[0:1, 0:64],
                                rrowb[:, :], start=True, stop=True), inc=1)
            k.wait("scalar", "tensor", tmr)
            smr = k.op("scalar", lambda e, o=repb[:, ni, :]:
                       e.activation(o, pRep2[:, :], AF.Copy), inc=1)
            rep_pend[0] = ("scalar", smr)
            k.wait("vector", "scalar", smr)
            vm2 = k.op("vector", lambda e, pp=pN[ni], o=oT[:, ni, :],
                       rr=repb[:, ni, :]:
                       e.tensor_tensor(o, pp[0:64, :], rr, ALU.mult), inc=1)
            pn_pend[ni] = ("vector", vm2)
            k.wait("sync", "vector", vm2)
            o_pend[ni] = ("sync",
                          k.dma("sync", ot_d[64 * h:64 * h + 64,
                                             w * 512:(w + 1) * 512],
                                oT[:, ni, :]))
    barrier()
    ctx2b.close()
    ctxL.close()
    barrier()

    ctx3 = ExitStack()
    emit_out_proj_dram(k, cm, ctx3, W["l0f_wo"][li], None, ot_d, xi_d, xi_d,
                       tg + "op", nk=4)
    barrier()
    ctx3.close()
    barrier()

    _l0f_ffn(k, cm, W, xi_d, hn_d, li, tg)


def _l0f_ffn(k, cm, W, xi_d, hn_d, li, tg):
    nc = k.nc

    def barrier():
        m = {e: k.cnt[e] for e in ENGINES}
        for e in ENGINES:
            for o in ENGINES:
                if o != e:
                    k.wait(e, o, m[o])

    ctxS = ExitStack()
    sbS = lambda n, s, dt=F32: ctxS.enter_context(
        nc.sbuf_tensor(tg + "f" + n, s, dt, side="right"))
    xib = sbS("xib", [128, NF, T + 1], BF16)
    rho_b = sbS("rho", [1, T], BF16)
    nmurho_b = sbS("nmr", [1, T], BF16)
    lgb = sbS("lg", [1, D], BF16)
    lbb = sbS("lb", [1, D], BF16)
    sq = sbS("sq", [128, 2, TC], BF16)
    murow = sbS("mu", [1, T])
    qrow = sbS("q", [1, T])
    tmp = sbS("tmp", [1, T])
    hnb = sbS("hnb", [128, 2, NF, TC], BF16)
    pS1 = ctxS.enter_context(nc.psum_tensor(tg + "f_pS", [1, TC], F32))
    pQ1 = ctxS.enter_context(nc.psum_tensor(tg + "f_pQ", [1, TC], F32))
    pR = ctxS.enter_context(nc.psum_tensor(tg + "f_pR", [128, 2, TC], F32))
    pM = ctxS.enter_context(nc.psum_tensor(tg + "f_pM", [128, 2, TC], F32))

    wm2 = k.dma_batch("gpsimd", [
        (xib[:, kc, 1:T + 1], xi_d[kc * 128:(kc + 1) * 128, :])
        for kc in range(NF)
    ] + [
        (lgb[:, :], W["l0f_lng"][li, :, :]),
        (lbb[:, :], W["l0f_lnb"][li, :, :]),
    ])
    wm = wm2
    k.wait("vector", "gpsimd", wm)
    zm = k.op("vector", lambda e: e.memset(xib[:, :, 0:1], 0.0), inc=1)
    k.wait("tensor", "vector", zm)
    k.wait("scalar", "vector", zm)
    k.wait("tensor", "gpsimd", wm2)

    def shifted(kc, lo, width):
        off = 1 if kc < 4 else 0
        return xib[:, kc, lo + off:lo + off + width]

    sq_te = {}
    for tc in range(NT):
        for fc in range(NF):
            gi = tc * NF + fc
            k.wait("scalar", "tensor", sq_te.get(gi - 2))
            m = k.op("scalar", lambda e, a=shifted(fc, tc * TC, TC),
                     o=sq[:, fc % 2, :]:
                     e.activation(o, a, AF.Square), inc=1)
            k.op("tensor", lambda e, a=shifted(fc, tc * TC, TC), p=pS1,
                 first=(fc == 0), last=(fc == NF - 1):
                 e.matmul(p[:, :], cm.invD[:], a, start=first, stop=last),
                 inc=(1 if fc == NF - 1 else 0))
            k.wait("tensor", "scalar", m)
            sq_te[gi] = k.op("tensor", lambda e, a=sq[:, fc % 2, :], p=pQ1,
                             first=(fc == 0), last=(fc == NF - 1):
                             e.matmul(p[:, :], cm.invD[:], a, start=first,
                                      stop=last), inc=1)
        te_m = k.cnt["tensor"]
        k.wait("vector", "tensor", te_m)
        k.op("vector", lambda e, p=pS1, o=murow[:, tc * TC:(tc + 1) * TC]:
             e.tensor_copy(o, p[:, :]), inc=1)
        em = k.op("vector", lambda e, p=pQ1,
                  o=qrow[:, tc * TC:(tc + 1) * TC]:
                  e.tensor_copy(o, p[:, :]), inc=1)
        if tc < NT - 1:
            k.wait("tensor", "vector", em)
    k.op("vector", lambda e: e.tensor_mul(tmp[:, :], murow[:, :],
                                          murow[:, :]), inc=1)
    vm = k.op("vector", lambda e: e.tensor_sub(tmp[:, :], qrow[:, :],
                                               tmp[:, :]), inc=1)
    k.wait("scalar", "vector", vm)
    k.op("scalar", lambda e: e.activation(tmp[:, :], tmp[:, :], AF.Ln,
                                          bias=cm.eps[0:1, 0:1]), inc=1)
    sm = k.op("scalar", lambda e: e.activation(tmp[:, :], tmp[:, :], AF.Exp,
                                               scale=-0.5), inc=1)
    k.wait("vector", "scalar", sm)
    k.op("vector", lambda e: e.tensor_copy(rho_b[:, :], tmp[:, :]), inc=1)
    k.op("vector", lambda e: e.tensor_mul(tmp[:, :], murow[:, :], tmp[:, :]),
         inc=1)
    k.op("vector", lambda e: e.tensor_scalar_mul(tmp[:, :], tmp[:, :], -1.0),
         inc=1)
    rm = k.op("vector", lambda e: e.tensor_copy(nmurho_b[:, :], tmp[:, :]),
              inc=1)
    k.wait("tensor", "vector", rm)

    pend_pm = [None, None]
    pend_hb = [None, None]
    for tc in range(NT):
        tl = tc * TC
        hb = tc % 2
        k.waitp("vector", pend_hb[hb])
        hv = None
        for kc in range(NF):
            pb = kc % 2
            k.waitp("tensor", pend_pm[pb])
            k.op("tensor", lambda e, p=pR[:, pb, :],
                 g=lgb[0:1, kc * 128:(kc + 1) * 128],
                 r=rho_b[:, tl:tl + TC]:
                 e.matmul(p, g, r, start=True, stop=True), inc=1)
            k.op("tensor", lambda e, p=pM[:, pb, :],
                 g=lgb[0:1, kc * 128:(kc + 1) * 128],
                 r=nmurho_b[:, tl:tl + TC]:
                 e.matmul(p, g, r, start=True, stop=False), inc=0)
            tm = k.op("tensor", lambda e, p=pM[:, pb, :],
                      b=lbb[0:1, kc * 128:(kc + 1) * 128]:
                      e.matmul(p, b, cm.onesrow[:, :], start=False,
                               stop=True), inc=1)
            k.wait("vector", "tensor", tm)
            k.op("vector", lambda e, a=shifted(kc, tl, TC), p=pR[:, pb, :],
                 o=hnb[:, hb, kc, :]:
                 e.tensor_tensor(o, a, p, ALU.mult), inc=1)
            hv = k.op("vector", lambda e, p=pM[:, pb, :], o=hnb[:, hb, kc, :]:
                      e.tensor_tensor(o, o, p, ALU.add), inc=1)
            pend_pm[pb] = ("vector", hv)
        k.wait("sync", "vector", hv)
        dm_h = k.dma_batch("sync", [
            (hn_d[kc * 128:(kc + 1) * 128, tl:tl + TC], hnb[:, hb, kc, :])
            for kc in range(NF)])
        pend_hb[hb] = ("sync", dm_h)
    barrier()
    ctxS.close()
    barrier()

    chunks = _l0f_chunks()
    ctxF = ExitStack()
    sbF = lambda n, s, dt=F32: ctxF.enter_context(
        nc.sbuf_tensor(tg + "fm" + n, s, dt, side="right"))
    w1b = sbF("w1b", [128, NF, 2 * L0F_HID], BF16)
    b1b = sbF("b1b", [1, 2 * L0F_HID], BF16)
    b2b = sbF("b2b", [1, D], BF16)
    hnt = sbF("hnt", [128, 2, NF, TC], BF16)
    hh = sbF("hh", [128, len(chunks), TC], BF16)
    gg = sbF("gg", [128, 2, TC], BF16)
    w2b = sbF("w2b", [128, len(chunks), D], BF16)
    xres = sbF("xres", [128, 2, TC])
    pA = [ctxF.enter_context(
        nc.psum_tensor(f"{tg}f_pA{i}", [128, TC], F32)) for i in range(2)]
    pG = [ctxF.enter_context(
        nc.psum_tensor(f"{tg}f_pG{i}", [128, TC], F32)) for i in range(2)]
    pZ = [ctxF.enter_context(
        nc.psum_tensor(f"{tg}f_pZ{i}", [128, TC], F32)) for i in range(4)]
    wm = k.dma_batch("gpsimd", [
        (w1b[:, kc, :], W["l0f_w1"][li, kc * 128:(kc + 1) * 128, :])
        for kc in range(NF)
    ] + [
        (w2b[0:wdt, i, :], W["l0f_w2"][li, c0:c0 + wdt, :])
        for i, (c0, wdt) in enumerate(chunks)
    ] + [
        (b1b[:, :], W["l0f_b1"][li, :, :]),
        (b2b[:, :], W["l0f_b2"][li, :, :]),
    ])
    k.wait("tensor", "gpsimd", wm)

    hnt_pend = [None, None]
    ag_pend = [None, None]
    hh_pend = [None]
    z_pend = [None]
    store_pend = [None, None]
    for tch in range(NT):
        tl = tch * TC
        hb = tch % 2
        k.waitp("gpsimd", hnt_pend[hb])
        dm = k.dma_batch("gpsimd", [
            (hnt[:, hb, kc, :], hn_d[kc * 128:(kc + 1) * 128, tl:tl + TC])
            for kc in range(NF)])
        k.wait("tensor", "gpsimd", dm)
        for i, (c0, wdt) in enumerate(chunks):
            bi = i % 2
            k.waitp("tensor", ag_pend[bi])
            for kc in range(NF):
                k.op("tensor", lambda e, l=w1b[:, kc, c0:c0 + wdt],
                     r=hnt[:, hb, kc, :], pp=pA[bi][0:wdt, :],
                     first=(kc == 0):
                     e.matmul(pp, l, r, start=first, stop=False), inc=0)
            tA = k.op("tensor", lambda e, pp=pA[bi][0:wdt, :],
                      b=b1b[0:1, c0:c0 + wdt]:
                      e.matmul(pp, b, cm.onesrow[:, :], start=False,
                               stop=True), inc=1)
            for kc in range(NF):
                k.op("tensor", lambda e,
                     l=w1b[:, kc, L0F_HID + c0:L0F_HID + c0 + wdt],
                     r=hnt[:, hb, kc, :], pp=pG[bi][0:wdt, :],
                     first=(kc == 0):
                     e.matmul(pp, l, r, start=first, stop=False), inc=0)
            tG = k.op("tensor", lambda e, pp=pG[bi][0:wdt, :],
                      b=b1b[0:1, L0F_HID + c0:L0F_HID + c0 + wdt]:
                      e.matmul(pp, b, cm.onesrow[:, :], start=False,
                               stop=True), inc=1)
            k.wait("scalar", "tensor", tG)
            k.waitp("scalar", ag_pend[bi])
            sg = k.op("scalar", lambda e, pp=pG[bi][0:wdt, :],
                      o=gg[0:wdt, bi, :]:
                      e.activation(o, pp, AF.Gelu), inc=1)
            k.wait("vector", "scalar", sg)
            k.wait("vector", "tensor", tA)
            if i < 2:
                k.waitp("vector", hh_pend[0])
            hv = k.op("vector", lambda e, pp=pA[bi][0:wdt, :],
                      o=hh[0:wdt, i, :], a=gg[0:wdt, bi, :]:
                      e.tensor_tensor(o, pp, a, ALU.mult), inc=1)
            ag_pend[bi] = ("vector", hv)
        hh_mark = k.cnt["vector"]
        k.wait("tensor", "vector", hh_mark)
        hnt_pend[hb] = ("tensor", k.cnt["tensor"])
        for och in range(2):
            for i, (c0, wdt) in enumerate(chunks):
                if i == 0:
                    k.waitp("tensor", z_pend[0])
                for o4 in range(4):
                    oc = och * 4 + o4
                    k.op("tensor", lambda e,
                         l=w2b[0:wdt, i, oc * 128:(oc + 1) * 128],
                         r=hh[0:wdt, i, :], pp=pZ[o4][:, :],
                         first=(i == 0):
                         e.matmul(pp, l, r, start=first, stop=False),
                         inc=(1 if o4 == 3 else 0))
            for o4 in range(4):
                oc = och * 4 + o4
                tb = k.op("tensor", lambda e, pp=pZ[o4][:, :],
                          b=b2b[0:1, oc * 128:(oc + 1) * 128]:
                          e.matmul(pp, b, cm.onesrow[:, :], start=False,
                                   stop=True), inc=1)
                sbf = o4 % 2
                k.waitp("sync", store_pend[sbf])
                dmx = k.dma("sync", xres[:, sbf, :],
                            xi_d[oc * 128:(oc + 1) * 128, tl:tl + TC])
                k.wait("vector", "sync", dmx)
                k.wait("vector", "tensor", tb)
                vz = k.op("vector", lambda e, o=xres[:, sbf, :],
                          pp=pZ[o4][:, :]:
                          e.tensor_tensor(o, o, pp, ALU.add), inc=1)
                k.wait("sync", "vector", vz)
                store_pend[sbf] = (
                    "sync",
                    k.dma("sync", xi_d[oc * 128:(oc + 1) * 128, tl:tl + TC],
                          xres[:, sbf, :]))
                z_pend[0] = ("vector", vz)
        hh_pend[0] = ("tensor", k.cnt["tensor"])
    barrier()
    ctxF.close()
    barrier()



def prep_weights_l0f(inp):
    qks = np.concatenate([inp["L0_ffn_q_scale"], inp["L0_ffn_k_scale"]],
                         axis=-1)
    return {
        "l0f_wqkv": np.ascontiguousarray(inp["L0_ffn_wqkv"]),
        "l0f_qks": np.ascontiguousarray(qks).reshape(2, 1, 128)
        .astype(np.float32),
        "l0f_wo": np.ascontiguousarray(inp["L0_ffn_wo"]),
        "l0f_lng": np.ascontiguousarray(inp["L0_ffn_ln_g"]).reshape(2, 1, D),
        "l0f_lnb": np.ascontiguousarray(inp["L0_ffn_ln_b"]).reshape(2, 1, D),
        "l0f_w1": np.ascontiguousarray(inp["L0_ffn_w1"]),
        "l0f_b1": np.ascontiguousarray(inp["L0_ffn_b1"])
        .reshape(2, 1, 2 * L0F_HID),
        "l0f_w2": np.ascontiguousarray(inp["L0_ffn_w2"]),
        "l0f_b2": np.ascontiguousarray(inp["L0_ffn_b2"]).reshape(2, 1, D),
    }


def prep_weights_l2f(inp):
    Fr, Fi = fourier_consts()
    return {
        "l2f_wr": np.ascontiguousarray(inp["L2_ffn_w_freq_real"]),
        "l2f_wi": np.ascontiguousarray(inp["L2_ffn_w_freq_imag"]),
        "l2f_bias": np.ascontiguousarray(inp["L2_ffn_bias"]).reshape(1, D),
        "c_Fr": Fr, "c_Fi": Fi,
    }


def build_full():
    ncr = bass.Bass("TRN2", target_bir_lowering=False, debug=False)
    nc = NCW(ncr)
    k = K(nc)
    ctx = ExitStack()
    cm = Common(nc, k, ctx)
    xt = nc.dram_tensor("xt", [D, T], F32, kind="ExternalInput").ap()
    xo = nc.dram_tensor("xo", [D, T], F32, kind="ExternalOutput").ap()
    hs = [nc.dram_tensor(f"hx{i}", [D, T], F32, kind="Internal").ap()
          for i in range(7)]
    chain = [build_L0a, build_L0f, build_L1a, build_L1f, build_L2a,
             build_L2f, build_L3a, build_L3f]
    xin = xt
    for i, b in enumerate(chain):
        xout = hs[i] if i < 7 else xo
        b(env=(nc, k, ctx, cm, xin, xout))
        xin = xout
    k.finish()
    ctx.close()
    return ncr, None


BLOCK_CHAIN = [
    ("l0a", "build_L0a", "prep_weights_l0a"),
    ("l0f", "build_L0f", "prep_weights_l0f"),
    ("l1a", "build_L1a", "prep_weights_l1a"),
    ("l1f", "build_L1f", "prep_weights_l1f"),
    ("l2a", "build_L2a", "prep_weights_l2a"),
    ("l2f", "build_L2f", "prep_weights_l2f"),
    ("l3a", "build_L3a", "prep_weights_l3a"),
    ("l3f", "build_L3f", "prep_weights_l3f"),
]


def kernel(**inputs):
    x = np.asarray(inputs["L0_attn_x"], np.float32)
    g = globals()
    wmap = dict(get_consts())
    for name, build, prep in BLOCK_CHAIN:
        wmap.update(g[prep](inputs))
    nc, _ = build_full()
    maps = []
    for c in range(8):
        m = {"xt": np.ascontiguousarray(x[c].T)}
        m.update(wmap)
        maps.append(m)
    res = run_bass_kernel_spmd(nc, maps, core_ids=list(range(8)))
    out = np.stack([np.asarray(res.results[c]["xo"]).T for c in range(8)],
                   axis=0)
    return np.ascontiguousarray(out, dtype=np.float32)
